# version 02
# baseline (speedup 1.0000x reference)
import math
import numpy as np
import ml_dtypes
import concourse.bass as bass
import concourse.mybir as mybir
from concourse.bass import IndirectOffsetOnAxis
from concourse.bass_utils import run_bass_kernel_spmd

F32 = mybir.dt.float32
BF16 = mybir.dt.bfloat16
I32 = mybir.dt.int32
U32 = mybir.dt.uint32
AX = mybir.AxisListType.X
OP = mybir.AluOpType
AF = mybir.ActivationFunctionType

NC = 8
D = 1024
S = 2048
NT = S // 128
E = 32
CAP = 640
EPC = E // NC
ROWS = EPC * CAP
NH = 16
HD = 64
NTG = NC * NT
BIG = 1.0e7
LN_EPS = 1e-5
ATT_SCALE = 1.0 / math.sqrt(HD)


class Ch:
    def __init__(self):
        self.steps = []

    def add(self, engine, fn, inc=1, gdma=False):
        self.steps.append((engine, fn, inc, gdma))

    def dma(self, engine, fn):
        self.add(engine, fn, inc=16)

    def dmas(self, engine, fn, n):
        fn.selfinc = True
        self.add(engine, fn, inc=16 * n)

    def gdma(self, fn):
        self.add("gpsimd", fn, inc=16, gdma=True)

    def emit(self, block, chain, gsem):
        waits = []
        c = g = 0
        for eng, fn, inc, isg in self.steps:
            waits.append((c, g))
            if isg:
                g += inc
            else:
                c += inc
        per_engine = {}
        for i, (eng, fn, inc, isg) in enumerate(self.steps):
            per_engine.setdefault(eng, []).append((waits[i], fn, inc, isg))

        def mk(items):
            def body(engine):
                for (cv, gv), fn, inc, isg in items:
                    if cv > 0:
                        engine.wait_ge(chain, cv)
                    if gv > 0:
                        engine.wait_ge(gsem, gv)
                    sem = gsem if isg else chain
                    inst = fn(engine, sem) if getattr(fn, "selfinc", False) \
                        else fn(engine)
                    if not getattr(fn, "selfinc", False):
                        assert inst is not None, "step fn must return last inst"
                        inst.then_inc(sem, inc)
            return body

        for eng, items in per_engine.items():
            getattr(block, eng)(mk(items))
        return c, g


def _host_consts():
    bf = ml_dtypes.bfloat16
    c = {}
    c["c_ident_f"] = np.eye(128, dtype=np.float32)
    c["c_ident_b"] = np.eye(128, dtype=np.float32).astype(bf)
    c["c_ltri_b"] = np.triu(np.ones((128, 128), np.float32)).astype(bf)
    c["c_lstr_b"] = (np.triu(np.ones((128, 128), np.float32)) -
                     np.eye(128, dtype=np.float32)).astype(bf)
    c["c_iota32"] = np.broadcast_to(np.arange(32, dtype=np.float32), (128, 32)).copy()
    c["c_iota128"] = np.arange(128, dtype=np.float32).reshape(128, 1).copy()
    c["c_ones_b"] = np.ones((128, 1), np.float32).astype(bf)
    inv_freq = 1.0 / 10000.0 ** (np.arange(0, HD, 2, dtype=np.float32) / HD)
    t = np.arange(S, dtype=np.float32)[:, None]
    freqs = t * inv_freq[None, :]
    emb = np.concatenate([freqs, freqs], axis=-1)
    cos = np.cos(emb).T
    sin = np.sin(emb).T
    c["c_cos"] = np.concatenate([cos, cos], axis=0).astype(bf)
    c["c_sin"] = np.concatenate([sin, sin], axis=0).astype(bf)
    P = np.zeros((128, 128), np.float32)
    for blk in (0, 64):
        for m in range(32):
            P[blk + m + 32, blk + m] = -1.0
            P[blk + m, blk + m + 32] = 1.0
    c["c_psign"] = P.astype(bf)
    return c


def _breg(g, val):
    cache = getattr(g, "_breg_cache", None)
    if cache is None:
        cache = {}
        g._breg_cache = cache
    if val not in cache:
        cache[val] = g.to_reg(val)
    return cache[val]


def _declare_consts(nc, names_shapes):
    t = {}
    for name, arr in names_shapes.items():
        dt = {np.dtype(np.float32): F32, np.dtype(ml_dtypes.bfloat16): BF16,
              np.dtype(np.int32): I32}[arr.dtype]
        t[name] = nc.dram_tensor(name, list(arr.shape), dt, kind="ExternalInput")
    return t


class Pool:
    def __init__(self, nc, st):
        def sb(name, shape, dtype=F32):
            return st.enter_context(nc.sbuf_tensor(name, shape, dtype))
        self.sb = sb
        self.x = sb("p_x", [128, D])
        self.sq = sb("p_sq", [128, D])
        self.xt = sb("p_xt", [128, D])
        self.xtT = sb("p_xtT", [128, D])
        self.s1 = sb("p_s1", [128, 1])
        self.s2 = sb("p_s2", [128, 1])
        self.mean = sb("p_mean", [128, 1])
        self.ex2 = sb("p_ex2", [128, 1])
        self.var = sb("p_var", [128, 1])
        self.rstd = sb("p_rstd", [128, 1])
        self.nmr = sb("p_nmr", [128, 1])
        self.lg = sb("p_lg", [128, 32])
        self.mx = sb("p_mx", [128, 1])
        self.nmx = sb("p_nmx", [128, 1])
        self.pe = sb("p_pe", [128, 32])
        self.sum = sb("p_sum", [128, 1])
        self.gate = sb("p_gate", [128, 1])
        self.m8 = sb("p_m8", [128, 8])
        self.ix8 = sb("p_ix8", [128, 8], U32)
        self.est = sb("p_est", [128, 1])
        self.oh = sb("p_oh", [128, 32])
        self.ohb = sb("p_ohb", [128, 32], BF16)
        self.ic = sb("p_ic", [128, 32])
        self.selp = sb("p_selp", [128, 32])
        self.pos = sb("p_pos", [128, 1])
        self.pt = sb("p_pt", [128, 1])
        self.pk = sb("p_pk", [128, 4])
        self.tot = sb("p_tot", [1, 512])
        self.t1 = sb("p_t1", [128, 1])
        self.t2 = sb("p_t2", [128, 1])
        self.t3 = sb("p_t3", [128, 1])
        self.t4 = sb("p_t4", [128, 1])
        self.eps = sb("p_eps", [128, 1])


def emit_ln(ch, P, src_getter, out=None):
    if src_getter is not None:
        ch.dma("sync", src_getter)
    out = out if out is not None else P.xt

    def v1(v):
        return v.reduce_sum(P.s1[:], P.x[:], axis=AX)
    ch.add("vector", v1)

    def s1(sc):
        return sc.activation(P.sq[:], P.x[:], AF.Square, accum_out=P.s2[:])
    ch.add("scalar", s1)

    def v2a(v):
        v.tensor_scalar_mul(P.mean[:], P.s1[:], 1.0 / D)
        return v.tensor_scalar_mul(P.ex2[:], P.s2[:], 1.0 / D)
    ch.add("vector", v2a)

    def v2b(v):
        return v.tensor_mul(P.t3[:], P.mean[:], P.mean[:])
    ch.add("vector", v2b)

    def v2c(v):
        return v.tensor_sub(P.var[:], P.ex2[:], P.t3[:])
    ch.add("vector", v2c)

    def s2a(sc):
        return sc.activation(P.t4[:], P.var[:], AF.Ln, bias=P.eps[:])
    ch.add("scalar", s2a)

    def s2b(sc):
        return sc.activation(P.rstd[:], P.t4[:], AF.Exp, scale=-0.5)
    ch.add("scalar", s2b)

    def v3(v):
        return v.tensor_scalar(out[:], P.x[:], P.mean[:], P.rstd[:],
                               OP.subtract, OP.mult)
    ch.add("vector", v3)


def emit_router_tile(ch, nc, P, C, sbR, PA, PB, ti, pertok_out, psum_tot_col):
    def t_tr(t):
        for cc in range(8):
            inst = t.matmul(PA[:, 128 * cc:128 * cc + 128],
                            P.xt[:, 128 * cc:128 * cc + 128], C["c_ident_f"][:],
                            start=True, stop=True)
        return inst
    ch.add("tensor", t_tr)

    def v_ev(v):
        return v.tensor_copy(P.xtT[:], PA[:, 0:D])
    ch.add("vector", v_ev)

    def t_lg(t):
        for cc in range(8):
            inst = t.matmul(PB[:, 0:32], P.xtT[:, 128 * cc:128 * cc + 128],
                            sbR[:, 32 * cc:32 * cc + 32],
                            start=(cc == 0), stop=(cc == 7))
        return inst
    ch.add("tensor", t_lg)

    def v_sm1(v):
        return v.tensor_copy(P.lg[:], PB[:, 0:32])
    ch.add("vector", v_sm1)

    def v_sm2(v):
        return v.reduce_max(P.mx[:], P.lg[:], axis=AX)
    ch.add("vector", v_sm2)

    def v_sm3(v):
        return v.tensor_scalar_mul(P.nmx[:], P.mx[:], -1.0)
    ch.add("vector", v_sm3)

    def s_exp(sc):
        return sc.activation(P.pe[:], P.lg[:], AF.Exp, bias=P.nmx[:],
                             scale=1.0, accum_out=P.sum[:])
    ch.add("scalar", s_exp)

    def v_t1(v):
        v.reciprocal(P.gate[:], P.sum[:])
        return v.max(P.m8[:], P.lg[:])
    ch.add("vector", v_t1)

    def v_t2(v):
        return v.max_index(P.ix8[:], P.m8[:], P.lg[:])
    ch.add("vector", v_t2)

    def v_t3(v):
        return v.tensor_copy(P.est[:], P.ix8[:, 0:1])
    ch.add("vector", v_t3)

    def v_t4(v):
        return v.tensor_single_scalar(P.oh[:], C["c_iota32"][:], P.est[:],
                                      OP.is_equal)
    ch.add("vector", v_t4)

    def v_t5(v):
        return v.tensor_copy(P.ohb[:], P.oh[:])
    ch.add("vector", v_t5)

    def t_ic(t):
        t.matmul(PB[:, 512:544], C["c_ltri_b"][:], P.ohb[:], start=True, stop=True)
        return t.matmul(psum_tot_col, C["c_ones_b"][:], P.ohb[:],
                        start=True, stop=True)
    ch.add("tensor", t_ic)

    def v_p1(v):
        return v.tensor_copy(P.ic[:], PB[:, 512:544])
    ch.add("vector", v_p1)

    def v_p2(v):
        return v.tensor_mul(P.selp[:], P.ic[:], P.oh[:])
    ch.add("vector", v_p2)

    def v_p3(v):
        return v.reduce_sum(P.pos[:], P.selp[:], axis=AX)
    ch.add("vector", v_p3)

    def v_p4(v):
        return v.tensor_scalar_add(P.pt[:], P.pos[:], -1.0)
    ch.add("vector", v_p4)

    def v_p5(v):
        v.tensor_copy(P.pk[:, 0:1], P.est[:])
        v.tensor_copy(P.pk[:, 1:2], P.gate[:])
        v.tensor_copy(P.pk[:, 2:3], P.pt[:])
        return v.memset(P.pk[:, 3:4], 0.0)
    ch.add("vector", v_p5)

    def d_out(s):
        return s.dma_start(out=pertok_out[128 * ti:128 * (ti + 1), :], in_=P.pk[:])
    ch.dma("sync", d_out)


def build_router_program(nc, C, x_in, pertok_out, totals_out, ch, st, P,
                         sbR, PA, PB):
    for i in range(NT):
        def load(s, i=i):
            return s.dma_start(out=P.x[:], in_=x_in[128 * i:128 * (i + 1), :])
        emit_ln(ch, P, load)
        emit_router_tile(ch, nc, P, C, sbR, PA, PB, i, pertok_out,
                         PB[0:1, 1024 + 32 * i:1024 + 32 * i + 32])

    def v_tot(v):
        return v.tensor_copy(P.tot[:], PB[0:1, 1024:1536])
    ch.add("vector", v_tot)

    def d_tot(s):
        return s.dma_start(out=totals_out[:], in_=P.tot[:])
    ch.dma("sync", d_tot)


def build_L1():
    from contextlib import ExitStack
    consts = _host_consts()
    nc = bass.Bass("TRN2", target_bir_lowering=False, debug=False,
                   detect_race_conditions=False)
    C = {}
    for k, v in consts.items():
        dt = BF16 if v.dtype == ml_dtypes.bfloat16 else F32
        C[k] = nc.dram_tensor(k, list(v.shape), dt, kind="ExternalInput")
    x_in = nc.dram_tensor("x_in", [S, D], F32, kind="ExternalInput")
    r_in = nc.dram_tensor("r_in", [D, E], F32, kind="ExternalInput")
    pertok_out = nc.dram_tensor("pertok", [S, 4], F32, kind="ExternalOutput")
    totals_out = nc.dram_tensor("totals", [1, NT * E], F32, kind="ExternalOutput")

    ch = Ch()
    with ExitStack() as st:
        P = Pool(nc, st)
        sbR = st.enter_context(nc.sbuf_tensor("sbR", [128, 256], F32))
        Csb = {}
        for k, v in consts.items():
            dt = BF16 if v.dtype == ml_dtypes.bfloat16 else F32
            if k in ("c_cos", "c_sin", "c_psign"):
                continue
            Csb[k] = st.enter_context(nc.sbuf_tensor("s" + k, list(v.shape), dt))
        PA = st.enter_context(nc.psum_tensor("PA", [128, 2048], F32))
        PB = st.enter_context(nc.psum_tensor("PB", [128, 2048], F32))
        chain = st.enter_context(nc.semaphore("chain"))
        gsem = st.enter_context(nc.semaphore("gsem"))
        block = st.enter_context(nc.Block())

        nconsts = len([k for k in consts
                       if k not in ("c_cos", "c_sin", "c_psign")]) + 1

        def load_consts(s, sem):
            for k in Csb:
                s.dma_start(out=Csb[k][:], in_=C[k][:]).then_inc(sem, 16)
            s.dma_start(out=sbR.ap().rearrange("p (c e) -> p c e", c=8),
                        in_=r_in.ap().rearrange("(c p) e -> p c e", p=128)
                        ).then_inc(sem, 16)
        ch.dmas("sync", load_consts, nconsts)

        def set_eps(v):
            return v.memset(P.eps[:], LN_EPS)
        ch.add("vector", set_eps)

        build_router_program(nc, Csb, x_in, pertok_out, totals_out, ch, st, P,
                             sbR, PA, PB)

        ctot, gtot = ch.emit(block, chain, gsem)

        def fin(s):
            s.wait_ge(chain, ctot)
            if gtot:
                s.wait_ge(gsem, gtot)
        block.sync(fin)

    return nc, consts


def l1_inputs(consts, x_slice, w_router):
    d = {k: v for k, v in consts.items()}
    d["x_in"] = np.ascontiguousarray(x_slice)
    d["r_in"] = np.ascontiguousarray(w_router)
    return d




def build_L2():
    from contextlib import ExitStack
    consts = _host_consts()
    nc = bass.Bass("TRN2", target_bir_lowering=False, debug=False,
                   detect_race_conditions=False)
    C = {}
    use = ("c_ident_b", "c_lstr_b", "c_iota32", "c_iota128")
    for k in use:
        v = consts[k]
        dt = BF16 if v.dtype == ml_dtypes.bfloat16 else F32
        C[k] = nc.dram_tensor(k, list(v.shape), dt, kind="ExternalInput")
    xf = nc.dram_tensor("xf", [NC * S, D], F32, kind="ExternalInput")
    ptf = nc.dram_tensor("ptf", [NC * S, 4], F32, kind="ExternalInput")
    totf = nc.dram_tensor("totf", [NTG, E], F32, kind="ExternalInput")
    wg_in = nc.dram_tensor("wg", [EPC * D, D], F32, kind="ExternalInput")
    wu_in = nc.dram_tensor("wu", [EPC * D, D], F32, kind="ExternalInput")
    wd_in = nc.dram_tensor("wd", [EPC * D, D], F32, kind="ExternalInput")
    exb = nc.dram_tensor("exb", [128, 1], F32, kind="ExternalInput")
    y_out = nc.dram_tensor("y_out", [ROWS, D], BF16, kind="ExternalOutput")
    slots_out = nc.dram_tensor("slots_out", [NC * S, 4], F32, kind="ExternalOutput")
    offs_d = nc.dram_tensor("offs_d", [NTG * E], F32)
    invd = nc.dram_tensor("invd", [ROWS + 128, 4], I32)

    ch = Ch()
    with ExitStack() as st:
        P = Pool(nc, st)
        def sb(name, shape, dtype=F32):
            return st.enter_context(nc.sbuf_tensor(name, shape, dtype))
        Csb = {}
        for k in use:
            v = consts[k]
            dt = BF16 if v.dtype == ml_dtypes.bfloat16 else F32
            Csb[k] = sb("s" + k, list(v.shape), dt)
        sbExb = sb("sbExb", [128, 1])
        sbTotB = sb("sbTotB", [128, 32], BF16)
        sbOffs = sb("sbOffs", [128, 32])
        sbOnesRow = sb("sbOnesRow", [1, 128])
        sbOffsRow = sb("sbOffsRow", [1, NTG * E])
        sbOffsBC = sb("sbOffsBC", [128, NTG * E])
        sbPkT = sb("sbPkT", [128, 4])
        sbOh = sb("sbOh", [128, 32])
        sbSlt = sb("sbSlt", [128, 4])
        sbIdxI = sb("sbIdxI", [128, 1], I32)
        sbTokF = sb("sbTokF", [128, 4])
        sbTok4 = sb("sbTok4", [128, 4], I32)
        sbSent = sb("sbSent", [128, 84], I32)
        sbInv = sb("sbInv", [128, 80], I32)
        sbWF = sb("sbWF", [128, D])
        sbWg = sb("sbWg", [128, 8 * D], BF16)
        sbWu = sb("sbWu", [128, 8 * D], BF16)
        sbWd = sb("sbWd", [128, 8 * D], BF16)
        sbXtB = sb("sbXtB", [128, D], BF16)
        sbBufT = sb("sbBufT", [128, 8 * CAP], BF16)
        sbSg = sb("sbSg", [128, CAP], BF16)
        sbST = sb("sbST", [128, 8 * CAP], BF16)
        sbYout = sb("sbYout", [128, D], BF16)
        PA = st.enter_context(nc.psum_tensor("PA", [128, 2048], F32))
        PB = st.enter_context(nc.psum_tensor("PB", [128, 2048], F32))
        chain = st.enter_context(nc.semaphore("chain"))
        gsem = st.enter_context(nc.semaphore("gsem"))
        block = st.enter_context(nc.Block())

        sbTotF = sb("sbTotF", [128, 32])

        def load_consts2(s, sem):
            for k in use:
                s.dma_start(out=Csb[k][:], in_=C[k][:]).then_inc(sem, 16)
            s.dma_start(out=sbExb[:], in_=exb[:]).then_inc(sem, 16)
            s.dma_start(out=sbTotF[:], in_=totf[:]).then_inc(sem, 16)
        ch.dmas("sync", load_consts2, len(use) + 2)

        def v_init(v):
            v.memset(P.eps[:], LN_EPS)
            v.memset(sbOnesRow[:], 1.0)
            v.memset(sbSent[:], 99999)
            return v.tensor_copy(sbTotB[:], sbTotF[:])
        ch.add("vector", v_init)

        def d_sent(s):
            return s.dma_start(
                out=invd.ap().rearrange("(p a) c -> p a c", p=128),
                in_=sbSent.ap().rearrange("p (a c) -> p a c", c=4))
        ch.dma("sync", d_sent)

        def t_offs(t):
            return t.matmul(PB[:, 0:32], Csb["c_lstr_b"][:], sbTotB[:],
                            start=True, stop=True)
        ch.add("tensor", t_offs)

        def v_offs(v):
            return v.tensor_copy(sbOffs[:], PB[:, 0:32])
        ch.add("vector", v_offs)

        def d_offs_out(s):
            return s.dma_start(out=offs_d.ap().rearrange("(p e) -> p e", p=128),
                               in_=sbOffs[:])
        ch.dma("sync", d_offs_out)

        def d_offs_in(s):
            return s.dma_start(out=sbOffsRow[:], in_=offs_d.ap()[None, :])
        ch.dma("sync", d_offs_in)

        for half in range(2):
            for q in range(4):
                def t_bc(t, half=half, q=q):
                    return t.matmul(PA[:, 512 * q:512 * q + 512], sbOnesRow[:],
                                    sbOffsRow[0:1, 2048 * half + 512 * q:
                                              2048 * half + 512 * q + 512],
                                    start=True, stop=True)
                ch.add("tensor", t_bc)

            def v_bc(v, half=half):
                return v.tensor_copy(sbOffsBC[:, 2048 * half:2048 * half + 2048],
                                     PA[:, 0:2048])
            ch.add("vector", v_bc)

        for T in range(NTG):
            def d_pt(s, T=T):
                return s.dma_start(out=sbPkT[:], in_=ptf[128 * T:128 * (T + 1), :])
            ch.dma("sync", d_pt)

            def vs1(v, T=T):
                v.tensor_single_scalar(sbOh[:], Csb["c_iota32"][:],
                                       sbPkT[:, 0:1], OP.is_equal)
                return v.tensor_scalar_add(P.t4[:], Csb["c_iota128"][:],
                                           float(128 * T))
            ch.add("vector", vs1)

            def vs2(v, T=T):
                v.tensor_mul(P.oh[:], sbOh[:], sbOffsBC[:, 32 * T:32 * T + 32])
                for cc in range(4):
                    inst = v.tensor_copy(sbTokF[:, cc:cc + 1], P.t4[:])
                return inst
            ch.add("vector", vs2)

            def vs3(v):
                v.reduce_sum(P.t1[:], P.oh[:], axis=AX)
                return v.tensor_copy(sbTok4[:], sbTokF[:])
            ch.add("vector", vs3)

            def vs4(v):
                return v.tensor_add(P.t1[:], P.t1[:], sbPkT[:, 2:3])
            ch.add("vector", vs4)

            def vs5(v):
                v.tensor_single_scalar(P.t2[:], P.t1[:], float(CAP), OP.is_lt)
                return v.tensor_scalar(P.t3[:], sbPkT[:, 0:1], float(CAP),
                                       None, OP.mult)
            ch.add("vector", vs5)

            def vs6(v):
                v.tensor_add(P.t3[:], P.t3[:], P.t1[:])
                return v.tensor_scalar(P.t4[:], P.t2[:], -BIG, BIG,
                                       OP.mult, OP.add)
            ch.add("vector", vs6)

            def vs7(v):
                return v.scalar_tensor_tensor(P.pos[:], P.t3[:], P.t2[:],
                                              P.t4[:], OP.mult, OP.add)
            ch.add("vector", vs7)

            def vs8(v):
                v.tensor_copy(sbSlt[:, 0:1], P.pos[:])
                v.tensor_copy(sbSlt[:, 1:2], P.t2[:])
                v.memset(sbSlt[:, 2:4], 0.0)
                return v.tensor_sub(P.t3[:], P.pos[:], sbExb[:])
            ch.add("vector", vs8)

            def vs9(v):
                v.tensor_single_scalar(P.t1[:], P.t3[:], 0.0, OP.is_ge)
                return v.tensor_single_scalar(P.t2[:], P.t3[:], float(ROWS),
                                              OP.is_lt)
            ch.add("vector", vs9)

            def vs10(v):
                return v.tensor_mul(P.t1[:], P.t1[:], P.t2[:])
            ch.add("vector", vs10)

            def vs11(v):
                v.tensor_mul(P.t3[:], P.t3[:], P.t1[:])
                return v.tensor_scalar(P.t2[:], P.t1[:], -BIG, BIG,
                                       OP.mult, OP.add)
            ch.add("vector", vs11)

            def vs12(v):
                return v.tensor_add(P.t3[:], P.t3[:], P.t2[:])
            ch.add("vector", vs12)

            def vs13(v):
                return v.tensor_copy(sbIdxI[:], P.t3[:])
            ch.add("vector", vs13)

            def d_slt(s, T=T):
                return s.dma_start(out=slots_out[128 * T:128 * (T + 1), :],
                                   in_=sbSlt[:])
            ch.dma("sync", d_slt)

            def g_scat(g, T=T):
                return g.indirect_dma_start(
                    invd[:], IndirectOffsetOnAxis(sbIdxI[:], 0),
                    sbTok4[:], None,
                    bounds_check=_breg(g, ROWS - 1), oob_is_err=False)
            ch.gdma(g_scat)

        def d_inv(s):
            return s.dma_start(
                out=sbInv.ap().rearrange("p (j c) -> p j c", c=4),
                in_=invd.ap()[0:ROWS, :].rearrange("(j p) c -> p j c", p=128))
        ch.dma("sync", d_inv)

        for e in range(EPC):
            for (w_in, sbW) in ((wg_in, sbWg), (wu_in, sbWu), (wd_in, sbWd)):
                for cc in range(8):
                    def d_w(s, w_in=w_in, e=e, cc=cc):
                        return s.dma_start(
                            out=sbWF[:],
                            in_=w_in[D * e + 128 * cc:D * e + 128 * (cc + 1), :])
                    ch.dma("sync", d_w)

                    def v_w(v, sbW=sbW, cc=cc):
                        return v.tensor_copy(sbW[:, D * cc:D * (cc + 1)], sbWF[:])
                    ch.add("vector", v_w)

            for j5 in range(5):
                def v_z(v):
                    return v.memset(P.x[:], 0.0)
                ch.add("vector", v_z)

                def g_gth(g, e=e, j5=j5):
                    col = 4 * (5 * e + j5)
                    return g.indirect_dma_start(
                        P.x[:], None, xf[:],
                        IndirectOffsetOnAxis(sbInv[:, col:col + 1], 0),
                        bounds_check=_breg(g, NC * S - 1), oob_is_err=False)
                ch.gdma(g_gth)

                emit_ln(ch, P, None)

                def v_cast(v):
                    return v.tensor_copy(sbXtB[:], P.xt[:])
                ch.add("vector", v_cast)

                def t_tr(t):
                    for cc in range(8):
                        inst = t.matmul(PA[:, 128 * cc:128 * (cc + 1)],
                                        sbXtB[:, 128 * cc:128 * (cc + 1)],
                                        Csb["c_ident_b"][:], start=True, stop=True)
                    return inst
                ch.add("tensor", t_tr)

                def v_tr(v, j5=j5):
                    for cc in range(8):
                        inst = v.tensor_copy(
                            sbBufT[:, CAP * cc + 128 * j5:CAP * cc + 128 * (j5 + 1)],
                            PA[:, 128 * cc:128 * (cc + 1)])
                    return inst
                ch.add("vector", v_tr)

            for fs in range(8):
                def t_gu(t, fs=fs):
                    for cc in range(8):
                        t.matmul(PA[:, 0:512],
                                 sbWg[:, D * cc + 128 * fs:D * cc + 128 * (fs + 1)],
                                 sbBufT[:, CAP * cc:CAP * cc + 512],
                                 start=(cc == 0), stop=(cc == 7))
                        t.matmul(PA[:, 512:640],
                                 sbWg[:, D * cc + 128 * fs:D * cc + 128 * (fs + 1)],
                                 sbBufT[:, CAP * cc + 512:CAP * cc + 640],
                                 start=(cc == 0), stop=(cc == 7))
                    for cc in range(8):
                        t.matmul(PA[:, 1024:1536],
                                 sbWu[:, D * cc + 128 * fs:D * cc + 128 * (fs + 1)],
                                 sbBufT[:, CAP * cc:CAP * cc + 512],
                                 start=(cc == 0), stop=(cc == 7))
                        inst = t.matmul(
                            PA[:, 1536:1664],
                            sbWu[:, D * cc + 128 * fs:D * cc + 128 * (fs + 1)],
                            sbBufT[:, CAP * cc + 512:CAP * cc + 640],
                            start=(cc == 0), stop=(cc == 7))
                    return inst
                ch.add("tensor", t_gu)

                def s_silu(sc):
                    return sc.activation(sbSg[:], PA[:, 0:640], AF.Silu)
                ch.add("scalar", s_silu)

                def v_st(v, fs=fs):
                    return v.tensor_tensor(sbST[:, CAP * fs:CAP * (fs + 1)],
                                           sbSg[:], PA[:, 1024:1664], OP.mult)
                ch.add("vector", v_st)

            for cs in range(5):
                def t_y(t, cs=cs):
                    for fs in range(8):
                        t.matmul(PB[:, 0:512],
                                 sbST[:, CAP * fs + 128 * cs:CAP * fs + 128 * (cs + 1)],
                                 sbWd[:, D * fs:D * fs + 512],
                                 start=(fs == 0), stop=(fs == 7))
                        inst = t.matmul(
                            PB[:, 512:1024],
                            sbST[:, CAP * fs + 128 * cs:CAP * fs + 128 * (cs + 1)],
                            sbWd[:, D * fs + 512:D * fs + 1024],
                            start=(fs == 0), stop=(fs == 7))
                    return inst
                ch.add("tensor", t_y)

                def s_yev(sc):
                    return sc.activation(sbYout[:], PB[:, 0:1024], AF.Copy)
                ch.add("scalar", s_yev)

                def d_y(s, e=e, cs=cs):
                    return s.dma_start(
                        out=y_out[CAP * e + 128 * cs:CAP * e + 128 * (cs + 1), :],
                        in_=sbYout[:])
                ch.dma("sync", d_y)

        ctot, gtot = ch.emit(block, chain, gsem)

        def fin(s):
            s.wait_ge(chain, ctot)
            if gtot:
                s.wait_ge(gsem, gtot)
        block.sync(fin)

    return nc, consts


def l2_inputs(consts, core, xfull, pertok_full, totals_full, w_gate, w_up, w_down):
    use = ("c_ident_b", "c_lstr_b", "c_iota32", "c_iota128")
    d = {k: consts[k] for k in use}
    d["xf"] = xfull
    d["ptf"] = pertok_full
    d["totf"] = totals_full
    d["wg"] = np.ascontiguousarray(w_gate[EPC * core:EPC * (core + 1)]).reshape(EPC * D, D)
    d["wu"] = np.ascontiguousarray(w_up[EPC * core:EPC * (core + 1)]).reshape(EPC * D, D)
    d["wd"] = np.ascontiguousarray(w_down[EPC * core:EPC * (core + 1)]).reshape(EPC * D, D)
    d["exb"] = np.full((128, 1), float(EPC * core * CAP), np.float32)
    return d


def build_L3():
    from contextlib import ExitStack
    consts = _host_consts()
    nc = bass.Bass("TRN2", target_bir_lowering=False, debug=False,
                   detect_race_conditions=False)
    C = {}
    for k, v in consts.items():
        dt = BF16 if v.dtype == ml_dtypes.bfloat16 else F32
        C[k] = nc.dram_tensor(k, list(v.shape), dt, kind="ExternalInput")
    x_own = nc.dram_tensor("x_own", [S, D], F32, kind="ExternalInput")
    y_full = nc.dram_tensor("y_full", [E * CAP, D], BF16, kind="ExternalInput")
    slots_own = nc.dram_tensor("slots_own", [S, 4], F32, kind="ExternalInput")
    pertok_own = nc.dram_tensor("pertok_own", [S, 4], F32, kind="ExternalInput")
    r_in = nc.dram_tensor("r_in", [D, E], F32, kind="ExternalInput")
    wqt = nc.dram_tensor("wqt", [D, D], F32, kind="ExternalInput")
    wkt = nc.dram_tensor("wkt", [D, D], F32, kind="ExternalInput")
    wvt = nc.dram_tensor("wvt", [D, D], F32, kind="ExternalInput")
    wot = nc.dram_tensor("wot", [D, D], F32, kind="ExternalInput")
    bq_in = nc.dram_tensor("bq_in", [8, 128], F32, kind="ExternalInput")
    bk_in = nc.dram_tensor("bk_in", [8, 128], F32, kind="ExternalInput")
    bv_rep = nc.dram_tensor("bv_rep", [128, D], F32, kind="ExternalInput")
    bo_rep = nc.dram_tensor("bo_rep", [128, D], F32, kind="ExternalInput")
    x2_out = nc.dram_tensor("x2_out", [S, D], F32, kind="ExternalOutput")
    pertok2 = nc.dram_tensor("pertok2", [S, 4], F32, kind="ExternalOutput")
    totals2 = nc.dram_tensor("totals2", [1, NT * E], F32, kind="ExternalOutput")
    x1_d = nc.dram_tensor("x1_d", [S, D], F32)

    ch = Ch()
    with ExitStack() as st:
        P = Pool(nc, st)
        def sb(name, shape, dtype=F32):
            return st.enter_context(nc.sbuf_tensor(name, shape, dtype))
        Csb = {}
        for k, v in consts.items():
            dt = BF16 if v.dtype == ml_dtypes.bfloat16 else F32
            Csb[k] = sb("s" + k, list(v.shape), dt)
        sbR = sb("sbR", [128, 256])
        sbXT = sb("sbXT", [128, NT * 1024], BF16)
        sbVt = sb("sbVt", [128, NT * 1040], BF16)
        sbA = sb("sbA", [128, NT * 1024], BF16)
        sbATok = sb("sbATok", [128, NT * 1024], BF16)
        sbW = sb("sbW", [128, 8 * D], BF16)
        sbWqk = sb("sbWqk", [128, D], BF16)
        sbWF = sb("sbWF", [128, 128])
        sbQ = sb("sbQ", [128, S], BF16)
        sbK = sb("sbK", [128, S], BF16)
        sbKp = sb("sbKp", [128, S], BF16)
        sbQp = sb("sbQp", [128, S], BF16)
        sbSc1 = sb("sbSc1", [128, S], BF16)
        sbPO = sb("sbPO", [65, 1024], BF16)
        sbBq = sb("sbBq", [128, 8])
        sbBk = sb("sbBk", [128, 8])
        sbBv = sb("sbBv", [128, D], BF16)
        sbBo = sb("sbBo", [128, D], BF16)
        sbYg = sb("sbYg", [128, D], BF16)
        sbSltT = sb("sbSltT", [128, 4])
        sbPkT = sb("sbPkT", [128, 4])
        sbIdxI = sb("sbIdxI", [128, 1], I32)
        sbX1 = P.xtT
        sbX2f = P.sq
        PA = st.enter_context(nc.psum_tensor("PA", [128, 2048], F32))
        PB = st.enter_context(nc.psum_tensor("PB", [128, 2048], F32))
        chain = st.enter_context(nc.semaphore("chain"))
        gsem = st.enter_context(nc.semaphore("gsem"))
        block = st.enter_context(nc.Block())

        nload = len(consts) + 1 + 16

        def load_consts(s, sem):
            for k in consts:
                s.dma_start(out=Csb[k][:], in_=C[k][:]).then_inc(sem, 16)
            s.dma_start(out=sbR.ap().rearrange("p (c e) -> p c e", c=8),
                        in_=r_in.ap().rearrange("(c p) e -> p c e", p=128)
                        ).then_inc(sem, 16)
            for j in range(8):
                s.dma_start(out=sbBq[:, j:j + 1],
                            in_=bq_in[j:j + 1, :].rearrange("a p -> p a")
                            ).then_inc(sem, 16)
                s.dma_start(out=sbBk[:, j:j + 1],
                            in_=bk_in[j:j + 1, :].rearrange("a p -> p a")
                            ).then_inc(sem, 16)
        ch.dmas("sync", load_consts, nload)

        def d_bv(s):
            return s.dma_start(out=P.x[:], in_=bv_rep[:])
        ch.dma("sync", d_bv)

        def v_bv(v):
            return v.tensor_copy(sbBv[:], P.x[:])
        ch.add("vector", v_bv)

        def d_bo(s):
            return s.dma_start(out=P.x[:], in_=bo_rep[:])
        ch.dma("sync", d_bo)

        def v_bo(v):
            return v.tensor_copy(sbBo[:], P.x[:])
        ch.add("vector", v_bo)

        def v_init(v):
            v.memset(P.eps[:], LN_EPS)
            v.memset(sbYg[:], 0.0)
            v.memset(sbKp[64:128, :], 0.0)
            v.memset(sbQp[64:128, :], 0.0)
            for tt in range(NT):
                inst = v.memset(
                    sbVt.ap()[:, 1040 * tt:1040 * (tt + 1)]
                    .rearrange("p (h c) -> p h c", c=65)[:, :, 64:65], 1.0)
            return inst
        ch.add("vector", v_init)

        for i in range(NT):
            def d_in(s, sem, i=i):
                s.dma_start(out=P.x[:], in_=x_own[128 * i:128 * (i + 1), :]
                            ).then_inc(sem, 16)
                s.dma_start(out=sbSltT[:], in_=slots_own[128 * i:128 * (i + 1), :]
                            ).then_inc(sem, 16)
                s.dma_start(out=sbPkT[:], in_=pertok_own[128 * i:128 * (i + 1), :]
                            ).then_inc(sem, 16)
            ch.dmas("sync", d_in, 3)

            def v_idx(v):
                return v.tensor_copy(sbIdxI[:], sbSltT[:, 0:1])
            ch.add("vector", v_idx)

            def g_y(g):
                return g.indirect_dma_start(
                    sbYg[:], None, y_full[:],
                    IndirectOffsetOnAxis(sbIdxI[:], 0),
                    bounds_check=_breg(g, E * CAP - 1), oob_is_err=False)
            ch.gdma(g_y)

            emit_ln(ch, P, None)

            def v_x1a(v):
                v.tensor_scalar(P.t1[:], sbPkT[:, 1:2], sbSltT[:, 1:2], 0.5,
                                OP.mult, OP.mult)
                return v.scalar_tensor_tensor(sbX1[:], P.xt[:], 0.5, P.x[:],
                                              OP.mult, OP.add)
            ch.add("vector", v_x1a)

            def v_x1b(v):
                return v.scalar_tensor_tensor(sbX2f[:], sbYg[:], P.t1[:],
                                              sbX1[:], OP.mult, OP.add)
            ch.add("vector", v_x1b)

            def d_x1(s, i=i):
                return s.dma_start(out=x1_d[128 * i:128 * (i + 1), :],
                                   in_=sbX2f[:])
            ch.dma("sync", d_x1)

        for i in range(NT):
            def d_x(s, i=i):
                return s.dma_start(out=P.x[:], in_=x1_d[128 * i:128 * (i + 1), :])
            ch.dma("sync", d_x)
            emit_ln(ch, P, None)

            def v_cb(v):
                return v.tensor_copy(sbYg[:], P.xt[:])
            ch.add("vector", v_cb)

            def t_tr(t):
                for cc in range(8):
                    inst = t.matmul(PA[:, 128 * cc:128 * (cc + 1)],
                                    sbYg[:, 128 * cc:128 * (cc + 1)],
                                    Csb["c_ident_b"][:], start=True, stop=True)
                return inst
            ch.add("tensor", t_tr)

            def v_tr(v, i=i):
                for cc in range(8):
                    inst = v.tensor_copy(sbXT[:, 2048 * cc + 128 * i:
                                              2048 * cc + 128 * (i + 1)],
                                         PA[:, 128 * cc:128 * (cc + 1)])
                return inst
            ch.add("vector", v_tr)

        for cc in range(8):
            def d_wv(s, cc=cc):
                return s.dma_start(out=P.x[:],
                                   in_=wvt[128 * cc:128 * (cc + 1), :])
            ch.dma("sync", d_wv)

            def v_wv(v, cc=cc):
                return v.tensor_copy(sbW[:, D * cc:D * (cc + 1)], P.x[:])
            ch.add("vector", v_wv)

        for tt in range(NT):
            for oh in range(2):
                def t_v(t, tt=tt, oh=oh):
                    for cc in range(8):
                        inst = t.matmul(PB[:, 0:512],
                                        sbXT[:, 2048 * cc + 128 * tt:
                                             2048 * cc + 128 * (tt + 1)],
                                        sbW[:, D * cc + 512 * oh:
                                            D * cc + 512 * (oh + 1)],
                                        start=(cc == 0), stop=(cc == 7))
                    return inst
                ch.add("tensor", t_v)

                def v_v1(v, oh=oh):
                    return v.tensor_add(sbX1[:, 0:512], PB[:, 0:512],
                                        sbBv[:, 512 * oh:512 * (oh + 1)])
                ch.add("vector", v_v1)

                def v_v2(v, tt=tt, oh=oh):
                    base = 1040 * tt
                    dst = sbVt.ap()[:, base:base + 1040].rearrange(
                        "p (h c) -> p h c", c=65)[:, 8 * oh:8 * (oh + 1), 0:64]
                    src = sbX1.ap()[:, 0:512].rearrange("p (h c) -> p h c", c=64)
                    return v.tensor_copy(dst, src)
                ch.add("vector", v_v2)

        for j in range(8):
            for (w_in, sbB, dstQ) in ((wqt, sbBq, sbQ), (wkt, sbBk, sbK)):
                for cc in range(8):
                    def d_w(s, w_in=w_in, cc=cc, j=j):
                        return s.dma_start(
                            out=sbWF[:],
                            in_=w_in[128 * cc:128 * (cc + 1),
                                     128 * j:128 * (j + 1)])
                    ch.dma("sync", d_w)

                    def v_w(v, cc=cc):
                        return v.tensor_copy(sbWqk[:, 128 * cc:128 * (cc + 1)],
                                             sbWF[:])
                    ch.add("vector", v_w)

                def t_p(t):
                    for ts4 in range(4):
                        for cc in range(8):
                            inst = t.matmul(PA[:, 512 * ts4:512 * (ts4 + 1)],
                                            sbWqk[:, 128 * cc:128 * (cc + 1)],
                                            sbXT[:, 2048 * cc + 512 * ts4:
                                                 2048 * cc + 512 * (ts4 + 1)],
                                            start=(cc == 0), stop=(cc == 7))
                    return inst
                ch.add("tensor", t_p)

                def v_bias(v, sbB=sbB, dstQ=dstQ, j=j):
                    return v.tensor_scalar_add(dstQ[:], PA[:, 0:2048],
                                               sbB[:, j:j + 1])
                ch.add("vector", v_bias)

                def t_rope(t, dstQ=dstQ):
                    for ts4 in range(4):
                        inst = t.matmul(PB[:, 512 * ts4:512 * (ts4 + 1)],
                                        Csb["c_psign"][:],
                                        dstQ[:, 512 * ts4:512 * (ts4 + 1)],
                                        start=True, stop=True)
                    return inst
                ch.add("tensor", t_rope)

                def v_rope1(v, dstQ=dstQ):
                    return v.tensor_mul(sbSc1[:], dstQ[:], Csb["c_cos"][:])
                ch.add("vector", v_rope1)

                def v_rope2(v, dstQ=dstQ):
                    return v.tensor_mul(dstQ[:], PB[:, 0:2048], Csb["c_sin"][:])
                ch.add("vector", v_rope2)

                def v_rope3(v, dstQ=dstQ):
                    return v.tensor_add(dstQ[:], dstQ[:], sbSc1[:])
                ch.add("vector", v_rope3)

            for h2 in range(2):
                h = 2 * j + h2
                hb = 64 * h2

                def v_kp(v, hb=hb):
                    return v.tensor_copy(sbKp[0:64, :], sbK[hb:hb + 64, :])
                ch.add("vector", v_kp)

                if h2 == 1:
                    def v_qp(v):
                        return v.tensor_copy(sbQp[0:64, :], sbQ[64:128, :])
                    ch.add("vector", v_qp)
                qsrc = sbQ if h2 == 0 else sbQp

                for qh in range(2):
                    for kt in range(NT):
                        def t_qk(t, kt=kt, qh=qh, qsrc=qsrc):
                            t.matmul(PA[:, 0:512], sbKp[:, 128 * kt:128 * (kt + 1)],
                                     qsrc[:, 1024 * qh:1024 * qh + 512],
                                     start=True, stop=True)
                            return t.matmul(PA[:, 512:1024],
                                            sbKp[:, 128 * kt:128 * (kt + 1)],
                                            qsrc[:, 1024 * qh + 512:1024 * (qh + 1)],
                                            start=True, stop=True)
                        ch.add("tensor", t_qk)

                        def s_exp(sc, kt=kt):
                            return sc.activation(sbA[:, 1024 * kt:1024 * (kt + 1)],
                                                 PA[:, 0:1024], AF.Exp,
                                                 scale=ATT_SCALE)
                        ch.add("scalar", s_exp)

                    def t_pv(t, h=h):
                        for kt in range(NT):
                            t.matmul(PB[0:65, 0:512],
                                     sbVt[:, 1040 * kt + 65 * h:
                                          1040 * kt + 65 * (h + 1)],
                                     sbA[:, 1024 * kt:1024 * kt + 512],
                                     start=(kt == 0), stop=(kt == NT - 1))
                        for kt in range(NT):
                            inst = t.matmul(PB[0:65, 512:1024],
                                            sbVt[:, 1040 * kt + 65 * h:
                                                 1040 * kt + 65 * (h + 1)],
                                            sbA[:, 1024 * kt + 512:1024 * (kt + 1)],
                                            start=(kt == 0), stop=(kt == NT - 1))
                        return inst
                    ch.add("tensor", t_pv)

                    def s_po(sc):
                        return sc.activation(sbPO[:], PB[0:65, 0:1024], AF.Copy)
                    ch.add("scalar", s_po)

                    for s8 in range(8):
                        def t_tr2(t, s8=s8):
                            return t.matmul(PA[:, 1536:1601],
                                            sbPO[:, 128 * s8:128 * (s8 + 1)],
                                            Csb["c_ident_b"][0:65, 0:65],
                                            start=True, stop=True)
                        ch.add("tensor", t_tr2)

                        def v_nrm1(v):
                            return v.reciprocal(P.t1[:], PA[:, 1600:1601])
                        ch.add("vector", v_nrm1)

                        def v_nrm2(v, s8=s8, qh=qh, h=h):
                            tt = 8 * qh + s8
                            return v.tensor_scalar_mul(
                                sbATok[:, 1024 * tt + 64 * h:
                                       1024 * tt + 64 * (h + 1)],
                                PA[:, 1536:1600], P.t1[:])
                        ch.add("vector", v_nrm2)

        for tt in range(NT):
            def t_tr3(t, tt=tt):
                for cc in range(8):
                    inst = t.matmul(PA[:, 128 * cc:128 * (cc + 1)],
                                    sbATok[:, 1024 * tt + 128 * cc:
                                           1024 * tt + 128 * (cc + 1)],
                                    Csb["c_ident_b"][:], start=True, stop=True)
                return inst
            ch.add("tensor", t_tr3)

            def v_tr3(v, tt=tt):
                for cc in range(8):
                    inst = v.tensor_copy(sbXT[:, 2048 * cc + 128 * tt:
                                              2048 * cc + 128 * (tt + 1)],
                                         PA[:, 128 * cc:128 * (cc + 1)])
                return inst
            ch.add("vector", v_tr3)

        for cc in range(8):
            def d_wo(s, cc=cc):
                return s.dma_start(out=P.x[:],
                                   in_=wot[128 * cc:128 * (cc + 1), :])
            ch.dma("sync", d_wo)

            def v_wo(v, cc=cc):
                return v.tensor_copy(sbW[:, D * cc:D * (cc + 1)], P.x[:])
            ch.add("vector", v_wo)

        for tt in range(NT):
            def t_o(t, tt=tt):
                for oh in range(2):
                    for cc in range(8):
                        inst = t.matmul(PB[:, 512 * oh:512 * (oh + 1)],
                                        sbXT[:, 2048 * cc + 128 * tt:
                                             2048 * cc + 128 * (tt + 1)],
                                        sbW[:, D * cc + 512 * oh:
                                            D * cc + 512 * (oh + 1)],
                                        start=(cc == 0), stop=(cc == 7))
                return inst
            ch.add("tensor", t_o)

            def d_x1b(s, tt=tt):
                return s.dma_start(out=P.x[:], in_=x1_d[128 * tt:128 * (tt + 1), :])
            ch.dma("sync", d_x1b)

            def v_x2a(v):
                return v.tensor_add(sbX2f[:], PB[:, 0:1024], sbBo[:])
            ch.add("vector", v_x2a)

            def v_x2b(v):
                return v.tensor_add(sbX2f[:], sbX2f[:], P.x[:])
            ch.add("vector", v_x2b)

            def d_x2(s, tt=tt):
                return s.dma_start(out=x2_out[128 * tt:128 * (tt + 1), :],
                                   in_=sbX2f[:])
            ch.dma("sync", d_x2)

        build_router_program(nc, Csb, x2_out, pertok2, totals2, ch, st, P,
                             sbR, PA, PB)

        ctot, gtot = ch.emit(block, chain, gsem)

        def fin(s):
            s.wait_ge(chain, ctot)
            if gtot:
                s.wait_ge(gsem, gtot)
        block.sync(fin)

    return nc, consts


def l3_inputs(consts, inp, core, x_own, y_full, slots_own, pertok_own):
    d = {k: v for k, v in consts.items()}
    d["x_own"] = x_own
    d["y_full"] = y_full
    d["slots_own"] = slots_own
    d["pertok_own"] = pertok_own
    d["r_in"] = np.ascontiguousarray(inp["L0_ffn_w_router"])
    d["wqt"] = np.ascontiguousarray(np.asarray(inp["L0_attn_Wq"]).T)
    d["wkt"] = np.ascontiguousarray(np.asarray(inp["L0_attn_Wk"]).T)
    d["wvt"] = np.ascontiguousarray(np.asarray(inp["L0_attn_Wv"]).T)
    d["wot"] = np.ascontiguousarray(np.asarray(inp["L0_attn_Wo"]).T)
    d["bq_in"] = np.ascontiguousarray(np.asarray(inp["L0_attn_bq"]).reshape(8, 128))
    d["bk_in"] = np.ascontiguousarray(np.asarray(inp["L0_attn_bk"]).reshape(8, 128))
    d["bv_rep"] = np.ascontiguousarray(
        np.broadcast_to(np.asarray(inp["L0_attn_bv"]), (128, D)))
    d["bo_rep"] = np.ascontiguousarray(
        np.broadcast_to(np.asarray(inp["L0_attn_bo"]), (128, D)))
    return d


def build_L5():
    from contextlib import ExitStack
    consts = _host_consts()
    nc = bass.Bass("TRN2", target_bir_lowering=False, debug=False,
                   detect_race_conditions=False)
    x_own = nc.dram_tensor("x_own", [S, D], F32, kind="ExternalInput")
    y_full = nc.dram_tensor("y_full", [E * CAP, D], BF16, kind="ExternalInput")
    slots_own = nc.dram_tensor("slots_own", [S, 4], F32, kind="ExternalInput")
    pertok_own = nc.dram_tensor("pertok_own", [S, 4], F32, kind="ExternalInput")
    x3_out = nc.dram_tensor("x3_out", [S, D], F32, kind="ExternalOutput")

    ch = Ch()
    with ExitStack() as st:
        P = Pool(nc, st)
        def sb(name, shape, dtype=F32):
            return st.enter_context(nc.sbuf_tensor(name, shape, dtype))
        sbYg = sb("sbYg", [128, D], BF16)
        sbSltT = sb("sbSltT", [128, 4])
        sbPkT = sb("sbPkT", [128, 4])
        sbIdxI = sb("sbIdxI", [128, 1], I32)
        sbX3 = sb("sbX3", [128, D])
        chain = st.enter_context(nc.semaphore("chain"))
        gsem = st.enter_context(nc.semaphore("gsem"))
        block = st.enter_context(nc.Block())

        def v0(v):
            v.memset(sbYg[:], 0.0)
            return v.memset(P.eps[:], LN_EPS)
        ch.add("vector", v0)

        for i in range(NT):
            def d_in(s, sem, i=i):
                s.dma_start(out=P.x[:], in_=x_own[128 * i:128 * (i + 1), :]
                            ).then_inc(sem, 16)
                s.dma_start(out=sbSltT[:], in_=slots_own[128 * i:128 * (i + 1), :]
                            ).then_inc(sem, 16)
                s.dma_start(out=sbPkT[:], in_=pertok_own[128 * i:128 * (i + 1), :]
                            ).then_inc(sem, 16)
            ch.dmas("sync", d_in, 3)

            def v_idx(v):
                return v.tensor_copy(sbIdxI[:], sbSltT[:, 0:1])
            ch.add("vector", v_idx)

            def g_y(g):
                return g.indirect_dma_start(
                    sbYg[:], None, y_full[:],
                    IndirectOffsetOnAxis(sbIdxI[:], 0),
                    bounds_check=_breg(g, E * CAP - 1), oob_is_err=False)
            ch.gdma(g_y)

            emit_ln(ch, P, None)

            def v_x3a(v):
                v.tensor_scalar(P.t1[:], sbPkT[:, 1:2], sbSltT[:, 1:2], 0.5,
                                OP.mult, OP.mult)
                return v.scalar_tensor_tensor(sbX3[:], P.xt[:], 0.5, P.x[:],
                                              OP.mult, OP.add)
            ch.add("vector", v_x3a)

            def v_x3b(v):
                return v.scalar_tensor_tensor(P.sq[:], sbYg[:], P.t1[:],
                                              sbX3[:], OP.mult, OP.add)
            ch.add("vector", v_x3b)

            def d_x3(s, i=i):
                return s.dma_start(out=x3_out[128 * i:128 * (i + 1), :],
                                   in_=P.sq[:])
            ch.dma("sync", d_x3)

        ctot, gtot = ch.emit(block, chain, gsem)

        def fin(s):
            s.wait_ge(chain, ctot)
            if gtot:
                s.wait_ge(gsem, gtot)
        block.sync(fin)

    return nc, consts


_PROGRAMS = {}


def _get_program(name, builder):
    if name not in _PROGRAMS:
        _PROGRAMS[name] = builder()
    return _PROGRAMS[name]


def kernel(**inputs):
    inputs = {k: np.asarray(v) for k, v in inputs.items()}
    x0 = np.ascontiguousarray(inputs["L0_attn_x"].reshape(NC * S, D)
                              .astype(np.float32, copy=False))
    wr = np.ascontiguousarray(inputs["L0_ffn_w_router"].astype(np.float32, copy=False))
    wg = np.asarray(inputs["L0_ffn_w_gate"], np.float32)
    wu = np.asarray(inputs["L0_ffn_w_up"], np.float32)
    wd = np.asarray(inputs["L0_ffn_w_down"], np.float32)
    cores = list(range(NC))

    nc1, consts = _get_program("L1", build_L1)
    res1 = run_bass_kernel_spmd(
        nc1, [l1_inputs(consts, x0[c * S:(c + 1) * S], wr) for c in cores], cores)
    pertok1 = np.concatenate([np.asarray(r["pertok"]) for r in res1.results], 0)
    totals1 = np.concatenate(
        [np.asarray(r["totals"]).reshape(NT, E) for r in res1.results], 0)

    nc2, consts2 = _get_program("L2", build_L2)
    res2 = run_bass_kernel_spmd(
        nc2, [l2_inputs(consts2, c, x0, pertok1, totals1, wg, wu, wd)
              for c in cores], cores)
    y1 = np.concatenate([np.asarray(r["y_out"]) for r in res2.results], 0)
    slots1 = np.asarray(res2.results[0]["slots_out"])

    nc3, consts3 = _get_program("L3", build_L3)
    res3 = run_bass_kernel_spmd(
        nc3, [l3_inputs(consts3, inputs, c, x0[c * S:(c + 1) * S], y1,
                        slots1[c * S:(c + 1) * S], pertok1[c * S:(c + 1) * S])
              for c in cores], cores)
    x2 = np.concatenate([np.asarray(r["x2_out"]) for r in res3.results], 0)
    pertok2 = np.concatenate([np.asarray(r["pertok2"]) for r in res3.results], 0)
    totals2 = np.concatenate(
        [np.asarray(r["totals2"]).reshape(NT, E) for r in res3.results], 0)

    res4 = run_bass_kernel_spmd(
        nc2, [l2_inputs(consts2, c, x2, pertok2, totals2, wg, wu, wd)
              for c in cores], cores)
    y2 = np.concatenate([np.asarray(r["y_out"]) for r in res4.results], 0)
    slots2 = np.asarray(res4.results[0]["slots_out"])

    nc5, _ = _get_program("L5", build_L5)
    res5 = run_bass_kernel_spmd(
        nc5, [{"x_own": x2[c * S:(c + 1) * S], "y_full": y2,
               "slots_own": slots2[c * S:(c + 1) * S],
               "pertok_own": pertok2[c * S:(c + 1) * S]} for c in cores], cores)
    x3 = np.concatenate([np.asarray(r["x3_out"]) for r in res5.results], 0)
    return x3.reshape(NC, S, D)


if __name__ == "__main__":
    pass


# version 03
# speedup vs baseline: 1.0426x; 1.0426x over previous
import math
import numpy as np
import ml_dtypes
import concourse.bass as bass
import concourse.mybir as mybir
from concourse.bass import IndirectOffsetOnAxis
from concourse.bass_utils import run_bass_kernel_spmd

F32 = mybir.dt.float32
BF16 = mybir.dt.bfloat16
I32 = mybir.dt.int32
U32 = mybir.dt.uint32
AX = mybir.AxisListType.X
OP = mybir.AluOpType
AF = mybir.ActivationFunctionType

NC = 8
D = 1024
S = 2048
NT = S // 128
E = 32
CAP = 640
EPC = E // NC
ROWS = EPC * CAP
NH = 16
HD = 64
NTG = NC * NT
BIG = 1.0e7
LN_EPS = 1e-5
ATT_SCALE = 1.0 / math.sqrt(HD)


class Ev:
    __slots__ = ("eng", "val")

    def __init__(self, eng, val):
        self.eng = eng
        self.val = val


class Ch:

    ENGINES = ("sync", "vector", "scalar", "tensor", "gpsimd")

    def __init__(self):
        self.steps = []
        self.counts = {e: 0 for e in self.ENGINES}
        self.counts["g"] = 0

    def _push(self, eng, fn, inc, sem_stream, after):
        if after is None:
            after = [Ev(e, self.counts[e]) for e in self.counts
                     if self.counts[e] > 0]
        self.steps.append(dict(eng=eng, fn=fn, inc=inc, sem=sem_stream,
                               after=list(after)))
        self.counts[sem_stream] += inc
        return Ev(sem_stream, self.counts[sem_stream])

    def add(self, eng, fn, inc=1, after=None):
        return self._push(eng, fn, inc, eng, after)

    def dma(self, eng, fn, after=None):
        return self._push(eng, fn, 16, eng, after)

    def dmas(self, eng, fn, n, after=None):
        fn.selfinc = True
        return self._push(eng, fn, 16 * n, eng, after)

    def gdma(self, fn, after=None):
        return self._push("gpsimd", fn, 16, "g", after)

    def emit(self, block, sems):
        per_engine = {}
        for st_ in self.steps:
            per_engine.setdefault(st_["eng"], []).append(st_)

        def mk(eng, items):
            def body(engine):
                waited = {k: 0 for k in self.counts}
                emitted = {k: 0 for k in self.counts}
                for st_ in items:
                    for ev in st_["after"]:
                        need = ev.val
                        if need <= waited[ev.eng]:
                            continue
                        engine.wait_ge(sems[ev.eng], need)
                        waited[ev.eng] = need
                    sem = sems[st_["sem"]]
                    if getattr(st_["fn"], "selfinc", False):
                        st_["fn"](engine, sem)
                    else:
                        inst = st_["fn"](engine)
                        assert inst is not None, "step fn must return last inst"
                        inst.then_inc(sem, st_["inc"])
                    emitted[st_["sem"]] += st_["inc"]
            return body

        for eng, items in per_engine.items():
            block_fn = getattr(block, eng)
            block_fn(mk(eng, items))
        return dict(self.counts)


def _host_consts():
    bf = ml_dtypes.bfloat16
    c = {}
    c["c_ident_f"] = np.eye(128, dtype=np.float32)
    c["c_ident_b"] = np.eye(128, dtype=np.float32).astype(bf)
    c["c_ltri_b"] = np.triu(np.ones((128, 128), np.float32)).astype(bf)
    c["c_lstr_b"] = (np.triu(np.ones((128, 128), np.float32)) -
                     np.eye(128, dtype=np.float32)).astype(bf)
    c["c_iota32"] = np.broadcast_to(np.arange(32, dtype=np.float32), (128, 32)).copy()
    c["c_iota128"] = np.arange(128, dtype=np.float32).reshape(128, 1).copy()
    c["c_ones_b"] = np.ones((128, 1), np.float32).astype(bf)
    inv_freq = 1.0 / 10000.0 ** (np.arange(0, HD, 2, dtype=np.float32) / HD)
    t = np.arange(S, dtype=np.float32)[:, None]
    freqs = t * inv_freq[None, :]
    emb = np.concatenate([freqs, freqs], axis=-1)
    cos = np.cos(emb).T
    sin = np.sin(emb).T
    c["c_cos"] = np.concatenate([cos, cos], axis=0).astype(bf)
    c["c_sin"] = np.concatenate([sin, sin], axis=0).astype(bf)
    P = np.zeros((128, 128), np.float32)
    for blk in (0, 64):
        for m in range(32):
            P[blk + m + 32, blk + m] = -1.0
            P[blk + m, blk + m + 32] = 1.0
    c["c_psign"] = P.astype(bf)
    return c


def _breg(g, val):
    cache = getattr(g, "_breg_cache", None)
    if cache is None:
        cache = {}
        g._breg_cache = cache
    if val not in cache:
        cache[val] = g.to_reg(val)
    return cache[val]


def _declare_consts(nc, names_shapes):
    t = {}
    for name, arr in names_shapes.items():
        dt = {np.dtype(np.float32): F32, np.dtype(ml_dtypes.bfloat16): BF16,
              np.dtype(np.int32): I32}[arr.dtype]
        t[name] = nc.dram_tensor(name, list(arr.shape), dt, kind="ExternalInput")
    return t


class Pool:
    def __init__(self, nc, st):
        def sb(name, shape, dtype=F32):
            return st.enter_context(nc.sbuf_tensor(name, shape, dtype))
        self.sb = sb
        self.x = sb("p_x", [128, D])
        self.sq = sb("p_sq", [128, D])
        self.xt = sb("p_xt", [128, D])
        self.xtT = sb("p_xtT", [128, D])
        self.s1 = sb("p_s1", [128, 1])
        self.s2 = sb("p_s2", [128, 1])
        self.mean = sb("p_mean", [128, 1])
        self.ex2 = sb("p_ex2", [128, 1])
        self.var = sb("p_var", [128, 1])
        self.rstd = sb("p_rstd", [128, 1])
        self.nmr = sb("p_nmr", [128, 1])
        self.lg = sb("p_lg", [128, 32])
        self.mx = sb("p_mx", [128, 1])
        self.nmx = sb("p_nmx", [128, 1])
        self.pe = sb("p_pe", [128, 32])
        self.sum = sb("p_sum", [128, 1])
        self.gate = sb("p_gate", [128, 1])
        self.m8 = sb("p_m8", [128, 8])
        self.ix8 = sb("p_ix8", [128, 8], U32)
        self.est = sb("p_est", [128, 1])
        self.oh = sb("p_oh", [128, 32])
        self.ohb = sb("p_ohb", [128, 32], BF16)
        self.ic = sb("p_ic", [128, 32])
        self.selp = sb("p_selp", [128, 32])
        self.pos = sb("p_pos", [128, 1])
        self.pt = sb("p_pt", [128, 1])
        self.pk = sb("p_pk", [128, 4])
        self.tot = sb("p_tot", [1, 512])
        self.t1 = sb("p_t1", [128, 1])
        self.t2 = sb("p_t2", [128, 1])
        self.t3 = sb("p_t3", [128, 1])
        self.t4 = sb("p_t4", [128, 1])
        self.eps = sb("p_eps", [128, 1])


def emit_ln(ch, P, src_getter, out=None):
    if src_getter is not None:
        ch.dma("sync", src_getter)
    out = out if out is not None else P.xt

    def v1(v):
        return v.reduce_sum(P.s1[:], P.x[:], axis=AX)
    ch.add("vector", v1)

    def s1(sc):
        return sc.activation(P.sq[:], P.x[:], AF.Square, accum_out=P.s2[:])
    ch.add("scalar", s1)

    def v2a(v):
        v.tensor_scalar_mul(P.mean[:], P.s1[:], 1.0 / D)
        return v.tensor_scalar_mul(P.ex2[:], P.s2[:], 1.0 / D)
    ch.add("vector", v2a)

    def v2b(v):
        return v.tensor_mul(P.t3[:], P.mean[:], P.mean[:])
    ch.add("vector", v2b)

    def v2c(v):
        return v.tensor_sub(P.var[:], P.ex2[:], P.t3[:])
    ch.add("vector", v2c)

    def s2a(sc):
        return sc.activation(P.t4[:], P.var[:], AF.Ln, bias=P.eps[:])
    ch.add("scalar", s2a)

    def s2b(sc):
        return sc.activation(P.rstd[:], P.t4[:], AF.Exp, scale=-0.5)
    ch.add("scalar", s2b)

    def v3(v):
        return v.tensor_scalar(out[:], P.x[:], P.mean[:], P.rstd[:],
                               OP.subtract, OP.mult)
    ch.add("vector", v3)


def emit_router_tile(ch, nc, P, C, sbR, PA, PB, ti, pertok_out, psum_tot_col):
    def t_tr(t):
        for cc in range(8):
            inst = t.matmul(PA[:, 128 * cc:128 * cc + 128],
                            P.xt[:, 128 * cc:128 * cc + 128], C["c_ident_f"][:],
                            start=True, stop=True)
        return inst
    ch.add("tensor", t_tr)

    def v_ev(v):
        return v.tensor_copy(P.xtT[:], PA[:, 0:D])
    ch.add("vector", v_ev)

    def t_lg(t):
        for cc in range(8):
            inst = t.matmul(PB[:, 0:32], P.xtT[:, 128 * cc:128 * cc + 128],
                            sbR[:, 32 * cc:32 * cc + 32],
                            start=(cc == 0), stop=(cc == 7))
        return inst
    ch.add("tensor", t_lg)

    def v_sm1(v):
        return v.tensor_copy(P.lg[:], PB[:, 0:32])
    ch.add("vector", v_sm1)

    def v_sm2(v):
        return v.reduce_max(P.mx[:], P.lg[:], axis=AX)
    ch.add("vector", v_sm2)

    def v_sm3(v):
        return v.tensor_scalar_mul(P.nmx[:], P.mx[:], -1.0)
    ch.add("vector", v_sm3)

    def s_exp(sc):
        return sc.activation(P.pe[:], P.lg[:], AF.Exp, bias=P.nmx[:],
                             scale=1.0, accum_out=P.sum[:])
    ch.add("scalar", s_exp)

    def v_t1(v):
        v.reciprocal(P.gate[:], P.sum[:])
        return v.max(P.m8[:], P.lg[:])
    ch.add("vector", v_t1)

    def v_t2(v):
        return v.max_index(P.ix8[:], P.m8[:], P.lg[:])
    ch.add("vector", v_t2)

    def v_t3(v):
        return v.tensor_copy(P.est[:], P.ix8[:, 0:1])
    ch.add("vector", v_t3)

    def v_t4(v):
        return v.tensor_single_scalar(P.oh[:], C["c_iota32"][:], P.est[:],
                                      OP.is_equal)
    ch.add("vector", v_t4)

    def v_t5(v):
        return v.tensor_copy(P.ohb[:], P.oh[:])
    ch.add("vector", v_t5)

    def t_ic(t):
        t.matmul(PB[:, 512:544], C["c_ltri_b"][:], P.ohb[:], start=True, stop=True)
        return t.matmul(psum_tot_col, C["c_ones_b"][:], P.ohb[:],
                        start=True, stop=True)
    ch.add("tensor", t_ic)

    def v_p1(v):
        return v.tensor_copy(P.ic[:], PB[:, 512:544])
    ch.add("vector", v_p1)

    def v_p2(v):
        return v.tensor_mul(P.selp[:], P.ic[:], P.oh[:])
    ch.add("vector", v_p2)

    def v_p3(v):
        return v.reduce_sum(P.pos[:], P.selp[:], axis=AX)
    ch.add("vector", v_p3)

    def v_p4(v):
        return v.tensor_scalar_add(P.pt[:], P.pos[:], -1.0)
    ch.add("vector", v_p4)

    def v_p5(v):
        v.tensor_copy(P.pk[:, 0:1], P.est[:])
        v.tensor_copy(P.pk[:, 1:2], P.gate[:])
        v.tensor_copy(P.pk[:, 2:3], P.pt[:])
        return v.memset(P.pk[:, 3:4], 0.0)
    ch.add("vector", v_p5)

    def d_out(s):
        return s.dma_start(out=pertok_out[128 * ti:128 * (ti + 1), :], in_=P.pk[:])
    ch.dma("sync", d_out)


def build_router_program(nc, C, x_in, pertok_out, totals_out, ch, st, P,
                         sbR, PA, PB):
    for i in range(NT):
        def load(s, i=i):
            return s.dma_start(out=P.x[:], in_=x_in[128 * i:128 * (i + 1), :])
        emit_ln(ch, P, load)
        emit_router_tile(ch, nc, P, C, sbR, PA, PB, i, pertok_out,
                         PB[0:1, 1024 + 32 * i:1024 + 32 * i + 32])

    def v_tot(v):
        return v.tensor_copy(P.tot[:], PB[0:1, 1024:1536])
    ch.add("vector", v_tot)

    def d_tot(s):
        return s.dma_start(out=totals_out[:], in_=P.tot[:])
    ch.dma("sync", d_tot)


def build_L1():
    from contextlib import ExitStack
    consts = _host_consts()
    nc = bass.Bass("TRN2", target_bir_lowering=False, debug=False,
                   detect_race_conditions=False)
    C = {}
    for k, v in consts.items():
        dt = BF16 if v.dtype == ml_dtypes.bfloat16 else F32
        C[k] = nc.dram_tensor(k, list(v.shape), dt, kind="ExternalInput")
    x_in = nc.dram_tensor("x_in", [S, D], F32, kind="ExternalInput")
    r_in = nc.dram_tensor("r_in", [D, E], F32, kind="ExternalInput")
    pertok_out = nc.dram_tensor("pertok", [S, 4], F32, kind="ExternalOutput")
    totals_out = nc.dram_tensor("totals", [1, NT * E], F32, kind="ExternalOutput")

    ch = Ch()
    with ExitStack() as st:
        P = Pool(nc, st)
        sbR = st.enter_context(nc.sbuf_tensor("sbR", [128, 256], F32))
        Csb = {}
        for k, v in consts.items():
            dt = BF16 if v.dtype == ml_dtypes.bfloat16 else F32
            if k in ("c_cos", "c_sin", "c_psign"):
                continue
            Csb[k] = st.enter_context(nc.sbuf_tensor("s" + k, list(v.shape), dt))
        PA = st.enter_context(nc.psum_tensor("PA", [128, 2048], F32))
        PB = st.enter_context(nc.psum_tensor("PB", [128, 2048], F32))
        SEMS = {k: st.enter_context(nc.semaphore("sem_" + k))
                for k in list(Ch.ENGINES) + ["g"]}
        block = st.enter_context(nc.Block())

        nconsts = len([k for k in consts
                       if k not in ("c_cos", "c_sin", "c_psign")]) + 1

        def load_consts(s, sem):
            for k in Csb:
                s.dma_start(out=Csb[k][:], in_=C[k][:]).then_inc(sem, 16)
            s.dma_start(out=sbR.ap().rearrange("p (c e) -> p c e", c=8),
                        in_=r_in.ap().rearrange("(c p) e -> p c e", p=128)
                        ).then_inc(sem, 16)
        ch.dmas("sync", load_consts, nconsts)

        def set_eps(v):
            return v.memset(P.eps[:], LN_EPS)
        ch.add("vector", set_eps)

        build_router_program(nc, Csb, x_in, pertok_out, totals_out, ch, st, P,
                             sbR, PA, PB)

        totals_c = ch.emit(block, SEMS)

        def fin(s):
            for k, vv in totals_c.items():
                if vv > 0:
                    s.wait_ge(SEMS[k], vv)
        block.sync(fin)

    return nc, consts


def l1_inputs(consts, x_slice, w_router):
    d = {k: v for k, v in consts.items()}
    d["x_in"] = np.ascontiguousarray(x_slice)
    d["r_in"] = np.ascontiguousarray(w_router)
    return d




def build_L2():
    from contextlib import ExitStack
    consts = _host_consts()
    nc = bass.Bass("TRN2", target_bir_lowering=False, debug=False,
                   detect_race_conditions=False)
    C = {}
    use = ("c_ident_b", "c_lstr_b", "c_iota32", "c_iota128")
    for k in use:
        v = consts[k]
        dt = BF16 if v.dtype == ml_dtypes.bfloat16 else F32
        C[k] = nc.dram_tensor(k, list(v.shape), dt, kind="ExternalInput")
    xf = nc.dram_tensor("xf", [NC * S, D], F32, kind="ExternalInput")
    ptf = nc.dram_tensor("ptf", [NC * S, 4], F32, kind="ExternalInput")
    totf = nc.dram_tensor("totf", [NTG, E], F32, kind="ExternalInput")
    wg_in = nc.dram_tensor("wg", [EPC * D, D], F32, kind="ExternalInput")
    wu_in = nc.dram_tensor("wu", [EPC * D, D], F32, kind="ExternalInput")
    wd_in = nc.dram_tensor("wd", [EPC * D, D], F32, kind="ExternalInput")
    exb = nc.dram_tensor("exb", [128, 1], F32, kind="ExternalInput")
    y_out = nc.dram_tensor("y_out", [ROWS, D], BF16, kind="ExternalOutput")
    slots_out = nc.dram_tensor("slots_out", [NC * S, 4], F32, kind="ExternalOutput")
    offs_d = nc.dram_tensor("offs_d", [NTG * E], F32)
    invd = nc.dram_tensor("invd", [ROWS + 128, 4], I32)

    ch = Ch()
    with ExitStack() as st:
        P = Pool(nc, st)
        def sb(name, shape, dtype=F32):
            return st.enter_context(nc.sbuf_tensor(name, shape, dtype))
        Csb = {}
        for k in use:
            v = consts[k]
            dt = BF16 if v.dtype == ml_dtypes.bfloat16 else F32
            Csb[k] = sb("s" + k, list(v.shape), dt)
        sbExb = sb("sbExb", [128, 1])
        sbTotB = sb("sbTotB", [128, 32], BF16)
        sbOffs = sb("sbOffs", [128, 32])
        sbOnesRow = sb("sbOnesRow", [1, 128])
        sbOffsRow = sb("sbOffsRow", [1, NTG * E])
        sbOffsBC = sb("sbOffsBC", [128, NTG * E])
        sbPkT = sb("sbPkT", [128, 4])
        sbOh = sb("sbOh", [128, 32])
        sbSlt = sb("sbSlt", [128, 4])
        sbIdxI = sb("sbIdxI", [128, 1], I32)
        sbTokF = sb("sbTokF", [128, 4])
        sbTok4 = sb("sbTok4", [128, 4], I32)
        sbSent = sb("sbSent", [128, 84], I32)
        sbInv = sb("sbInv", [128, 80], I32)
        sbWF = sb("sbWF", [128, D])
        sbWg = sb("sbWg", [128, 8 * D], BF16)
        sbWu = sb("sbWu", [128, 8 * D], BF16)
        sbWd = sb("sbWd", [128, 8 * D], BF16)
        sbXtB = sb("sbXtB", [128, D], BF16)
        sbBufT = sb("sbBufT", [128, 8 * CAP], BF16)
        sbSg = sb("sbSg", [128, CAP], BF16)
        sbST = sb("sbST", [128, 8 * CAP], BF16)
        sbYout = sb("sbYout", [128, D], BF16)
        PA = st.enter_context(nc.psum_tensor("PA", [128, 2048], F32))
        PB = st.enter_context(nc.psum_tensor("PB", [128, 2048], F32))
        SEMS = {k: st.enter_context(nc.semaphore("sem_" + k))
                for k in list(Ch.ENGINES) + ["g"]}
        block = st.enter_context(nc.Block())

        sbTotF = sb("sbTotF", [128, 32])

        def load_consts2(s, sem):
            for k in use:
                s.dma_start(out=Csb[k][:], in_=C[k][:]).then_inc(sem, 16)
            s.dma_start(out=sbExb[:], in_=exb[:]).then_inc(sem, 16)
            s.dma_start(out=sbTotF[:], in_=totf[:]).then_inc(sem, 16)
        ch.dmas("sync", load_consts2, len(use) + 2)

        def v_init(v):
            v.memset(P.eps[:], LN_EPS)
            v.memset(sbOnesRow[:], 1.0)
            v.memset(sbSent[:], 99999)
            return v.tensor_copy(sbTotB[:], sbTotF[:])
        ch.add("vector", v_init)

        def d_sent(s):
            return s.dma_start(
                out=invd.ap().rearrange("(p a) c -> p a c", p=128),
                in_=sbSent.ap().rearrange("p (a c) -> p a c", c=4))
        ch.dma("sync", d_sent)

        def t_offs(t):
            return t.matmul(PB[:, 0:32], Csb["c_lstr_b"][:], sbTotB[:],
                            start=True, stop=True)
        ch.add("tensor", t_offs)

        def v_offs(v):
            return v.tensor_copy(sbOffs[:], PB[:, 0:32])
        ch.add("vector", v_offs)

        def d_offs_out(s):
            return s.dma_start(out=offs_d.ap().rearrange("(p e) -> p e", p=128),
                               in_=sbOffs[:])
        ch.dma("sync", d_offs_out)

        def d_offs_in(s):
            return s.dma_start(out=sbOffsRow[:], in_=offs_d.ap()[None, :])
        ch.dma("sync", d_offs_in)

        for half in range(2):
            for q in range(4):
                def t_bc(t, half=half, q=q):
                    return t.matmul(PA[:, 512 * q:512 * q + 512], sbOnesRow[:],
                                    sbOffsRow[0:1, 2048 * half + 512 * q:
                                              2048 * half + 512 * q + 512],
                                    start=True, stop=True)
                ch.add("tensor", t_bc)

            def v_bc(v, half=half):
                return v.tensor_copy(sbOffsBC[:, 2048 * half:2048 * half + 2048],
                                     PA[:, 0:2048])
            ch.add("vector", v_bc)

        for T in range(NTG):
            def d_pt(s, T=T):
                return s.dma_start(out=sbPkT[:], in_=ptf[128 * T:128 * (T + 1), :])
            ch.dma("sync", d_pt)

            def vs1(v, T=T):
                v.tensor_single_scalar(sbOh[:], Csb["c_iota32"][:],
                                       sbPkT[:, 0:1], OP.is_equal)
                return v.tensor_scalar_add(P.t4[:], Csb["c_iota128"][:],
                                           float(128 * T))
            ch.add("vector", vs1)

            def vs2(v, T=T):
                v.tensor_mul(P.oh[:], sbOh[:], sbOffsBC[:, 32 * T:32 * T + 32])
                for cc in range(4):
                    inst = v.tensor_copy(sbTokF[:, cc:cc + 1], P.t4[:])
                return inst
            ch.add("vector", vs2)

            def vs3(v):
                v.reduce_sum(P.t1[:], P.oh[:], axis=AX)
                return v.tensor_copy(sbTok4[:], sbTokF[:])
            ch.add("vector", vs3)

            def vs4(v):
                return v.tensor_add(P.t1[:], P.t1[:], sbPkT[:, 2:3])
            ch.add("vector", vs4)

            def vs5(v):
                v.tensor_single_scalar(P.t2[:], P.t1[:], float(CAP), OP.is_lt)
                return v.tensor_scalar(P.t3[:], sbPkT[:, 0:1], float(CAP),
                                       None, OP.mult)
            ch.add("vector", vs5)

            def vs6(v):
                v.tensor_add(P.t3[:], P.t3[:], P.t1[:])
                return v.tensor_scalar(P.t4[:], P.t2[:], -BIG, BIG,
                                       OP.mult, OP.add)
            ch.add("vector", vs6)

            def vs7(v):
                return v.scalar_tensor_tensor(P.pos[:], P.t3[:], P.t2[:],
                                              P.t4[:], OP.mult, OP.add)
            ch.add("vector", vs7)

            def vs8(v):
                v.tensor_copy(sbSlt[:, 0:1], P.pos[:])
                v.tensor_copy(sbSlt[:, 1:2], P.t2[:])
                v.memset(sbSlt[:, 2:4], 0.0)
                return v.tensor_sub(P.t3[:], P.pos[:], sbExb[:])
            ch.add("vector", vs8)

            def vs9(v):
                v.tensor_single_scalar(P.t1[:], P.t3[:], 0.0, OP.is_ge)
                return v.tensor_single_scalar(P.t2[:], P.t3[:], float(ROWS),
                                              OP.is_lt)
            ch.add("vector", vs9)

            def vs10(v):
                return v.tensor_mul(P.t1[:], P.t1[:], P.t2[:])
            ch.add("vector", vs10)

            def vs11(v):
                v.tensor_mul(P.t3[:], P.t3[:], P.t1[:])
                return v.tensor_scalar(P.t2[:], P.t1[:], -BIG, BIG,
                                       OP.mult, OP.add)
            ch.add("vector", vs11)

            def vs12(v):
                return v.tensor_add(P.t3[:], P.t3[:], P.t2[:])
            ch.add("vector", vs12)

            def vs13(v):
                return v.tensor_copy(sbIdxI[:], P.t3[:])
            ch.add("vector", vs13)

            def d_slt(s, T=T):
                return s.dma_start(out=slots_out[128 * T:128 * (T + 1), :],
                                   in_=sbSlt[:])
            ch.dma("sync", d_slt)

            def g_scat(g, T=T):
                return g.indirect_dma_start(
                    invd[:], IndirectOffsetOnAxis(sbIdxI[:], 0),
                    sbTok4[:], None,
                    bounds_check=_breg(g, ROWS - 1), oob_is_err=False)
            ch.gdma(g_scat)

        def d_inv(s):
            return s.dma_start(
                out=sbInv.ap().rearrange("p (j c) -> p j c", c=4),
                in_=invd.ap()[0:ROWS, :].rearrange("(j p) c -> p j c", p=128))
        ch.dma("sync", d_inv)

        for e in range(EPC):
            for (w_in, sbW) in ((wg_in, sbWg), (wu_in, sbWu), (wd_in, sbWd)):
                for cc in range(8):
                    def d_w(s, w_in=w_in, e=e, cc=cc):
                        return s.dma_start(
                            out=sbWF[:],
                            in_=w_in[D * e + 128 * cc:D * e + 128 * (cc + 1), :])
                    ch.dma("sync", d_w)

                    def v_w(v, sbW=sbW, cc=cc):
                        return v.tensor_copy(sbW[:, D * cc:D * (cc + 1)], sbWF[:])
                    ch.add("vector", v_w)

            for j5 in range(5):
                def v_z(v):
                    return v.memset(P.x[:], 0.0)
                ch.add("vector", v_z)

                def g_gth(g, e=e, j5=j5):
                    col = 4 * (5 * e + j5)
                    return g.indirect_dma_start(
                        P.x[:], None, xf[:],
                        IndirectOffsetOnAxis(sbInv[:, col:col + 1], 0),
                        bounds_check=_breg(g, NC * S - 1), oob_is_err=False)
                ch.gdma(g_gth)

                emit_ln(ch, P, None)

                def v_cast(v):
                    return v.tensor_copy(sbXtB[:], P.xt[:])
                ch.add("vector", v_cast)

                def t_tr(t):
                    for cc in range(8):
                        inst = t.matmul(PA[:, 128 * cc:128 * (cc + 1)],
                                        sbXtB[:, 128 * cc:128 * (cc + 1)],
                                        Csb["c_ident_b"][:], start=True, stop=True)
                    return inst
                ch.add("tensor", t_tr)

                def v_tr(v, j5=j5):
                    for cc in range(8):
                        inst = v.tensor_copy(
                            sbBufT[:, CAP * cc + 128 * j5:CAP * cc + 128 * (j5 + 1)],
                            PA[:, 128 * cc:128 * (cc + 1)])
                    return inst
                ch.add("vector", v_tr)

            for fs in range(8):
                def t_gu(t, fs=fs):
                    for cc in range(8):
                        t.matmul(PA[:, 0:512],
                                 sbWg[:, D * cc + 128 * fs:D * cc + 128 * (fs + 1)],
                                 sbBufT[:, CAP * cc:CAP * cc + 512],
                                 start=(cc == 0), stop=(cc == 7))
                        t.matmul(PA[:, 512:640],
                                 sbWg[:, D * cc + 128 * fs:D * cc + 128 * (fs + 1)],
                                 sbBufT[:, CAP * cc + 512:CAP * cc + 640],
                                 start=(cc == 0), stop=(cc == 7))
                    for cc in range(8):
                        t.matmul(PA[:, 1024:1536],
                                 sbWu[:, D * cc + 128 * fs:D * cc + 128 * (fs + 1)],
                                 sbBufT[:, CAP * cc:CAP * cc + 512],
                                 start=(cc == 0), stop=(cc == 7))
                        inst = t.matmul(
                            PA[:, 1536:1664],
                            sbWu[:, D * cc + 128 * fs:D * cc + 128 * (fs + 1)],
                            sbBufT[:, CAP * cc + 512:CAP * cc + 640],
                            start=(cc == 0), stop=(cc == 7))
                    return inst
                ch.add("tensor", t_gu)

                def s_silu(sc):
                    return sc.activation(sbSg[:], PA[:, 0:640], AF.Silu)
                ch.add("scalar", s_silu)

                def v_st(v, fs=fs):
                    return v.tensor_tensor(sbST[:, CAP * fs:CAP * (fs + 1)],
                                           sbSg[:], PA[:, 1024:1664], OP.mult)
                ch.add("vector", v_st)

            for cs in range(5):
                def t_y(t, cs=cs):
                    for fs in range(8):
                        t.matmul(PB[:, 0:512],
                                 sbST[:, CAP * fs + 128 * cs:CAP * fs + 128 * (cs + 1)],
                                 sbWd[:, D * fs:D * fs + 512],
                                 start=(fs == 0), stop=(fs == 7))
                        inst = t.matmul(
                            PB[:, 512:1024],
                            sbST[:, CAP * fs + 128 * cs:CAP * fs + 128 * (cs + 1)],
                            sbWd[:, D * fs + 512:D * fs + 1024],
                            start=(fs == 0), stop=(fs == 7))
                    return inst
                ch.add("tensor", t_y)

                def s_yev(sc):
                    return sc.activation(sbYout[:], PB[:, 0:1024], AF.Copy)
                ch.add("scalar", s_yev)

                def d_y(s, e=e, cs=cs):
                    return s.dma_start(
                        out=y_out[CAP * e + 128 * cs:CAP * e + 128 * (cs + 1), :],
                        in_=sbYout[:])
                ch.dma("sync", d_y)

        totals_c = ch.emit(block, SEMS)

        def fin(s):
            for k, vv in totals_c.items():
                if vv > 0:
                    s.wait_ge(SEMS[k], vv)
        block.sync(fin)

    return nc, consts


def l2_inputs(consts, core, xfull, pertok_full, totals_full, w_gate, w_up, w_down):
    use = ("c_ident_b", "c_lstr_b", "c_iota32", "c_iota128")
    d = {k: consts[k] for k in use}
    d["xf"] = xfull
    d["ptf"] = pertok_full
    d["totf"] = totals_full
    d["wg"] = np.ascontiguousarray(w_gate[EPC * core:EPC * (core + 1)]).reshape(EPC * D, D)
    d["wu"] = np.ascontiguousarray(w_up[EPC * core:EPC * (core + 1)]).reshape(EPC * D, D)
    d["wd"] = np.ascontiguousarray(w_down[EPC * core:EPC * (core + 1)]).reshape(EPC * D, D)
    d["exb"] = np.full((128, 1), float(EPC * core * CAP), np.float32)
    return d


def build_L3():
    from contextlib import ExitStack
    consts = _host_consts()
    nc = bass.Bass("TRN2", target_bir_lowering=False, debug=False,
                   detect_race_conditions=False)
    C = {}
    for k, v in consts.items():
        dt = BF16 if v.dtype == ml_dtypes.bfloat16 else F32
        C[k] = nc.dram_tensor(k, list(v.shape), dt, kind="ExternalInput")
    x_own = nc.dram_tensor("x_own", [S, D], F32, kind="ExternalInput")
    y_full = nc.dram_tensor("y_full", [E * CAP, D], BF16, kind="ExternalInput")
    slots_own = nc.dram_tensor("slots_own", [S, 4], F32, kind="ExternalInput")
    pertok_own = nc.dram_tensor("pertok_own", [S, 4], F32, kind="ExternalInput")
    r_in = nc.dram_tensor("r_in", [D, E], F32, kind="ExternalInput")
    wqt = nc.dram_tensor("wqt", [D, D], F32, kind="ExternalInput")
    wkt = nc.dram_tensor("wkt", [D, D], F32, kind="ExternalInput")
    wvt = nc.dram_tensor("wvt", [D, D], F32, kind="ExternalInput")
    wot = nc.dram_tensor("wot", [D, D], F32, kind="ExternalInput")
    bq_in = nc.dram_tensor("bq_in", [8, 128], F32, kind="ExternalInput")
    bk_in = nc.dram_tensor("bk_in", [8, 128], F32, kind="ExternalInput")
    bv_rep = nc.dram_tensor("bv_rep", [128, D], F32, kind="ExternalInput")
    bo_rep = nc.dram_tensor("bo_rep", [128, D], F32, kind="ExternalInput")
    x2_out = nc.dram_tensor("x2_out", [S, D], F32, kind="ExternalOutput")
    pertok2 = nc.dram_tensor("pertok2", [S, 4], F32, kind="ExternalOutput")
    totals2 = nc.dram_tensor("totals2", [1, NT * E], F32, kind="ExternalOutput")
    x1_d = nc.dram_tensor("x1_d", [S, D], F32)

    ch = Ch()
    with ExitStack() as st:
        P = Pool(nc, st)
        def sb(name, shape, dtype=F32):
            return st.enter_context(nc.sbuf_tensor(name, shape, dtype))
        Csb = {}
        for k, v in consts.items():
            dt = BF16 if v.dtype == ml_dtypes.bfloat16 else F32
            Csb[k] = sb("s" + k, list(v.shape), dt)
        sbR = sb("sbR", [128, 256])
        sbXT = sb("sbXT", [128, NT * 1024], BF16)
        sbVt = sb("sbVt", [128, NT * 1040], BF16)
        sbA = sb("sbA", [128, NT * 1024], BF16)
        sbATok = sb("sbATok", [128, NT * 1024], BF16)
        sbW = sb("sbW", [128, 8 * D], BF16)
        sbWqk = sb("sbWqk", [128, D], BF16)
        sbWF = sb("sbWF", [128, 128])
        sbQ = sb("sbQ", [128, S], BF16)
        sbK = sb("sbK", [128, S], BF16)
        sbKp = sb("sbKp", [128, S], BF16)
        sbQp = sb("sbQp", [128, S], BF16)
        sbSc1 = sb("sbSc1", [128, S], BF16)
        sbPO = sb("sbPO", [65, 1024], BF16)
        sbBq = sb("sbBq", [128, 8])
        sbBk = sb("sbBk", [128, 8])
        sbBv = sb("sbBv", [128, D], BF16)
        sbBo = sb("sbBo", [128, D], BF16)
        sbYg = sb("sbYg", [128, D], BF16)
        sbSltT = sb("sbSltT", [128, 4])
        sbPkT = sb("sbPkT", [128, 4])
        sbIdxI = sb("sbIdxI", [128, 1], I32)
        sbX1 = P.xtT
        sbX2f = P.sq
        PA = st.enter_context(nc.psum_tensor("PA", [128, 2048], F32))
        PB = st.enter_context(nc.psum_tensor("PB", [128, 2048], F32))
        SEMS = {k: st.enter_context(nc.semaphore("sem_" + k))
                for k in list(Ch.ENGINES) + ["g"]}
        block = st.enter_context(nc.Block())

        nload = len(consts) + 1 + 16

        def load_consts(s, sem):
            for k in consts:
                s.dma_start(out=Csb[k][:], in_=C[k][:]).then_inc(sem, 16)
            s.dma_start(out=sbR.ap().rearrange("p (c e) -> p c e", c=8),
                        in_=r_in.ap().rearrange("(c p) e -> p c e", p=128)
                        ).then_inc(sem, 16)
            for j in range(8):
                s.dma_start(out=sbBq[:, j:j + 1],
                            in_=bq_in[j:j + 1, :].rearrange("a p -> p a")
                            ).then_inc(sem, 16)
                s.dma_start(out=sbBk[:, j:j + 1],
                            in_=bk_in[j:j + 1, :].rearrange("a p -> p a")
                            ).then_inc(sem, 16)
        ch.dmas("sync", load_consts, nload)

        def d_bv(s):
            return s.dma_start(out=P.x[:], in_=bv_rep[:])
        ch.dma("sync", d_bv)

        def v_bv(v):
            return v.tensor_copy(sbBv[:], P.x[:])
        ch.add("vector", v_bv)

        def d_bo(s):
            return s.dma_start(out=P.x[:], in_=bo_rep[:])
        ch.dma("sync", d_bo)

        def v_bo(v):
            return v.tensor_copy(sbBo[:], P.x[:])
        ch.add("vector", v_bo)

        def v_init(v):
            v.memset(P.eps[:], LN_EPS)
            v.memset(sbYg[:], 0.0)
            v.memset(sbKp[64:128, :], 0.0)
            v.memset(sbQp[64:128, :], 0.0)
            for tt in range(NT):
                inst = v.memset(
                    sbVt.ap()[:, 1040 * tt:1040 * (tt + 1)]
                    .rearrange("p (h c) -> p h c", c=65)[:, :, 64:65], 1.0)
            return inst
        ch.add("vector", v_init)

        for i in range(NT):
            def d_in(s, sem, i=i):
                s.dma_start(out=P.x[:], in_=x_own[128 * i:128 * (i + 1), :]
                            ).then_inc(sem, 16)
                s.dma_start(out=sbSltT[:], in_=slots_own[128 * i:128 * (i + 1), :]
                            ).then_inc(sem, 16)
                s.dma_start(out=sbPkT[:], in_=pertok_own[128 * i:128 * (i + 1), :]
                            ).then_inc(sem, 16)
            ch.dmas("sync", d_in, 3)

            def v_idx(v):
                return v.tensor_copy(sbIdxI[:], sbSltT[:, 0:1])
            ch.add("vector", v_idx)

            def g_y(g):
                return g.indirect_dma_start(
                    sbYg[:], None, y_full[:],
                    IndirectOffsetOnAxis(sbIdxI[:], 0),
                    bounds_check=_breg(g, E * CAP - 1), oob_is_err=False)
            ch.gdma(g_y)

            emit_ln(ch, P, None)

            def v_x1a(v):
                v.tensor_scalar(P.t1[:], sbPkT[:, 1:2], sbSltT[:, 1:2], 0.5,
                                OP.mult, OP.mult)
                return v.scalar_tensor_tensor(sbX1[:], P.xt[:], 0.5, P.x[:],
                                              OP.mult, OP.add)
            ch.add("vector", v_x1a)

            def v_x1b(v):
                return v.scalar_tensor_tensor(sbX2f[:], sbYg[:], P.t1[:],
                                              sbX1[:], OP.mult, OP.add)
            ch.add("vector", v_x1b)

            def d_x1(s, i=i):
                return s.dma_start(out=x1_d[128 * i:128 * (i + 1), :],
                                   in_=sbX2f[:])
            ch.dma("sync", d_x1)

        for i in range(NT):
            def d_x(s, i=i):
                return s.dma_start(out=P.x[:], in_=x1_d[128 * i:128 * (i + 1), :])
            ch.dma("sync", d_x)
            emit_ln(ch, P, None)

            def v_cb(v):
                return v.tensor_copy(sbYg[:], P.xt[:])
            ch.add("vector", v_cb)

            def t_tr(t):
                for cc in range(8):
                    inst = t.matmul(PA[:, 128 * cc:128 * (cc + 1)],
                                    sbYg[:, 128 * cc:128 * (cc + 1)],
                                    Csb["c_ident_b"][:], start=True, stop=True)
                return inst
            ch.add("tensor", t_tr)

            def v_tr(v, i=i):
                for cc in range(8):
                    inst = v.tensor_copy(sbXT[:, 2048 * cc + 128 * i:
                                              2048 * cc + 128 * (i + 1)],
                                         PA[:, 128 * cc:128 * (cc + 1)])
                return inst
            ch.add("vector", v_tr)

        for cc in range(8):
            def d_wv(s, cc=cc):
                return s.dma_start(out=P.x[:],
                                   in_=wvt[128 * cc:128 * (cc + 1), :])
            ch.dma("sync", d_wv)

            def v_wv(v, cc=cc):
                return v.tensor_copy(sbW[:, D * cc:D * (cc + 1)], P.x[:])
            ch.add("vector", v_wv)

        for tt in range(NT):
            for oh in range(2):
                def t_v(t, tt=tt, oh=oh):
                    for cc in range(8):
                        inst = t.matmul(PB[:, 0:512],
                                        sbXT[:, 2048 * cc + 128 * tt:
                                             2048 * cc + 128 * (tt + 1)],
                                        sbW[:, D * cc + 512 * oh:
                                            D * cc + 512 * (oh + 1)],
                                        start=(cc == 0), stop=(cc == 7))
                    return inst
                ch.add("tensor", t_v)

                def v_v1(v, oh=oh):
                    return v.tensor_add(sbX1[:, 0:512], PB[:, 0:512],
                                        sbBv[:, 512 * oh:512 * (oh + 1)])
                ch.add("vector", v_v1)

                def v_v2(v, tt=tt, oh=oh):
                    base = 1040 * tt
                    dst = sbVt.ap()[:, base:base + 1040].rearrange(
                        "p (h c) -> p h c", c=65)[:, 8 * oh:8 * (oh + 1), 0:64]
                    src = sbX1.ap()[:, 0:512].rearrange("p (h c) -> p h c", c=64)
                    return v.tensor_copy(dst, src)
                ch.add("vector", v_v2)

        for j in range(8):
            for (w_in, sbB, dstQ) in ((wqt, sbBq, sbQ), (wkt, sbBk, sbK)):
                for cc in range(8):
                    def d_w(s, w_in=w_in, cc=cc, j=j):
                        return s.dma_start(
                            out=sbWF[:],
                            in_=w_in[128 * cc:128 * (cc + 1),
                                     128 * j:128 * (j + 1)])
                    ch.dma("sync", d_w)

                    def v_w(v, cc=cc):
                        return v.tensor_copy(sbWqk[:, 128 * cc:128 * (cc + 1)],
                                             sbWF[:])
                    ch.add("vector", v_w)

                def t_p(t):
                    for ts4 in range(4):
                        for cc in range(8):
                            inst = t.matmul(PA[:, 512 * ts4:512 * (ts4 + 1)],
                                            sbWqk[:, 128 * cc:128 * (cc + 1)],
                                            sbXT[:, 2048 * cc + 512 * ts4:
                                                 2048 * cc + 512 * (ts4 + 1)],
                                            start=(cc == 0), stop=(cc == 7))
                    return inst
                ch.add("tensor", t_p)

                def v_bias(v, sbB=sbB, dstQ=dstQ, j=j):
                    return v.tensor_scalar_add(dstQ[:], PA[:, 0:2048],
                                               sbB[:, j:j + 1])
                ch.add("vector", v_bias)

                def t_rope(t, dstQ=dstQ):
                    for ts4 in range(4):
                        inst = t.matmul(PB[:, 512 * ts4:512 * (ts4 + 1)],
                                        Csb["c_psign"][:],
                                        dstQ[:, 512 * ts4:512 * (ts4 + 1)],
                                        start=True, stop=True)
                    return inst
                ch.add("tensor", t_rope)

                def v_rope1(v, dstQ=dstQ):
                    return v.tensor_mul(sbSc1[:], dstQ[:], Csb["c_cos"][:])
                ch.add("vector", v_rope1)

                def v_rope2(v, dstQ=dstQ):
                    return v.tensor_mul(dstQ[:], PB[:, 0:2048], Csb["c_sin"][:])
                ch.add("vector", v_rope2)

                def v_rope3(v, dstQ=dstQ):
                    return v.tensor_add(dstQ[:], dstQ[:], sbSc1[:])
                ch.add("vector", v_rope3)

            for h2 in range(2):
                h = 2 * j + h2
                hb = 64 * h2

                def v_kp(v, hb=hb):
                    return v.tensor_copy(sbKp[0:64, :], sbK[hb:hb + 64, :])
                ch.add("vector", v_kp)

                if h2 == 1:
                    def v_qp(v):
                        return v.tensor_copy(sbQp[0:64, :], sbQ[64:128, :])
                    ch.add("vector", v_qp)
                qsrc = sbQ if h2 == 0 else sbQp

                for qh in range(2):
                    for kt in range(NT):
                        def t_qk(t, kt=kt, qh=qh, qsrc=qsrc):
                            t.matmul(PA[:, 0:512], sbKp[:, 128 * kt:128 * (kt + 1)],
                                     qsrc[:, 1024 * qh:1024 * qh + 512],
                                     start=True, stop=True)
                            return t.matmul(PA[:, 512:1024],
                                            sbKp[:, 128 * kt:128 * (kt + 1)],
                                            qsrc[:, 1024 * qh + 512:1024 * (qh + 1)],
                                            start=True, stop=True)
                        ch.add("tensor", t_qk)

                        def s_exp(sc, kt=kt):
                            return sc.activation(sbA[:, 1024 * kt:1024 * (kt + 1)],
                                                 PA[:, 0:1024], AF.Exp,
                                                 scale=ATT_SCALE)
                        ch.add("scalar", s_exp)

                    def t_pv(t, h=h):
                        for kt in range(NT):
                            t.matmul(PB[0:65, 0:512],
                                     sbVt[:, 1040 * kt + 65 * h:
                                          1040 * kt + 65 * (h + 1)],
                                     sbA[:, 1024 * kt:1024 * kt + 512],
                                     start=(kt == 0), stop=(kt == NT - 1))
                        for kt in range(NT):
                            inst = t.matmul(PB[0:65, 512:1024],
                                            sbVt[:, 1040 * kt + 65 * h:
                                                 1040 * kt + 65 * (h + 1)],
                                            sbA[:, 1024 * kt + 512:1024 * (kt + 1)],
                                            start=(kt == 0), stop=(kt == NT - 1))
                        return inst
                    ch.add("tensor", t_pv)

                    def s_po(sc):
                        return sc.activation(sbPO[:], PB[0:65, 0:1024], AF.Copy)
                    ch.add("scalar", s_po)

                    for s8 in range(8):
                        def t_tr2(t, s8=s8):
                            return t.matmul(PA[:, 1536:1601],
                                            sbPO[:, 128 * s8:128 * (s8 + 1)],
                                            Csb["c_ident_b"][0:65, 0:65],
                                            start=True, stop=True)
                        ch.add("tensor", t_tr2)

                        def v_nrm1(v):
                            return v.reciprocal(P.t1[:], PA[:, 1600:1601])
                        ch.add("vector", v_nrm1)

                        def v_nrm2(v, s8=s8, qh=qh, h=h):
                            tt = 8 * qh + s8
                            return v.tensor_scalar_mul(
                                sbATok[:, 1024 * tt + 64 * h:
                                       1024 * tt + 64 * (h + 1)],
                                PA[:, 1536:1600], P.t1[:])
                        ch.add("vector", v_nrm2)

        for tt in range(NT):
            def t_tr3(t, tt=tt):
                for cc in range(8):
                    inst = t.matmul(PA[:, 128 * cc:128 * (cc + 1)],
                                    sbATok[:, 1024 * tt + 128 * cc:
                                           1024 * tt + 128 * (cc + 1)],
                                    Csb["c_ident_b"][:], start=True, stop=True)
                return inst
            ch.add("tensor", t_tr3)

            def v_tr3(v, tt=tt):
                for cc in range(8):
                    inst = v.tensor_copy(sbXT[:, 2048 * cc + 128 * tt:
                                              2048 * cc + 128 * (tt + 1)],
                                         PA[:, 128 * cc:128 * (cc + 1)])
                return inst
            ch.add("vector", v_tr3)

        for cc in range(8):
            def d_wo(s, cc=cc):
                return s.dma_start(out=P.x[:],
                                   in_=wot[128 * cc:128 * (cc + 1), :])
            ch.dma("sync", d_wo)

            def v_wo(v, cc=cc):
                return v.tensor_copy(sbW[:, D * cc:D * (cc + 1)], P.x[:])
            ch.add("vector", v_wo)

        for tt in range(NT):
            def t_o(t, tt=tt):
                for oh in range(2):
                    for cc in range(8):
                        inst = t.matmul(PB[:, 512 * oh:512 * (oh + 1)],
                                        sbXT[:, 2048 * cc + 128 * tt:
                                             2048 * cc + 128 * (tt + 1)],
                                        sbW[:, D * cc + 512 * oh:
                                            D * cc + 512 * (oh + 1)],
                                        start=(cc == 0), stop=(cc == 7))
                return inst
            ch.add("tensor", t_o)

            def d_x1b(s, tt=tt):
                return s.dma_start(out=P.x[:], in_=x1_d[128 * tt:128 * (tt + 1), :])
            ch.dma("sync", d_x1b)

            def v_x2a(v):
                return v.tensor_add(sbX2f[:], PB[:, 0:1024], sbBo[:])
            ch.add("vector", v_x2a)

            def v_x2b(v):
                return v.tensor_add(sbX2f[:], sbX2f[:], P.x[:])
            ch.add("vector", v_x2b)

            def d_x2(s, tt=tt):
                return s.dma_start(out=x2_out[128 * tt:128 * (tt + 1), :],
                                   in_=sbX2f[:])
            ch.dma("sync", d_x2)

        build_router_program(nc, Csb, x2_out, pertok2, totals2, ch, st, P,
                             sbR, PA, PB)

        totals_c = ch.emit(block, SEMS)

        def fin(s):
            for k, vv in totals_c.items():
                if vv > 0:
                    s.wait_ge(SEMS[k], vv)
        block.sync(fin)

    return nc, consts


def l3_inputs(consts, inp, core, x_own, y_full, slots_own, pertok_own):
    d = {k: v for k, v in consts.items()}
    d["x_own"] = x_own
    d["y_full"] = y_full
    d["slots_own"] = slots_own
    d["pertok_own"] = pertok_own
    d["r_in"] = np.ascontiguousarray(inp["L0_ffn_w_router"])
    d["wqt"] = np.ascontiguousarray(np.asarray(inp["L0_attn_Wq"]).T)
    d["wkt"] = np.ascontiguousarray(np.asarray(inp["L0_attn_Wk"]).T)
    d["wvt"] = np.ascontiguousarray(np.asarray(inp["L0_attn_Wv"]).T)
    d["wot"] = np.ascontiguousarray(np.asarray(inp["L0_attn_Wo"]).T)
    d["bq_in"] = np.ascontiguousarray(np.asarray(inp["L0_attn_bq"]).reshape(8, 128))
    d["bk_in"] = np.ascontiguousarray(np.asarray(inp["L0_attn_bk"]).reshape(8, 128))
    d["bv_rep"] = np.ascontiguousarray(
        np.broadcast_to(np.asarray(inp["L0_attn_bv"]), (128, D)))
    d["bo_rep"] = np.ascontiguousarray(
        np.broadcast_to(np.asarray(inp["L0_attn_bo"]), (128, D)))
    return d


def build_L5():
    from contextlib import ExitStack
    consts = _host_consts()
    nc = bass.Bass("TRN2", target_bir_lowering=False, debug=False,
                   detect_race_conditions=False)
    x_own = nc.dram_tensor("x_own", [S, D], F32, kind="ExternalInput")
    y_full = nc.dram_tensor("y_full", [E * CAP, D], BF16, kind="ExternalInput")
    slots_own = nc.dram_tensor("slots_own", [S, 4], F32, kind="ExternalInput")
    pertok_own = nc.dram_tensor("pertok_own", [S, 4], F32, kind="ExternalInput")
    x3_out = nc.dram_tensor("x3_out", [S, D], F32, kind="ExternalOutput")

    ch = Ch()
    with ExitStack() as st:
        P = Pool(nc, st)
        def sb(name, shape, dtype=F32):
            return st.enter_context(nc.sbuf_tensor(name, shape, dtype))
        sbYg = sb("sbYg", [128, D], BF16)
        sbSltT = sb("sbSltT", [128, 4])
        sbPkT = sb("sbPkT", [128, 4])
        sbIdxI = sb("sbIdxI", [128, 1], I32)
        sbX3 = sb("sbX3", [128, D])
        SEMS = {k: st.enter_context(nc.semaphore("sem_" + k))
                for k in list(Ch.ENGINES) + ["g"]}
        block = st.enter_context(nc.Block())

        def v0(v):
            v.memset(sbYg[:], 0.0)
            return v.memset(P.eps[:], LN_EPS)
        ch.add("vector", v0)

        for i in range(NT):
            def d_in(s, sem, i=i):
                s.dma_start(out=P.x[:], in_=x_own[128 * i:128 * (i + 1), :]
                            ).then_inc(sem, 16)
                s.dma_start(out=sbSltT[:], in_=slots_own[128 * i:128 * (i + 1), :]
                            ).then_inc(sem, 16)
                s.dma_start(out=sbPkT[:], in_=pertok_own[128 * i:128 * (i + 1), :]
                            ).then_inc(sem, 16)
            ch.dmas("sync", d_in, 3)

            def v_idx(v):
                return v.tensor_copy(sbIdxI[:], sbSltT[:, 0:1])
            ch.add("vector", v_idx)

            def g_y(g):
                return g.indirect_dma_start(
                    sbYg[:], None, y_full[:],
                    IndirectOffsetOnAxis(sbIdxI[:], 0),
                    bounds_check=_breg(g, E * CAP - 1), oob_is_err=False)
            ch.gdma(g_y)

            emit_ln(ch, P, None)

            def v_x3a(v):
                v.tensor_scalar(P.t1[:], sbPkT[:, 1:2], sbSltT[:, 1:2], 0.5,
                                OP.mult, OP.mult)
                return v.scalar_tensor_tensor(sbX3[:], P.xt[:], 0.5, P.x[:],
                                              OP.mult, OP.add)
            ch.add("vector", v_x3a)

            def v_x3b(v):
                return v.scalar_tensor_tensor(P.sq[:], sbYg[:], P.t1[:],
                                              sbX3[:], OP.mult, OP.add)
            ch.add("vector", v_x3b)

            def d_x3(s, i=i):
                return s.dma_start(out=x3_out[128 * i:128 * (i + 1), :],
                                   in_=P.sq[:])
            ch.dma("sync", d_x3)

        totals_c = ch.emit(block, SEMS)

        def fin(s):
            for k, vv in totals_c.items():
                if vv > 0:
                    s.wait_ge(SEMS[k], vv)
        block.sync(fin)

    return nc, consts


_PROGRAMS = {}


def _get_program(name, builder):
    if name not in _PROGRAMS:
        _PROGRAMS[name] = builder()
    return _PROGRAMS[name]


def kernel(**inputs):
    inputs = {k: np.asarray(v) for k, v in inputs.items()}
    x0 = np.ascontiguousarray(inputs["L0_attn_x"].reshape(NC * S, D)
                              .astype(np.float32, copy=False))
    wr = np.ascontiguousarray(inputs["L0_ffn_w_router"].astype(np.float32, copy=False))
    wg = np.asarray(inputs["L0_ffn_w_gate"], np.float32)
    wu = np.asarray(inputs["L0_ffn_w_up"], np.float32)
    wd = np.asarray(inputs["L0_ffn_w_down"], np.float32)
    cores = list(range(NC))

    nc1, consts = _get_program("L1", build_L1)
    res1 = run_bass_kernel_spmd(
        nc1, [l1_inputs(consts, x0[c * S:(c + 1) * S], wr) for c in cores], cores)
    pertok1 = np.concatenate([np.asarray(r["pertok"]) for r in res1.results], 0)
    totals1 = np.concatenate(
        [np.asarray(r["totals"]).reshape(NT, E) for r in res1.results], 0)

    nc2, consts2 = _get_program("L2", build_L2)
    res2 = run_bass_kernel_spmd(
        nc2, [l2_inputs(consts2, c, x0, pertok1, totals1, wg, wu, wd)
              for c in cores], cores)
    y1 = np.concatenate([np.asarray(r["y_out"]) for r in res2.results], 0)
    slots1 = np.asarray(res2.results[0]["slots_out"])

    nc3, consts3 = _get_program("L3", build_L3)
    res3 = run_bass_kernel_spmd(
        nc3, [l3_inputs(consts3, inputs, c, x0[c * S:(c + 1) * S], y1,
                        slots1[c * S:(c + 1) * S], pertok1[c * S:(c + 1) * S])
              for c in cores], cores)
    x2 = np.concatenate([np.asarray(r["x2_out"]) for r in res3.results], 0)
    pertok2 = np.concatenate([np.asarray(r["pertok2"]) for r in res3.results], 0)
    totals2 = np.concatenate(
        [np.asarray(r["totals2"]).reshape(NT, E) for r in res3.results], 0)

    res4 = run_bass_kernel_spmd(
        nc2, [l2_inputs(consts2, c, x2, pertok2, totals2, wg, wu, wd)
              for c in cores], cores)
    y2 = np.concatenate([np.asarray(r["y_out"]) for r in res4.results], 0)
    slots2 = np.asarray(res4.results[0]["slots_out"])

    nc5, _ = _get_program("L5", build_L5)
    res5 = run_bass_kernel_spmd(
        nc5, [{"x_own": x2[c * S:(c + 1) * S], "y_full": y2,
               "slots_own": slots2[c * S:(c + 1) * S],
               "pertok_own": pertok2[c * S:(c + 1) * S]} for c in cores], cores)
    x3 = np.concatenate([np.asarray(r["x3_out"]) for r in res5.results], 0)
    return x3.reshape(NC, S, D)


if __name__ == "__main__":
    pass


# version 04
# speedup vs baseline: 1.4914x; 1.4304x over previous
import math
import numpy as np
import ml_dtypes
import concourse.bass as bass
import concourse.mybir as mybir
from concourse.bass import IndirectOffsetOnAxis
from concourse.bass_utils import run_bass_kernel_spmd

F32 = mybir.dt.float32
BF16 = mybir.dt.bfloat16
I32 = mybir.dt.int32
U32 = mybir.dt.uint32
AX = mybir.AxisListType.X
OP = mybir.AluOpType
AF = mybir.ActivationFunctionType

NC = 8
D = 1024
S = 2048
NT = S // 128
E = 32
CAP = 640
EPC = E // NC
ROWS = EPC * CAP
NH = 16
HD = 64
NTG = NC * NT
BIG = 1.0e7
LN_EPS = 1e-5
ATT_SCALE = 1.0 / math.sqrt(HD)


class Ev:
    __slots__ = ("eng", "val")

    def __init__(self, eng, val):
        self.eng = eng
        self.val = val


class Ch:

    ENGINES = ("sync", "vector", "scalar", "tensor", "gpsimd")

    def __init__(self):
        self.steps = []
        self.counts = {e: 0 for e in self.ENGINES}
        self.counts["g"] = 0

    def _push(self, eng, fn, inc, sem_stream, after):
        if getattr(self, "force_serial", False):
            after = None
        if after is None:
            after = [Ev(e, self.counts[e]) for e in self.counts
                     if self.counts[e] > 0]
        self.steps.append(dict(eng=eng, fn=fn, inc=inc, sem=sem_stream,
                               after=list(after)))
        self.counts[sem_stream] += inc
        return Ev(sem_stream, self.counts[sem_stream])

    def add(self, eng, fn, inc=1, after=None):
        return self._push(eng, fn, inc, eng, after)

    def dma(self, eng, fn, after=None):
        return self._push(eng, fn, 16, eng, after)

    def dmas(self, eng, fn, n, after=None):
        fn.selfinc = True
        return self._push(eng, fn, 16 * n, eng, after)

    def gdma(self, fn, after=None):
        return self._push("gpsimd", fn, 16, "g", after)

    def emit(self, block, sems):
        per_engine = {}
        for st_ in self.steps:
            per_engine.setdefault(st_["eng"], []).append(st_)

        def mk(eng, items):
            def body(engine):
                waited = {k: 0 for k in self.counts}
                emitted = {k: 0 for k in self.counts}
                for st_ in items:
                    for ev in st_["after"]:
                        need = ev.val
                        if need <= waited[ev.eng]:
                            continue
                        engine.wait_ge(sems[ev.eng], need)
                        waited[ev.eng] = need
                    sem = sems[st_["sem"]]
                    if getattr(st_["fn"], "selfinc", False):
                        st_["fn"](engine, sem)
                    else:
                        inst = st_["fn"](engine)
                        assert inst is not None, "step fn must return last inst"
                        inst.then_inc(sem, st_["inc"])
                    emitted[st_["sem"]] += st_["inc"]
            return body

        for eng, items in per_engine.items():
            block_fn = getattr(block, eng)
            block_fn(mk(eng, items))
        return dict(self.counts)


def _host_consts():
    bf = ml_dtypes.bfloat16
    c = {}
    c["c_ident_f"] = np.eye(128, dtype=np.float32)
    c["c_ident_b"] = np.eye(128, dtype=np.float32).astype(bf)
    c["c_ltri_b"] = np.triu(np.ones((128, 128), np.float32)).astype(bf)
    c["c_lstr_b"] = (np.triu(np.ones((128, 128), np.float32)) -
                     np.eye(128, dtype=np.float32)).astype(bf)
    c["c_iota32"] = np.broadcast_to(np.arange(32, dtype=np.float32), (128, 32)).copy()
    c["c_iota128"] = np.arange(128, dtype=np.float32).reshape(128, 1).copy()
    c["c_ones_b"] = np.ones((128, 1), np.float32).astype(bf)
    inv_freq = 1.0 / 10000.0 ** (np.arange(0, HD, 2, dtype=np.float32) / HD)
    t = np.arange(S, dtype=np.float32)[:, None]
    freqs = t * inv_freq[None, :]
    emb = np.concatenate([freqs, freqs], axis=-1)
    cos = np.cos(emb).T
    sin = np.sin(emb).T
    c["c_cos"] = np.concatenate([cos, cos], axis=0).astype(bf)
    c["c_sin"] = np.concatenate([sin, sin], axis=0).astype(bf)
    P = np.zeros((128, 128), np.float32)
    for blk in (0, 64):
        for m in range(32):
            P[blk + m + 32, blk + m] = -1.0
            P[blk + m, blk + m + 32] = 1.0
    c["c_psign"] = P.astype(bf)
    tok = (np.arange(NTG)[None, :] * 128 + np.arange(128)[:, None]).astype(np.int32)
    c["c_tokid"] = np.repeat(tok, 4, axis=1).copy()
    return c


def _breg(g, val):
    cache = getattr(g, "_breg_cache", None)
    if cache is None:
        cache = {}
        g._breg_cache = cache
    if val not in cache:
        cache[val] = g.to_reg(val)
    return cache[val]


def _declare_consts(nc, names_shapes):
    t = {}
    for name, arr in names_shapes.items():
        dt = {np.dtype(np.float32): F32, np.dtype(ml_dtypes.bfloat16): BF16,
              np.dtype(np.int32): I32}[arr.dtype]
        t[name] = nc.dram_tensor(name, list(arr.shape), dt, kind="ExternalInput")
    return t


class Pool:
    def __init__(self, nc, st):
        def sb(name, shape, dtype=F32):
            return st.enter_context(nc.sbuf_tensor(name, shape, dtype))
        self.sb = sb
        self.x = sb("p_x", [128, D])
        self.sq = sb("p_sq", [128, D])
        self.xt = sb("p_xt", [128, D])
        self.xtT = sb("p_xtT", [128, D])
        self.s1 = sb("p_s1", [128, 1])
        self.s2 = sb("p_s2", [128, 1])
        self.mean = sb("p_mean", [128, 1])
        self.ex2 = sb("p_ex2", [128, 1])
        self.var = sb("p_var", [128, 1])
        self.rstd = sb("p_rstd", [128, 1])
        self.nmr = sb("p_nmr", [128, 1])
        self.lg = sb("p_lg", [128, 32])
        self.mx = sb("p_mx", [128, 1])
        self.nmx = sb("p_nmx", [128, 1])
        self.pe = sb("p_pe", [128, 32])
        self.sum = sb("p_sum", [128, 1])
        self.gate = sb("p_gate", [128, 1])
        self.m8 = sb("p_m8", [128, 8])
        self.ix8 = sb("p_ix8", [128, 8], U32)
        self.est = sb("p_est", [128, 1])
        self.oh = sb("p_oh", [128, 32])
        self.ohb = sb("p_ohb", [128, 32], BF16)
        self.ic = sb("p_ic", [128, 32])
        self.selp = sb("p_selp", [128, 32])
        self.pos = sb("p_pos", [128, 1])
        self.pt = sb("p_pt", [128, 1])
        self.pk = sb("p_pk", [128, 4])
        self.tot = sb("p_tot", [1, 512])
        self.t1 = sb("p_t1", [128, 1])
        self.t2 = sb("p_t2", [128, 1])
        self.t3 = sb("p_t3", [128, 1])
        self.t4 = sb("p_t4", [128, 1])
        self.eps = sb("p_eps", [128, 1])
        self.one = sb("p_one", [128, 1])


def emit_ln(ch, P, src_getter, out=None, src=None, after=None, guard=None):
    if src_getter is not None:
        ev = ch.dma("sync", src_getter)
        after = [ev] if after is None else list(after) + [ev]
    out = out if out is not None else P.xt
    src = src if src is not None else P.x

    def v1(v):
        return v.reduce_sum(P.s1[:], src[:], axis=AX)
    e_v1 = ch.add("vector", v1, after=after)

    def s1(sc):
        return sc.activation(P.sq[:], src[:], AF.Square, accum_out=P.s2[:])
    e_s1 = ch.add("scalar", s1, after=after)

    def v2a(v):
        v.tensor_scalar_mul(P.mean[:], P.s1[:], 1.0 / D)
        return v.tensor_scalar_mul(P.ex2[:], P.s2[:], 1.0 / D)
    e_v2a = ch.add("vector", v2a, after=[e_v1, e_s1])

    def v2b(v):
        return v.tensor_mul(P.t3[:], P.mean[:], P.mean[:])
    e_v2b = ch.add("vector", v2b, after=[e_v2a])

    def v2c(v):
        return v.tensor_sub(P.var[:], P.ex2[:], P.t3[:])
    e_v2c = ch.add("vector", v2c, after=[e_v2b])

    def s2a(sc):
        return sc.activation(P.t4[:], P.var[:], AF.Ln, bias=P.eps[:])
    e_s2a = ch.add("scalar", s2a, after=[e_v2c])

    def s2b(sc):
        return sc.activation(P.rstd[:], P.t4[:], AF.Exp, scale=-0.5)
    e_s2b = ch.add("scalar", s2b, after=[e_s2a])

    def v3(v):
        return v.tensor_scalar(out[:], src[:], P.mean[:], P.rstd[:],
                               OP.subtract, OP.mult)
    return ch.add("vector", v3,
                  after=[e_s2b, e_v2a] + (list(guard) if guard else []))


def emit_router_tile(ch, nc, P, C, sbR, PA, PB, ti, pertok_out, psum_tot_col):
    def t_tr(t):
        for cc in range(8):
            inst = t.matmul(PA[:, 128 * cc:128 * cc + 128],
                            P.xt[:, 128 * cc:128 * cc + 128], C["c_ident_f"][:],
                            start=True, stop=True)
        return inst
    ch.add("tensor", t_tr)

    def v_ev(v):
        return v.tensor_copy(P.xtT[:], PA[:, 0:D])
    ch.add("vector", v_ev)

    def t_lg(t):
        for cc in range(8):
            inst = t.matmul(PB[:, 0:32], P.xtT[:, 128 * cc:128 * cc + 128],
                            sbR[:, 32 * cc:32 * cc + 32],
                            start=(cc == 0), stop=(cc == 7))
        return inst
    ch.add("tensor", t_lg)

    def v_sm1(v):
        return v.tensor_copy(P.lg[:], PB[:, 0:32])
    ch.add("vector", v_sm1)

    def v_sm2(v):
        return v.reduce_max(P.mx[:], P.lg[:], axis=AX)
    ch.add("vector", v_sm2)

    def v_sm3(v):
        return v.tensor_scalar_mul(P.nmx[:], P.mx[:], -1.0)
    ch.add("vector", v_sm3)

    def s_exp(sc):
        return sc.activation(P.pe[:], P.lg[:], AF.Exp, bias=P.nmx[:],
                             scale=1.0, accum_out=P.sum[:])
    ch.add("scalar", s_exp)

    def v_t1(v):
        v.reciprocal(P.gate[:], P.sum[:])
        return v.max(P.m8[:], P.lg[:])
    ch.add("vector", v_t1)

    def v_t2(v):
        return v.max_index(P.ix8[:], P.m8[:], P.lg[:])
    ch.add("vector", v_t2)

    def v_t3(v):
        return v.tensor_copy(P.est[:], P.ix8[:, 0:1])
    ch.add("vector", v_t3)

    def v_t4(v):
        return v.tensor_single_scalar(P.oh[:], C["c_iota32"][:], P.est[:],
                                      OP.is_equal)
    ch.add("vector", v_t4)

    def v_t5(v):
        return v.tensor_copy(P.ohb[:], P.oh[:])
    ch.add("vector", v_t5)

    def t_ic(t):
        t.matmul(PB[:, 512:544], C["c_ltri_b"][:], P.ohb[:], start=True, stop=True)
        return t.matmul(psum_tot_col, C["c_ones_b"][:], P.ohb[:],
                        start=True, stop=True)
    ch.add("tensor", t_ic)

    def v_p1(v):
        return v.tensor_copy(P.ic[:], PB[:, 512:544])
    ch.add("vector", v_p1)

    def v_p2(v):
        return v.tensor_mul(P.selp[:], P.ic[:], P.oh[:])
    ch.add("vector", v_p2)

    def v_p3(v):
        return v.reduce_sum(P.pos[:], P.selp[:], axis=AX)
    ch.add("vector", v_p3)

    def v_p4(v):
        return v.tensor_scalar_add(P.pt[:], P.pos[:], -1.0)
    ch.add("vector", v_p4)

    def v_p5(v):
        v.tensor_copy(P.pk[:, 0:1], P.est[:])
        v.tensor_copy(P.pk[:, 1:2], P.gate[:])
        v.tensor_copy(P.pk[:, 2:3], P.pt[:])
        return v.memset(P.pk[:, 3:4], 0.0)
    ch.add("vector", v_p5)

    def d_out(s):
        return s.dma_start(out=pertok_out[128 * ti:128 * (ti + 1), :], in_=P.pk[:])
    ch.dma("sync", d_out)


def build_router_program(nc, C, x_in, pertok_out, totals_out, ch, st, P,
                         sbR, PA, PB):
    for i in range(NT):
        def load(s, i=i):
            return s.dma_start(out=P.x[:], in_=x_in[128 * i:128 * (i + 1), :])
        emit_ln(ch, P, load)
        emit_router_tile(ch, nc, P, C, sbR, PA, PB, i, pertok_out,
                         PB[0:1, 1024 + 32 * i:1024 + 32 * i + 32])

    def v_tot(v):
        return v.tensor_copy(P.tot[:], PB[0:1, 1024:1536])
    ch.add("vector", v_tot)

    def d_tot(s):
        return s.dma_start(out=totals_out[:], in_=P.tot[:])
    ch.dma("sync", d_tot)


def build_L1():
    from contextlib import ExitStack
    consts = _host_consts()
    nc = bass.Bass("TRN2", target_bir_lowering=False, debug=False,
                   detect_race_conditions=False)
    C = {}
    for k, v in consts.items():
        dt = BF16 if v.dtype == ml_dtypes.bfloat16 else F32
        C[k] = nc.dram_tensor(k, list(v.shape), dt, kind="ExternalInput")
    x_in = nc.dram_tensor("x_in", [S, D], F32, kind="ExternalInput")
    r_in = nc.dram_tensor("r_in", [D, E], F32, kind="ExternalInput")
    pertok_out = nc.dram_tensor("pertok", [S, 4], F32, kind="ExternalOutput")
    totals_out = nc.dram_tensor("totals", [1, NT * E], F32, kind="ExternalOutput")

    ch = Ch()
    with ExitStack() as st:
        P = Pool(nc, st)
        sbR = st.enter_context(nc.sbuf_tensor("sbR", [128, 256], F32))
        Csb = {}
        for k, v in consts.items():
            dt = BF16 if v.dtype == ml_dtypes.bfloat16 else F32
            if k in ("c_cos", "c_sin", "c_psign"):
                continue
            Csb[k] = st.enter_context(nc.sbuf_tensor("s" + k, list(v.shape), dt))
        PA = st.enter_context(nc.psum_tensor("PA", [128, 2048], F32))
        PB = st.enter_context(nc.psum_tensor("PB", [128, 2048], F32))
        SEMS = {k: st.enter_context(nc.semaphore("sem_" + k))
                for k in list(Ch.ENGINES) + ["g"]}
        block = st.enter_context(nc.Block())

        nconsts = len([k for k in consts
                       if k not in ("c_cos", "c_sin", "c_psign")]) + 1

        def load_consts(s, sem):
            for k in Csb:
                s.dma_start(out=Csb[k][:], in_=C[k][:]).then_inc(sem, 16)
            s.dma_start(out=sbR.ap().rearrange("p (c e) -> p c e", c=8),
                        in_=r_in.ap().rearrange("(c p) e -> p c e", p=128)
                        ).then_inc(sem, 16)
        ch.dmas("sync", load_consts, nconsts)

        def set_eps(v):
            return v.memset(P.eps[:], LN_EPS)
        ch.add("vector", set_eps)

        build_router_program(nc, Csb, x_in, pertok_out, totals_out, ch, st, P,
                             sbR, PA, PB)

        totals_c = ch.emit(block, SEMS)

        def fin(s):
            for k, vv in totals_c.items():
                if vv > 0:
                    s.wait_ge(SEMS[k], vv)
        block.sync(fin)

    return nc, consts


def l1_inputs(consts, x_slice, w_router):
    d = {k: v for k, v in consts.items()}
    d["x_in"] = np.ascontiguousarray(x_slice)
    d["r_in"] = np.ascontiguousarray(w_router)
    return d




def build_L2():
    from contextlib import ExitStack
    consts = _host_consts()
    nc = bass.Bass("TRN2", target_bir_lowering=False, debug=False,
                   detect_race_conditions=False)
    C = {}
    use = ("c_ident_b", "c_lstr_b", "c_iota32", "c_iota128")
    for k in use:
        v = consts[k]
        dt = BF16 if v.dtype == ml_dtypes.bfloat16 else F32
        C[k] = nc.dram_tensor(k, list(v.shape), dt, kind="ExternalInput")
    C["c_tokid"] = nc.dram_tensor("c_tokid", [128, NTG * 4], I32,
                                  kind="ExternalInput")
    xf = nc.dram_tensor("xf", [NC * S, D], F32, kind="ExternalInput")
    ptf = nc.dram_tensor("ptf", [NC * S, 4], F32, kind="ExternalInput")
    totf = nc.dram_tensor("totf", [NTG, E], F32, kind="ExternalInput")
    wg_in = nc.dram_tensor("wg", [EPC * D, D], F32, kind="ExternalInput")
    wu_in = nc.dram_tensor("wu", [EPC * D, D], F32, kind="ExternalInput")
    wd_in = nc.dram_tensor("wd", [EPC * D, D], F32, kind="ExternalInput")
    exb = nc.dram_tensor("exb", [128, 1], F32, kind="ExternalInput")
    y_out = nc.dram_tensor("y_out", [ROWS, D], BF16, kind="ExternalOutput")
    slots_out = nc.dram_tensor("slots_out", [NC * S, 4], F32, kind="ExternalOutput")
    offs_d = nc.dram_tensor("offs_d", [NTG * E], F32)
    invd = nc.dram_tensor("invd", [ROWS + 128, 4], I32)

    ch = Ch()
    with ExitStack() as st:
        P = Pool(nc, st)
        def sb(name, shape, dtype=F32):
            return st.enter_context(nc.sbuf_tensor(name, shape, dtype))
        Csb = {}
        for k in use:
            v = consts[k]
            dt = BF16 if v.dtype == ml_dtypes.bfloat16 else F32
            Csb[k] = sb("s" + k, list(v.shape), dt)
        sbExb = sb("sbExb", [128, 1])
        sbTotB = sb("sbTotB", [128, 32], BF16)
        sbOffs = sb("sbOffs", [128, 32])
        sbOnesRow = sb("sbOnesRow", [1, 128])
        sbOffsRow = sb("sbOffsRow", [1, 512])
        sbOffsBC = sb("sbOffsBC", [128, NTG * E])
        sbPk8 = [sb("sbPk8a", [128, 32]), sb("sbPk8b", [128, 32])]
        sbOh8 = [sb("sbOh8a", [128, 256]), sb("sbOh8b", [128, 256])]
        sbW8 = [sb("sbW8a", [128, 32]), sb("sbW8b", [128, 32])]
        sbIdx8 = [sb("sbIdx8a", [128, 8], I32), sb("sbIdx8b", [128, 8], I32)]
        sbSlt8 = [sb("sbSlt8a", [128, 32]), sb("sbSlt8b", [128, 32])]
        sbTok = sb("sbTok", [128, NTG * 4], I32)
        sbSent = sb("sbSent", [128, 84], I32)
        sbInv = sb("sbInv", [128, 80], I32)
        sbWF = [sb("sbWFa", [128, D]), sb("sbWFb", [128, D])]
        sbWg = [sb("sbWg0", [128, 8 * D], BF16), sb("sbWg1", [128, 8 * D], BF16)]
        sbWu = [sb("sbWu0", [128, 8 * D], BF16), sb("sbWu1", [128, 8 * D], BF16)]
        sbWd = [sb("sbWd0", [128, 8 * D], BF16), sb("sbWd1", [128, 8 * D], BF16)]
        sbGx = [sb("sbGx0", [128, D]), sb("sbGx1", [128, D])]
        sbXtB = [sb("sbXtB0", [128, D], BF16), sb("sbXtB1", [128, D], BF16)]
        sbBufT = [sb("sbBufT0", [128, 8 * CAP], BF16),
                  sb("sbBufT1", [128, 8 * CAP], BF16)]
        sbSg = sb("sbSg", [128, CAP], BF16)
        sbSg2 = sb("sbSg2", [128, CAP], BF16)
        sbST = [sb("sbST0", [128, 8 * CAP], BF16),
                sb("sbST1", [128, 8 * CAP], BF16)]
        sbYout = [sb("sbYout0", [128, D], BF16), sb("sbYout1", [128, D], BF16)]
        PA = st.enter_context(nc.psum_tensor("PA", [128, 2048], F32))
        PB = st.enter_context(nc.psum_tensor("PB", [128, 2048], F32))
        SEMS = {k: st.enter_context(nc.semaphore("sem_" + k))
                for k in list(Ch.ENGINES) + ["g"]}
        block = st.enter_context(nc.Block())

        sbTotF = sb("sbTotF", [128, 32])

        def load_consts2(s, sem):
            for k in use:
                s.dma_start(out=Csb[k][:], in_=C[k][:]).then_inc(sem, 16)
            s.dma_start(out=sbExb[:], in_=exb[:]).then_inc(sem, 16)
            s.dma_start(out=sbTotF[:], in_=totf[:]).then_inc(sem, 16)
            s.dma_start(out=sbTok[:], in_=C["c_tokid"][:]).then_inc(sem, 16)
        ev_tok = ch.dmas("sync", load_consts2, len(use) + 3)

        def v_init(v):
            v.memset(P.eps[:], LN_EPS)
            v.memset(P.one[:], 1.0)
            v.memset(sbOnesRow[:], 1.0)
            v.memset(sbSent[:], 99999)
            return v.tensor_copy(sbTotB[:], sbTotF[:])
        ch.add("vector", v_init)

        def d_sent(s):
            return s.dma_start(
                out=invd.ap().rearrange("(p a) c -> p a c", p=128),
                in_=sbSent.ap().rearrange("p (a c) -> p a c", c=4))
        ch.dma("sync", d_sent)

        def t_offs(t):
            return t.matmul(PB[:, 0:32], Csb["c_lstr_b"][:], sbTotB[:],
                            start=True, stop=True)
        ch.add("tensor", t_offs)

        def v_offs(v):
            return v.tensor_copy(sbOffs[:], PB[:, 0:32])
        ch.add("vector", v_offs)

        def d_offs_out(s):
            return s.dma_start(out=offs_d.ap().rearrange("(p e) -> p e", p=128),
                               in_=sbOffs[:])
        ch.dma("sync", d_offs_out)

        for half in range(2):
            for q in range(4):
                def d_offs_in(s, half=half, q=q):
                    o = 2048 * half + 512 * q
                    return s.dma_start(out=sbOffsRow[:],
                                       in_=offs_d.ap()[None, o:o + 512])
                ch.dma("sync", d_offs_in)

                def t_bc(t, half=half, q=q):
                    return t.matmul(PA[:, 512 * q:512 * q + 512], sbOnesRow[:],
                                    sbOffsRow[:], start=True, stop=True)
                ch.add("tensor", t_bc)

            def v_bc(v, half=half):
                return v.tensor_copy(sbOffsBC[:, 2048 * half:2048 * half + 2048],
                                     PA[:, 0:2048])
            ch.add("vector", v_bc)

        ev_bc = Ev("vector", ch.counts["vector"])
        ev_sent = Ev("sync", ch.counts["sync"])
        prev_grp = {0: None, 1: None}
        ev_scat = []
        for G in range(NTG // 8):
            par = G % 2
            pk8 = sbPk8[par]
            oh8 = sbOh8[par]
            w8 = sbW8[par]
            idx8 = sbIdx8[par]
            slt8 = sbSlt8[par]
            aft = list(prev_grp[par]) if prev_grp[par] else None

            def d_pt(s, G=G, pk8=pk8):
                return s.dma_start(
                    out=pk8.ap().rearrange("p (t c) -> p t c", c=4),
                    in_=ptf[1024 * G:1024 * (G + 1), :]
                    .rearrange("(t p) c -> p t c", p=128))
            e_pt = ch.dma("sync", d_pt, after=aft)

            def l_oh(v, G=G, pk8=pk8, oh8=oh8):
                for t in range(8):
                    inst = v.tensor_single_scalar(
                        oh8[:, 32 * t:32 * (t + 1)], Csb["c_iota32"][:],
                        pk8[:, 4 * t:4 * t + 1], OP.is_equal)
                return inst
            e1 = ch.add("vector", l_oh, after=[e_pt] +
                        (list(prev_grp[par]) if prev_grp[par] else []))

            def l_mul(v, G=G, oh8=oh8):
                for t in range(8):
                    T = 8 * G + t
                    inst = v.tensor_mul(oh8[:, 32 * t:32 * (t + 1)],
                                        oh8[:, 32 * t:32 * (t + 1)],
                                        sbOffsBC[:, 32 * T:32 * (T + 1)])
                return inst
            e2 = ch.add("vector", l_mul, after=[e1, ev_bc])

            def l_red(v, oh8=oh8, w8=w8):
                for t in range(8):
                    inst = v.reduce_sum(w8[:, t:t + 1],
                                        oh8[:, 32 * t:32 * (t + 1)], axis=AX)
                return inst
            e3 = ch.add("vector", l_red, after=[e2])

            def l_a(v, pk8=pk8, w8=w8):
                pt8 = pk8.ap().rearrange("p (t c) -> p t c", c=4)[:, :, 2:3]\
                    .rearrange("p t c -> p (t c)")
                return v.tensor_add(w8[:, 0:8], w8[:, 0:8], pt8)
            e4 = ch.add("vector", l_a, after=[e3])

            def l_b(v, pk8=pk8, w8=w8):
                v.tensor_single_scalar(w8[:, 8:16], w8[:, 0:8], float(CAP),
                                       OP.is_lt)
                es8 = pk8.ap().rearrange("p (t c) -> p t c", c=4)[:, :, 0:1]\
                    .rearrange("p t c -> p (t c)")
                return v.tensor_scalar(w8[:, 16:24], es8, float(CAP), None,
                                       OP.mult)
            e5 = ch.add("vector", l_b, after=[e4])

            def l_c(v, w8=w8):
                v.tensor_add(w8[:, 16:24], w8[:, 16:24], w8[:, 0:8])
                return v.tensor_scalar(w8[:, 24:32], w8[:, 8:16], -BIG, BIG,
                                       OP.mult, OP.add)
            e6 = ch.add("vector", l_c, after=[e5])

            def l_d(v, w8=w8):
                return v.tensor_mul(w8[:, 0:8], w8[:, 16:24], w8[:, 8:16])
            e7 = ch.add("vector", l_d, after=[e6])

            def l_e(v, w8=w8):
                return v.tensor_add(w8[:, 0:8], w8[:, 0:8], w8[:, 24:32])
            e8 = ch.add("vector", l_e, after=[e7])

            def l_f(v, w8=w8):
                return v.tensor_scalar(w8[:, 16:24], w8[:, 0:8], sbExb[:],
                                       None, OP.subtract)
            e9 = ch.add("vector", l_f, after=[e8])

            def l_g(v, w8=w8):
                v.tensor_single_scalar(w8[:, 24:32], w8[:, 16:24], 0.0,
                                       OP.is_ge)
                return v.tensor_single_scalar(w8[:, 8:16], w8[:, 16:24],
                                              float(ROWS), OP.is_lt)
            e10 = ch.add("vector", l_g, after=[e9])

            def l_h(v, w8=w8):
                return v.tensor_mul(w8[:, 24:32], w8[:, 24:32], w8[:, 8:16])
            e11 = ch.add("vector", l_h, after=[e10])

            def l_i(v, w8=w8):
                v.tensor_mul(w8[:, 16:24], w8[:, 16:24], w8[:, 24:32])
                return v.tensor_scalar(w8[:, 8:16], w8[:, 24:32], -BIG, BIG,
                                       OP.mult, OP.add)
            e12 = ch.add("vector", l_i, after=[e11])

            def l_j(v, w8=w8):
                return v.tensor_add(w8[:, 16:24], w8[:, 16:24], w8[:, 8:16])
            e13 = ch.add("vector", l_j, after=[e12])

            def l_k(v, w8=w8, idx8=idx8):
                return v.tensor_copy(idx8[:], w8[:, 16:24])
            e14 = ch.add("vector", l_k, after=[e13])

            def l_m(v, w8=w8, slt8=slt8):
                dst = slt8.ap().rearrange("p (t c) -> p t c", c=4)
                v.tensor_copy(dst[:, :, 0:1].rearrange("p t c -> p (t c)"),
                              w8[:, 0:8])
                v.tensor_single_scalar(
                    dst[:, :, 1:2].rearrange("p t c -> p (t c)"),
                    w8[:, 0:8], float(E * CAP), OP.is_lt)
                return v.memset(dst[:, :, 2:4], 0.0)
            e15 = ch.add("vector", l_m, after=[e8, e13])

            def d_slt(s, G=G, slt8=slt8):
                return s.dma_start(
                    out=slots_out[1024 * G:1024 * (G + 1), :]
                    .rearrange("(t p) c -> p t c", p=128),
                    in_=slt8.ap().rearrange("p (t c) -> p t c", c=4))
            e_ds = ch.dma("sync", d_slt, after=[e15])

            escs = []
            for t in range(8):
                def g_scat(g, G=G, t=t, idx8=idx8):
                    T = 8 * G + t
                    return g.indirect_dma_start(
                        invd[:], IndirectOffsetOnAxis(idx8[:, t:t + 1], 0),
                        sbTok[:, 4 * T:4 * (T + 1)], None,
                        bounds_check=_breg(g, ROWS - 1), oob_is_err=False)
                escs.append(ch.gdma(g_scat, after=[e14, ev_sent, ev_tok]))
            ev_scat.extend(escs)
            prev_grp[par] = [escs[-1], e_ds]
            if G == 0:
                pass
        e_slot_done = Ev("g", ch.counts["g"])
        def d_inv2(s):
            return s.dma_start(
                out=sbInv.ap().rearrange("p (j c) -> p j c", c=4),
                in_=invd.ap()[0:ROWS, :].rearrange("(j p) c -> p j c", p=128))

        e_inv = ch.dma("sync", d_inv2, after=ev_scat)
        ev_wfree = [None, None]
        ev_tprev = [None, None]
        evw = []
        ev_bfree = [None, None]
        ev_sfree = [None, None]
        ev_gu_done = None
        ev_y_done = None
        prev_ln = None
        for e in range(EPC):
            pe = e % 2
            for mi, (w_in, sbW) in enumerate(((wg_in, sbWg[pe]),
                                              (wu_in, sbWu[pe]),
                                              (wd_in, sbWd[pe]))):
                for cc in range(8):
                    k = mi * 8 + cc
                    wfpar = k % 2
                    aft = [ev_wfree[pe]] if ev_wfree[pe] else None
                    if evw[-2:] and len(evw) >= 2:
                        aft = (aft or []) + [evw[-2][1]]
                    def d_w(sy, w_in=w_in, e=e, cc=cc, wfpar=wfpar):
                        return sy.dma_start(
                            out=sbWF[wfpar][:],
                            in_=w_in[D * e + 128 * cc:D * e + 128 * (cc + 1), :])
                    e_d = ch.dma("sync", d_w, after=aft)
                    def v_w(v, sbW=sbW, cc=cc, wfpar=wfpar):
                        return v.tensor_copy(sbW[:, D * cc:D * (cc + 1)],
                                             sbWF[wfpar][:])
                    e_c = ch.add("vector", v_w, after=[e_d])
                    evw.append((e_d, e_c))
            e_wdone = evw[-1][1]

            evtr = []
            for j5 in range(5):
                gpar = j5 % 2
                aft = [e_inv]
                if ev_bfree[pe]:
                    aft.append(ev_bfree[pe])
                if prev_ln:
                    aft.append(prev_ln)
                def v_z(v, gpar=gpar):
                    return v.memset(sbGx[gpar][:], 0.0)
                e_z = ch.add("vector", v_z, after=aft)

                def g_gth(g, e=e, j5=j5, gpar=gpar):
                    col = 4 * (5 * e + j5)
                    return g.indirect_dma_start(
                        sbGx[gpar][:], None, xf[:],
                        IndirectOffsetOnAxis(sbInv[:, col:col + 1], 0),
                        bounds_check=_breg(g, NC * S - 1), oob_is_err=False)
                e_g = ch.gdma(g_gth, after=[e_z])

                e_ln = emit_ln(ch, P, None, src=sbGx[gpar],
                               out=sbXtB[gpar], after=[e_g],
                               guard=[ev_tprev[gpar]] if ev_tprev[gpar] else None)
                prev_ln = e_ln

                def t_tr(t, gpar=gpar):
                    for cc in range(8):
                        inst = t.matmul(PB[:, 1024 + 128 * cc:1024 + 128 * (cc + 1)],
                                        sbXtB[gpar][:, 128 * cc:128 * (cc + 1)],
                                        Csb["c_ident_b"][:], start=True, stop=True)
                    return inst
                aft_t = [e_ln]
                if evtr:
                    aft_t.append(evtr[-1])
                e_t = ch.add("tensor", t_tr, after=aft_t)
                ev_tprev[gpar] = e_t

                def v_tr(v, j5=j5, pe=pe):
                    for cc in range(8):
                        inst = v.tensor_copy(
                            sbBufT[pe][:, CAP * cc + 128 * j5:
                                        CAP * cc + 128 * (j5 + 1)],
                            PB[:, 1024 + 128 * cc:1024 + 128 * (cc + 1)])
                    return inst
                e_tr = ch.add("vector", v_tr, after=[e_t])
                evtr.append(e_tr)
            e_buf = evtr[-1]

            ev_st = []
            prev_gu = None
            for fs in range(8):
                aft = [e_buf, e_wdone]
                if ev_gu_done and fs == 0:
                    aft.append(ev_gu_done)
                if prev_gu:
                    aft.append(prev_gu)
                def t_gu(t, fs=fs, pe=pe):
                    for cc in range(8):
                        t.matmul(PA[:, 0:512],
                                 sbWg[pe][:, D * cc + 128 * fs:
                                           D * cc + 128 * (fs + 1)],
                                 sbBufT[pe][:, CAP * cc:CAP * cc + 512],
                                 start=(cc == 0), stop=(cc == 7))
                        t.matmul(PA[:, 512:640],
                                 sbWg[pe][:, D * cc + 128 * fs:
                                           D * cc + 128 * (fs + 1)],
                                 sbBufT[pe][:, CAP * cc + 512:CAP * cc + 640],
                                 start=(cc == 0), stop=(cc == 7))
                    for cc in range(8):
                        t.matmul(PA[:, 1024:1536],
                                 sbWu[pe][:, D * cc + 128 * fs:
                                           D * cc + 128 * (fs + 1)],
                                 sbBufT[pe][:, CAP * cc:CAP * cc + 512],
                                 start=(cc == 0), stop=(cc == 7))
                        inst = t.matmul(
                            PA[:, 1536:1664],
                            sbWu[pe][:, D * cc + 128 * fs:
                                      D * cc + 128 * (fs + 1)],
                            sbBufT[pe][:, CAP * cc + 512:CAP * cc + 640],
                            start=(cc == 0), stop=(cc == 7))
                    return inst
                e_gu = ch.add("tensor", t_gu, after=aft)

                def s_e1(sc):
                    return sc.activation(sbSg[:], PA[:, 0:640], AF.Exp,
                                         scale=-1.0)
                e_a1 = ch.add("scalar", s_e1, after=[e_gu])

                def s_e2(sc):
                    return sc.activation(sbSg2[:], sbSg[:], AF.Ln,
                                         bias=P.one[:])
                e_a2 = ch.add("scalar", s_e2, after=[e_a1])

                def s_e3(sc):
                    return sc.activation(sbSg[:], sbSg2[:], AF.Exp, scale=-1.0)
                e_a3 = ch.add("scalar", s_e3, after=[e_a2])

                def v_sig(v):
                    return v.tensor_mul(sbSg[:], sbSg[:], PA[:, 0:640])
                e_a4 = ch.add("vector", v_sig, after=[e_a3])

                def v_st(v, fs=fs, pe=pe):
                    return v.tensor_tensor(sbST[pe][:, CAP * fs:CAP * (fs + 1)],
                                           sbSg[:], PA[:, 1024:1664], OP.mult)
                aftst = [e_a4]
                if ev_sfree[pe] and fs == 0:
                    aftst.append(ev_sfree[pe])
                e_st = ch.add("vector", v_st, after=aftst)
                ev_st.append(e_st)
                prev_gu = e_st
            ev_gu_done = ev_st[-1]
            ev_bfree[pe] = ev_gu_done
            ev_wfree[pe] = ev_gu_done

            prev_y = None
            ev_ycs = []
            for cs in range(5):
                aft = [ev_st[-1]]
                if ev_y_done and cs == 0:
                    aft.append(ev_y_done)
                if prev_y:
                    aft.append(prev_y)
                def t_y(t, cs=cs, pe=pe):
                    for fs in range(8):
                        t.matmul(PB[:, 0:512],
                                 sbST[pe][:, CAP * fs + 128 * cs:
                                           CAP * fs + 128 * (cs + 1)],
                                 sbWd[pe][:, D * fs:D * fs + 512],
                                 start=(fs == 0), stop=(fs == 7))
                        inst = t.matmul(
                            PB[:, 512:1024],
                            sbST[pe][:, CAP * fs + 128 * cs:
                                      CAP * fs + 128 * (cs + 1)],
                            sbWd[pe][:, D * fs + 512:D * fs + 1024],
                            start=(fs == 0), stop=(fs == 7))
                    return inst
                e_y = ch.add("tensor", t_y, after=aft)

                def s_yev(sc, cs=cs):
                    return sc.activation(sbYout[cs % 2][:], PB[:, 0:1024],
                                         AF.Copy)
                e_ev = ch.add("scalar", s_yev, after=[e_y] +
                              ([ev_ycs[-2][1]] if len(ev_ycs) >= 2 else []))

                def d_y(sy, e=e, cs=cs):
                    return sy.dma_start(
                        out=y_out[CAP * e + 128 * cs:CAP * e + 128 * (cs + 1), :],
                        in_=sbYout[cs % 2][:])
                e_dy = ch.dma("sync", d_y, after=[e_ev])
                ev_ycs.append((e_y, e_dy))
                prev_y = e_ev
            ev_y_done = ev_ycs[-1][0]
            ev_sfree[pe] = ev_y_done
            ev_wfree[pe] = ev_y_done

        totals_c = ch.emit(block, SEMS)

        def fin(s):
            for k, vv in totals_c.items():
                if vv > 0:
                    s.wait_ge(SEMS[k], vv)
        block.sync(fin)

    return nc, consts


def l2_inputs(consts, core, xfull, pertok_full, totals_full, w_gate, w_up, w_down):
    use = ("c_ident_b", "c_lstr_b", "c_iota32", "c_iota128", "c_tokid")
    d = {k: consts[k] for k in use}
    d["xf"] = xfull
    d["ptf"] = pertok_full
    d["totf"] = totals_full
    d["wg"] = np.ascontiguousarray(w_gate[EPC * core:EPC * (core + 1)]).reshape(EPC * D, D)
    d["wu"] = np.ascontiguousarray(w_up[EPC * core:EPC * (core + 1)]).reshape(EPC * D, D)
    d["wd"] = np.ascontiguousarray(w_down[EPC * core:EPC * (core + 1)]).reshape(EPC * D, D)
    d["exb"] = np.full((128, 1), float(EPC * core * CAP), np.float32)
    return d


def build_L3():
    from contextlib import ExitStack
    consts = _host_consts()
    nc = bass.Bass("TRN2", target_bir_lowering=False, debug=False,
                   detect_race_conditions=False)
    C = {}
    for k, v in consts.items():
        dt = BF16 if v.dtype == ml_dtypes.bfloat16 else F32
        C[k] = nc.dram_tensor(k, list(v.shape), dt, kind="ExternalInput")
    x_own = nc.dram_tensor("x_own", [S, D], F32, kind="ExternalInput")
    y_full = nc.dram_tensor("y_full", [E * CAP, D], BF16, kind="ExternalInput")
    slots_own = nc.dram_tensor("slots_own", [S, 4], F32, kind="ExternalInput")
    pertok_own = nc.dram_tensor("pertok_own", [S, 4], F32, kind="ExternalInput")
    r_in = nc.dram_tensor("r_in", [D, E], F32, kind="ExternalInput")
    wqt = nc.dram_tensor("wqt", [D, D], F32, kind="ExternalInput")
    wkt = nc.dram_tensor("wkt", [D, D], F32, kind="ExternalInput")
    wvt = nc.dram_tensor("wvt", [D, D], F32, kind="ExternalInput")
    wot = nc.dram_tensor("wot", [D, D], F32, kind="ExternalInput")
    bq_in = nc.dram_tensor("bq_in", [8, 128], F32, kind="ExternalInput")
    bk_in = nc.dram_tensor("bk_in", [8, 128], F32, kind="ExternalInput")
    bv_rep = nc.dram_tensor("bv_rep", [128, D], F32, kind="ExternalInput")
    bo_rep = nc.dram_tensor("bo_rep", [128, D], F32, kind="ExternalInput")
    x2_out = nc.dram_tensor("x2_out", [S, D], F32, kind="ExternalOutput")
    pertok2 = nc.dram_tensor("pertok2", [S, 4], F32, kind="ExternalOutput")
    totals2 = nc.dram_tensor("totals2", [1, NT * E], F32, kind="ExternalOutput")
    x1_d = nc.dram_tensor("x1_d", [S, D], F32)

    ch = Ch()
    with ExitStack() as st:
        P = Pool(nc, st)
        def sb(name, shape, dtype=F32):
            return st.enter_context(nc.sbuf_tensor(name, shape, dtype))
        Csb = {}
        for k, v in consts.items():
            dt = BF16 if v.dtype == ml_dtypes.bfloat16 else F32
            Csb[k] = sb("s" + k, list(v.shape), dt)
        sbR = sb("sbR", [128, 256])
        sbXT = sb("sbXT", [128, NT * 1024], BF16)
        sbVt = sb("sbVt", [128, NT * 1040], BF16)
        sbA = sb("sbA", [128, NT * 1024], BF16)
        sbATok = sb("sbATok", [128, NT * 1024], BF16)
        sbW = sb("sbW", [128, 8 * D], BF16)
        sbWqk = sb("sbWqk", [128, D], BF16)
        sbWF = sb("sbWF", [128, 128])
        sbQ = sb("sbQ", [128, S], BF16)
        sbK = sb("sbK", [128, S], BF16)
        sbKp = sb("sbKp", [128, S], BF16)
        sbQp = sb("sbQp", [128, S], BF16)
        sbSc1 = sb("sbSc1", [128, S], BF16)
        sbPO = sb("sbPO", [65, 1024], BF16)
        sbBq = sb("sbBq", [128, 8])
        sbBk = sb("sbBk", [128, 8])
        sbBv = sb("sbBv", [128, D], BF16)
        sbBo = sb("sbBo", [128, D], BF16)
        sbYg = sb("sbYg", [128, D], BF16)
        sbSltT = sb("sbSltT", [128, 4])
        sbPkT = sb("sbPkT", [128, 4])
        sbIdxI = sb("sbIdxI", [128, 1], I32)
        sbX1 = P.xtT
        sbX2f = P.sq
        PA = st.enter_context(nc.psum_tensor("PA", [128, 2048], F32))
        PB = st.enter_context(nc.psum_tensor("PB", [128, 2048], F32))
        SEMS = {k: st.enter_context(nc.semaphore("sem_" + k))
                for k in list(Ch.ENGINES) + ["g"]}
        block = st.enter_context(nc.Block())

        nload = len(consts) + 1 + 16

        def load_consts(s, sem):
            for k in consts:
                s.dma_start(out=Csb[k][:], in_=C[k][:]).then_inc(sem, 16)
            s.dma_start(out=sbR.ap().rearrange("p (c e) -> p c e", c=8),
                        in_=r_in.ap().rearrange("(c p) e -> p c e", p=128)
                        ).then_inc(sem, 16)
            for j in range(8):
                s.dma_start(out=sbBq[:, j:j + 1],
                            in_=bq_in[j:j + 1, :].rearrange("a p -> p a")
                            ).then_inc(sem, 16)
                s.dma_start(out=sbBk[:, j:j + 1],
                            in_=bk_in[j:j + 1, :].rearrange("a p -> p a")
                            ).then_inc(sem, 16)
        ch.dmas("sync", load_consts, nload)

        def d_bv(s):
            return s.dma_start(out=P.x[:], in_=bv_rep[:])
        ch.dma("sync", d_bv)

        def v_bv(v):
            return v.tensor_copy(sbBv[:], P.x[:])
        ch.add("vector", v_bv)

        def d_bo(s):
            return s.dma_start(out=P.x[:], in_=bo_rep[:])
        ch.dma("sync", d_bo)

        def v_bo(v):
            return v.tensor_copy(sbBo[:], P.x[:])
        ch.add("vector", v_bo)

        def v_init(v):
            v.memset(P.eps[:], LN_EPS)
            v.memset(sbYg[:], 0.0)
            v.memset(sbKp[64:128, :], 0.0)
            v.memset(sbQp[64:128, :], 0.0)
            for tt in range(NT):
                inst = v.memset(
                    sbVt.ap()[:, 1040 * tt:1040 * (tt + 1)]
                    .rearrange("p (h c) -> p h c", c=65)[:, :, 64:65], 1.0)
            return inst
        ch.add("vector", v_init)

        for i in range(NT):
            def d_in(s, sem, i=i):
                s.dma_start(out=P.x[:], in_=x_own[128 * i:128 * (i + 1), :]
                            ).then_inc(sem, 16)
                s.dma_start(out=sbSltT[:], in_=slots_own[128 * i:128 * (i + 1), :]
                            ).then_inc(sem, 16)
                s.dma_start(out=sbPkT[:], in_=pertok_own[128 * i:128 * (i + 1), :]
                            ).then_inc(sem, 16)
            ch.dmas("sync", d_in, 3)

            def v_idx(v):
                return v.tensor_copy(sbIdxI[:], sbSltT[:, 0:1])
            ch.add("vector", v_idx)

            def g_y(g):
                return g.indirect_dma_start(
                    sbYg[:], None, y_full[:],
                    IndirectOffsetOnAxis(sbIdxI[:], 0),
                    bounds_check=_breg(g, E * CAP - 1), oob_is_err=False)
            ch.gdma(g_y)

            emit_ln(ch, P, None)

            def v_x1a(v):
                v.tensor_scalar(P.t1[:], sbPkT[:, 1:2], sbSltT[:, 1:2], 0.5,
                                OP.mult, OP.mult)
                return v.scalar_tensor_tensor(sbX1[:], P.xt[:], 0.5, P.x[:],
                                              OP.mult, OP.add)
            ch.add("vector", v_x1a)

            def v_x1b(v):
                return v.scalar_tensor_tensor(sbX2f[:], sbYg[:], P.t1[:],
                                              sbX1[:], OP.mult, OP.add)
            ch.add("vector", v_x1b)

            def d_x1(s, i=i):
                return s.dma_start(out=x1_d[128 * i:128 * (i + 1), :],
                                   in_=sbX2f[:])
            ch.dma("sync", d_x1)

        for i in range(NT):
            def d_x(s, i=i):
                return s.dma_start(out=P.x[:], in_=x1_d[128 * i:128 * (i + 1), :])
            ch.dma("sync", d_x)
            emit_ln(ch, P, None)

            def v_cb(v):
                return v.tensor_copy(sbYg[:], P.xt[:])
            ch.add("vector", v_cb)

            def t_tr(t):
                for cc in range(8):
                    inst = t.matmul(PA[:, 128 * cc:128 * (cc + 1)],
                                    sbYg[:, 128 * cc:128 * (cc + 1)],
                                    Csb["c_ident_b"][:], start=True, stop=True)
                return inst
            ch.add("tensor", t_tr)

            def v_tr(v, i=i):
                for cc in range(8):
                    inst = v.tensor_copy(sbXT[:, 2048 * cc + 128 * i:
                                              2048 * cc + 128 * (i + 1)],
                                         PA[:, 128 * cc:128 * (cc + 1)])
                return inst
            ch.add("vector", v_tr)

        for cc in range(8):
            def d_wv(s, cc=cc):
                return s.dma_start(out=P.x[:],
                                   in_=wvt[128 * cc:128 * (cc + 1), :])
            ch.dma("sync", d_wv)

            def v_wv(v, cc=cc):
                return v.tensor_copy(sbW[:, D * cc:D * (cc + 1)], P.x[:])
            ch.add("vector", v_wv)

        for tt in range(NT):
            for oh in range(2):
                def t_v(t, tt=tt, oh=oh):
                    for cc in range(8):
                        inst = t.matmul(PB[:, 0:512],
                                        sbXT[:, 2048 * cc + 128 * tt:
                                             2048 * cc + 128 * (tt + 1)],
                                        sbW[:, D * cc + 512 * oh:
                                            D * cc + 512 * (oh + 1)],
                                        start=(cc == 0), stop=(cc == 7))
                    return inst
                ch.add("tensor", t_v)

                def v_v1(v, oh=oh):
                    return v.tensor_add(sbX1[:, 0:512], PB[:, 0:512],
                                        sbBv[:, 512 * oh:512 * (oh + 1)])
                ch.add("vector", v_v1)

                def v_v2(v, tt=tt, oh=oh):
                    base = 1040 * tt
                    dst = sbVt.ap()[:, base:base + 1040].rearrange(
                        "p (h c) -> p h c", c=65)[:, 8 * oh:8 * (oh + 1), 0:64]
                    src = sbX1.ap()[:, 0:512].rearrange("p (h c) -> p h c", c=64)
                    return v.tensor_copy(dst, src)
                ch.add("vector", v_v2)

        for j in range(8):
            for (w_in, sbB, dstQ) in ((wqt, sbBq, sbQ), (wkt, sbBk, sbK)):
                for cc in range(8):
                    def d_w(s, w_in=w_in, cc=cc, j=j):
                        return s.dma_start(
                            out=sbWF[:],
                            in_=w_in[128 * cc:128 * (cc + 1),
                                     128 * j:128 * (j + 1)])
                    ch.dma("sync", d_w)

                    def v_w(v, cc=cc):
                        return v.tensor_copy(sbWqk[:, 128 * cc:128 * (cc + 1)],
                                             sbWF[:])
                    ch.add("vector", v_w)

                def t_p(t):
                    for ts4 in range(4):
                        for cc in range(8):
                            inst = t.matmul(PA[:, 512 * ts4:512 * (ts4 + 1)],
                                            sbWqk[:, 128 * cc:128 * (cc + 1)],
                                            sbXT[:, 2048 * cc + 512 * ts4:
                                                 2048 * cc + 512 * (ts4 + 1)],
                                            start=(cc == 0), stop=(cc == 7))
                    return inst
                ch.add("tensor", t_p)

                def v_bias(v, sbB=sbB, dstQ=dstQ, j=j):
                    return v.tensor_scalar_add(dstQ[:], PA[:, 0:2048],
                                               sbB[:, j:j + 1])
                ch.add("vector", v_bias)

                def t_rope(t, dstQ=dstQ):
                    for ts4 in range(4):
                        inst = t.matmul(PB[:, 512 * ts4:512 * (ts4 + 1)],
                                        Csb["c_psign"][:],
                                        dstQ[:, 512 * ts4:512 * (ts4 + 1)],
                                        start=True, stop=True)
                    return inst
                ch.add("tensor", t_rope)

                def v_rope1(v, dstQ=dstQ):
                    return v.tensor_mul(sbSc1[:], dstQ[:], Csb["c_cos"][:])
                ch.add("vector", v_rope1)

                def v_rope2(v, dstQ=dstQ):
                    return v.tensor_mul(dstQ[:], PB[:, 0:2048], Csb["c_sin"][:])
                ch.add("vector", v_rope2)

                def v_rope3(v, dstQ=dstQ):
                    return v.tensor_add(dstQ[:], dstQ[:], sbSc1[:])
                ch.add("vector", v_rope3)

            for h2 in range(2):
                h = 2 * j + h2
                hb = 64 * h2

                def v_kp(v, hb=hb):
                    return v.tensor_copy(sbKp[0:64, :], sbK[hb:hb + 64, :])
                ch.add("vector", v_kp)

                if h2 == 1:
                    def v_qp(v):
                        return v.tensor_copy(sbQp[0:64, :], sbQ[64:128, :])
                    ch.add("vector", v_qp)
                qsrc = sbQ if h2 == 0 else sbQp

                for qh in range(2):
                    for kt in range(NT):
                        def t_qk(t, kt=kt, qh=qh, qsrc=qsrc):
                            t.matmul(PA[:, 0:512], sbKp[:, 128 * kt:128 * (kt + 1)],
                                     qsrc[:, 1024 * qh:1024 * qh + 512],
                                     start=True, stop=True)
                            return t.matmul(PA[:, 512:1024],
                                            sbKp[:, 128 * kt:128 * (kt + 1)],
                                            qsrc[:, 1024 * qh + 512:1024 * (qh + 1)],
                                            start=True, stop=True)
                        ch.add("tensor", t_qk)

                        def s_exp(sc, kt=kt):
                            return sc.activation(sbA[:, 1024 * kt:1024 * (kt + 1)],
                                                 PA[:, 0:1024], AF.Exp,
                                                 scale=ATT_SCALE)
                        ch.add("scalar", s_exp)

                    def t_pv(t, h=h):
                        for kt in range(NT):
                            t.matmul(PB[0:65, 0:512],
                                     sbVt[:, 1040 * kt + 65 * h:
                                          1040 * kt + 65 * (h + 1)],
                                     sbA[:, 1024 * kt:1024 * kt + 512],
                                     start=(kt == 0), stop=(kt == NT - 1))
                        for kt in range(NT):
                            inst = t.matmul(PB[0:65, 512:1024],
                                            sbVt[:, 1040 * kt + 65 * h:
                                                 1040 * kt + 65 * (h + 1)],
                                            sbA[:, 1024 * kt + 512:1024 * (kt + 1)],
                                            start=(kt == 0), stop=(kt == NT - 1))
                        return inst
                    ch.add("tensor", t_pv)

                    def s_po(sc):
                        return sc.activation(sbPO[:], PB[0:65, 0:1024], AF.Copy)
                    ch.add("scalar", s_po)

                    for s8 in range(8):
                        def t_tr2(t, s8=s8):
                            return t.matmul(PA[:, 1536:1601],
                                            sbPO[:, 128 * s8:128 * (s8 + 1)],
                                            Csb["c_ident_b"][0:65, 0:65],
                                            start=True, stop=True)
                        ch.add("tensor", t_tr2)

                        def v_nrm1(v):
                            return v.reciprocal(P.t1[:], PA[:, 1600:1601])
                        ch.add("vector", v_nrm1)

                        def v_nrm2(v, s8=s8, qh=qh, h=h):
                            tt = 8 * qh + s8
                            return v.tensor_scalar_mul(
                                sbATok[:, 1024 * tt + 64 * h:
                                       1024 * tt + 64 * (h + 1)],
                                PA[:, 1536:1600], P.t1[:])
                        ch.add("vector", v_nrm2)

        for tt in range(NT):
            def t_tr3(t, tt=tt):
                for cc in range(8):
                    inst = t.matmul(PA[:, 128 * cc:128 * (cc + 1)],
                                    sbATok[:, 1024 * tt + 128 * cc:
                                           1024 * tt + 128 * (cc + 1)],
                                    Csb["c_ident_b"][:], start=True, stop=True)
                return inst
            ch.add("tensor", t_tr3)

            def v_tr3(v, tt=tt):
                for cc in range(8):
                    inst = v.tensor_copy(sbXT[:, 2048 * cc + 128 * tt:
                                              2048 * cc + 128 * (tt + 1)],
                                         PA[:, 128 * cc:128 * (cc + 1)])
                return inst
            ch.add("vector", v_tr3)

        for cc in range(8):
            def d_wo(s, cc=cc):
                return s.dma_start(out=P.x[:],
                                   in_=wot[128 * cc:128 * (cc + 1), :])
            ch.dma("sync", d_wo)

            def v_wo(v, cc=cc):
                return v.tensor_copy(sbW[:, D * cc:D * (cc + 1)], P.x[:])
            ch.add("vector", v_wo)

        for tt in range(NT):
            def t_o(t, tt=tt):
                for oh in range(2):
                    for cc in range(8):
                        inst = t.matmul(PB[:, 512 * oh:512 * (oh + 1)],
                                        sbXT[:, 2048 * cc + 128 * tt:
                                             2048 * cc + 128 * (tt + 1)],
                                        sbW[:, D * cc + 512 * oh:
                                            D * cc + 512 * (oh + 1)],
                                        start=(cc == 0), stop=(cc == 7))
                return inst
            ch.add("tensor", t_o)

            def d_x1b(s, tt=tt):
                return s.dma_start(out=P.x[:], in_=x1_d[128 * tt:128 * (tt + 1), :])
            ch.dma("sync", d_x1b)

            def v_x2a(v):
                return v.tensor_add(sbX2f[:], PB[:, 0:1024], sbBo[:])
            ch.add("vector", v_x2a)

            def v_x2b(v):
                return v.tensor_add(sbX2f[:], sbX2f[:], P.x[:])
            ch.add("vector", v_x2b)

            def d_x2(s, tt=tt):
                return s.dma_start(out=x2_out[128 * tt:128 * (tt + 1), :],
                                   in_=sbX2f[:])
            ch.dma("sync", d_x2)

        build_router_program(nc, Csb, x2_out, pertok2, totals2, ch, st, P,
                             sbR, PA, PB)

        totals_c = ch.emit(block, SEMS)

        def fin(s):
            for k, vv in totals_c.items():
                if vv > 0:
                    s.wait_ge(SEMS[k], vv)
        block.sync(fin)

    return nc, consts


def l3_inputs(consts, inp, core, x_own, y_full, slots_own, pertok_own):
    d = {k: v for k, v in consts.items()}
    d["x_own"] = x_own
    d["y_full"] = y_full
    d["slots_own"] = slots_own
    d["pertok_own"] = pertok_own
    d["r_in"] = np.ascontiguousarray(inp["L0_ffn_w_router"])
    d["wqt"] = np.ascontiguousarray(np.asarray(inp["L0_attn_Wq"]).T)
    d["wkt"] = np.ascontiguousarray(np.asarray(inp["L0_attn_Wk"]).T)
    d["wvt"] = np.ascontiguousarray(np.asarray(inp["L0_attn_Wv"]).T)
    d["wot"] = np.ascontiguousarray(np.asarray(inp["L0_attn_Wo"]).T)
    d["bq_in"] = np.ascontiguousarray(np.asarray(inp["L0_attn_bq"]).reshape(8, 128))
    d["bk_in"] = np.ascontiguousarray(np.asarray(inp["L0_attn_bk"]).reshape(8, 128))
    d["bv_rep"] = np.ascontiguousarray(
        np.broadcast_to(np.asarray(inp["L0_attn_bv"]), (128, D)))
    d["bo_rep"] = np.ascontiguousarray(
        np.broadcast_to(np.asarray(inp["L0_attn_bo"]), (128, D)))
    return d


def build_L5():
    from contextlib import ExitStack
    consts = _host_consts()
    nc = bass.Bass("TRN2", target_bir_lowering=False, debug=False,
                   detect_race_conditions=False)
    x_own = nc.dram_tensor("x_own", [S, D], F32, kind="ExternalInput")
    y_full = nc.dram_tensor("y_full", [E * CAP, D], BF16, kind="ExternalInput")
    slots_own = nc.dram_tensor("slots_own", [S, 4], F32, kind="ExternalInput")
    pertok_own = nc.dram_tensor("pertok_own", [S, 4], F32, kind="ExternalInput")
    x3_out = nc.dram_tensor("x3_out", [S, D], F32, kind="ExternalOutput")

    ch = Ch()
    with ExitStack() as st:
        P = Pool(nc, st)
        def sb(name, shape, dtype=F32):
            return st.enter_context(nc.sbuf_tensor(name, shape, dtype))
        sbYg = sb("sbYg", [128, D], BF16)
        sbSltT = sb("sbSltT", [128, 4])
        sbPkT = sb("sbPkT", [128, 4])
        sbIdxI = sb("sbIdxI", [128, 1], I32)
        sbX3 = sb("sbX3", [128, D])
        SEMS = {k: st.enter_context(nc.semaphore("sem_" + k))
                for k in list(Ch.ENGINES) + ["g"]}
        block = st.enter_context(nc.Block())

        def v0(v):
            v.memset(sbYg[:], 0.0)
            return v.memset(P.eps[:], LN_EPS)
        ch.add("vector", v0)

        for i in range(NT):
            def d_in(s, sem, i=i):
                s.dma_start(out=P.x[:], in_=x_own[128 * i:128 * (i + 1), :]
                            ).then_inc(sem, 16)
                s.dma_start(out=sbSltT[:], in_=slots_own[128 * i:128 * (i + 1), :]
                            ).then_inc(sem, 16)
                s.dma_start(out=sbPkT[:], in_=pertok_own[128 * i:128 * (i + 1), :]
                            ).then_inc(sem, 16)
            ch.dmas("sync", d_in, 3)

            def v_idx(v):
                return v.tensor_copy(sbIdxI[:], sbSltT[:, 0:1])
            ch.add("vector", v_idx)

            def g_y(g):
                return g.indirect_dma_start(
                    sbYg[:], None, y_full[:],
                    IndirectOffsetOnAxis(sbIdxI[:], 0),
                    bounds_check=_breg(g, E * CAP - 1), oob_is_err=False)
            ch.gdma(g_y)

            emit_ln(ch, P, None)

            def v_x3a(v):
                v.tensor_scalar(P.t1[:], sbPkT[:, 1:2], sbSltT[:, 1:2], 0.5,
                                OP.mult, OP.mult)
                return v.scalar_tensor_tensor(sbX3[:], P.xt[:], 0.5, P.x[:],
                                              OP.mult, OP.add)
            ch.add("vector", v_x3a)

            def v_x3b(v):
                return v.scalar_tensor_tensor(P.sq[:], sbYg[:], P.t1[:],
                                              sbX3[:], OP.mult, OP.add)
            ch.add("vector", v_x3b)

            def d_x3(s, i=i):
                return s.dma_start(out=x3_out[128 * i:128 * (i + 1), :],
                                   in_=P.sq[:])
            ch.dma("sync", d_x3)

        totals_c = ch.emit(block, SEMS)

        def fin(s):
            for k, vv in totals_c.items():
                if vv > 0:
                    s.wait_ge(SEMS[k], vv)
        block.sync(fin)

    return nc, consts


_PROGRAMS = {}


def _get_program(name, builder):
    if name not in _PROGRAMS:
        _PROGRAMS[name] = builder()
    return _PROGRAMS[name]


def kernel(**inputs):
    inputs = {k: np.asarray(v) for k, v in inputs.items()}
    x0 = np.ascontiguousarray(inputs["L0_attn_x"].reshape(NC * S, D)
                              .astype(np.float32, copy=False))
    wr = np.ascontiguousarray(inputs["L0_ffn_w_router"].astype(np.float32, copy=False))
    wg = np.asarray(inputs["L0_ffn_w_gate"], np.float32)
    wu = np.asarray(inputs["L0_ffn_w_up"], np.float32)
    wd = np.asarray(inputs["L0_ffn_w_down"], np.float32)
    cores = list(range(NC))

    nc1, consts = _get_program("L1", build_L1)
    res1 = run_bass_kernel_spmd(
        nc1, [l1_inputs(consts, x0[c * S:(c + 1) * S], wr) for c in cores], cores)
    pertok1 = np.concatenate([np.asarray(r["pertok"]) for r in res1.results], 0)
    totals1 = np.concatenate(
        [np.asarray(r["totals"]).reshape(NT, E) for r in res1.results], 0)

    nc2, consts2 = _get_program("L2", build_L2)
    res2 = run_bass_kernel_spmd(
        nc2, [l2_inputs(consts2, c, x0, pertok1, totals1, wg, wu, wd)
              for c in cores], cores)
    y1 = np.concatenate([np.asarray(r["y_out"]) for r in res2.results], 0)
    slots1 = np.asarray(res2.results[0]["slots_out"])

    nc3, consts3 = _get_program("L3", build_L3)
    res3 = run_bass_kernel_spmd(
        nc3, [l3_inputs(consts3, inputs, c, x0[c * S:(c + 1) * S], y1,
                        slots1[c * S:(c + 1) * S], pertok1[c * S:(c + 1) * S])
              for c in cores], cores)
    x2 = np.concatenate([np.asarray(r["x2_out"]) for r in res3.results], 0)
    pertok2 = np.concatenate([np.asarray(r["pertok2"]) for r in res3.results], 0)
    totals2 = np.concatenate(
        [np.asarray(r["totals2"]).reshape(NT, E) for r in res3.results], 0)

    res4 = run_bass_kernel_spmd(
        nc2, [l2_inputs(consts2, c, x2, pertok2, totals2, wg, wu, wd)
              for c in cores], cores)
    y2 = np.concatenate([np.asarray(r["y_out"]) for r in res4.results], 0)
    slots2 = np.asarray(res4.results[0]["slots_out"])

    nc5, _ = _get_program("L5", build_L5)
    res5 = run_bass_kernel_spmd(
        nc5, [{"x_own": x2[c * S:(c + 1) * S], "y_full": y2,
               "slots_own": slots2[c * S:(c + 1) * S],
               "pertok_own": pertok2[c * S:(c + 1) * S]} for c in cores], cores)
    x3 = np.concatenate([np.asarray(r["x3_out"]) for r in res5.results], 0)
    return x3.reshape(NC, S, D)


if __name__ == "__main__":
    pass


# version 05
# speedup vs baseline: 1.7103x; 1.1468x over previous
import math
import numpy as np
import ml_dtypes
import concourse.bass as bass
import concourse.mybir as mybir
from concourse.bass import IndirectOffsetOnAxis
from concourse.bass_utils import run_bass_kernel_spmd

F32 = mybir.dt.float32
BF16 = mybir.dt.bfloat16
I32 = mybir.dt.int32
U32 = mybir.dt.uint32
AX = mybir.AxisListType.X
OP = mybir.AluOpType
AF = mybir.ActivationFunctionType

NC = 8
D = 1024
S = 2048
NT = S // 128
E = 32
CAP = 640
EPC = E // NC
ROWS = EPC * CAP
NH = 16
HD = 64
NTG = NC * NT
BIG = 1.0e7
LN_EPS = 1e-5
ATT_SCALE = 1.0 / math.sqrt(HD)


class Ev:
    __slots__ = ("eng", "val")

    def __init__(self, eng, val):
        self.eng = eng
        self.val = val


class Ch:

    ENGINES = ("sync", "vector", "scalar", "tensor", "gpsimd")

    def __init__(self):
        self.steps = []
        self.counts = {e: 0 for e in self.ENGINES}
        self.counts["g"] = 0

    def _push(self, eng, fn, inc, sem_stream, after):
        if getattr(self, "force_serial", False):
            after = None
        if after is None:
            after = [Ev(e, self.counts[e]) for e in self.counts
                     if self.counts[e] > 0]
        self.steps.append(dict(eng=eng, fn=fn, inc=inc, sem=sem_stream,
                               after=list(after)))
        self.counts[sem_stream] += inc
        return Ev(sem_stream, self.counts[sem_stream])

    def add(self, eng, fn, inc=1, after=None):
        return self._push(eng, fn, inc, eng, after)

    def dma(self, eng, fn, after=None):
        return self._push(eng, fn, 16, eng, after)

    def dmas(self, eng, fn, n, after=None):
        fn.selfinc = True
        return self._push(eng, fn, 16 * n, eng, after)

    def gdma(self, fn, after=None):
        return self._push("gpsimd", fn, 16, "g", after)

    def emit(self, block, sems):
        per_engine = {}
        for st_ in self.steps:
            per_engine.setdefault(st_["eng"], []).append(st_)

        def mk(eng, items):
            def body(engine):
                waited = {k: 0 for k in self.counts}
                emitted = {k: 0 for k in self.counts}
                for st_ in items:
                    for ev in st_["after"]:
                        need = ev.val
                        if need <= waited[ev.eng]:
                            continue
                        engine.wait_ge(sems[ev.eng], need)
                        waited[ev.eng] = need
                    sem = sems[st_["sem"]]
                    if getattr(st_["fn"], "selfinc", False):
                        st_["fn"](engine, sem)
                    else:
                        inst = st_["fn"](engine)
                        assert inst is not None, "step fn must return last inst"
                        inst.then_inc(sem, st_["inc"])
                    emitted[st_["sem"]] += st_["inc"]
            return body

        for eng, items in per_engine.items():
            block_fn = getattr(block, eng)
            block_fn(mk(eng, items))
        return dict(self.counts)


def _host_consts():
    bf = ml_dtypes.bfloat16
    c = {}
    c["c_ident_f"] = np.eye(128, dtype=np.float32)
    c["c_ident_b"] = np.eye(128, dtype=np.float32).astype(bf)
    c["c_ltri_b"] = np.triu(np.ones((128, 128), np.float32)).astype(bf)
    c["c_lstr_b"] = (np.triu(np.ones((128, 128), np.float32)) -
                     np.eye(128, dtype=np.float32)).astype(bf)
    c["c_iota32"] = np.broadcast_to(np.arange(32, dtype=np.float32), (128, 32)).copy()
    c["c_iota128"] = np.arange(128, dtype=np.float32).reshape(128, 1).copy()
    c["c_ones_b"] = np.ones((128, 1), np.float32).astype(bf)
    inv_freq = 1.0 / 10000.0 ** (np.arange(0, HD, 2, dtype=np.float32) / HD)
    t = np.arange(S, dtype=np.float32)[:, None]
    freqs = t * inv_freq[None, :]
    emb = np.concatenate([freqs, freqs], axis=-1)
    cos = np.cos(emb).T
    sin = np.sin(emb).T
    c["c_cos"] = np.concatenate([cos, cos], axis=0).astype(bf)
    c["c_sin"] = np.concatenate([sin, sin], axis=0).astype(bf)
    P = np.zeros((128, 128), np.float32)
    for blk in (0, 64):
        for m in range(32):
            P[blk + m + 32, blk + m] = -1.0
            P[blk + m, blk + m + 32] = 1.0
    c["c_psign"] = P.astype(bf)
    tok = (np.arange(NTG)[None, :] * 128 + np.arange(128)[:, None]).astype(np.int32)
    c["c_tokid"] = np.repeat(tok, 4, axis=1).copy()
    return c


def _breg(g, val):
    cache = getattr(g, "_breg_cache", None)
    if cache is None:
        cache = {}
        g._breg_cache = cache
    if val not in cache:
        cache[val] = g.to_reg(val)
    return cache[val]


def _declare_consts(nc, names_shapes):
    t = {}
    for name, arr in names_shapes.items():
        dt = {np.dtype(np.float32): F32, np.dtype(ml_dtypes.bfloat16): BF16,
              np.dtype(np.int32): I32}[arr.dtype]
        t[name] = nc.dram_tensor(name, list(arr.shape), dt, kind="ExternalInput")
    return t


class Pool:
    def __init__(self, nc, st):
        def sb(name, shape, dtype=F32):
            return st.enter_context(nc.sbuf_tensor(name, shape, dtype))
        self.sb = sb
        self.x = sb("p_x", [128, D])
        self.sq = sb("p_sq", [128, D])
        self.xt = sb("p_xt", [128, D])
        self.xtT = sb("p_xtT", [128, D])
        self.s1 = sb("p_s1", [128, 1])
        self.s2 = sb("p_s2", [128, 1])
        self.mean = sb("p_mean", [128, 1])
        self.ex2 = sb("p_ex2", [128, 1])
        self.var = sb("p_var", [128, 1])
        self.rstd = sb("p_rstd", [128, 1])
        self.nmr = sb("p_nmr", [128, 1])
        self.lg = sb("p_lg", [128, 32])
        self.mx = sb("p_mx", [128, 1])
        self.nmx = sb("p_nmx", [128, 1])
        self.pe = sb("p_pe", [128, 32])
        self.sum = sb("p_sum", [128, 1])
        self.gate = sb("p_gate", [128, 1])
        self.m8 = sb("p_m8", [128, 8])
        self.ix8 = sb("p_ix8", [128, 8], U32)
        self.est = sb("p_est", [128, 1])
        self.oh = sb("p_oh", [128, 32])
        self.ohb = sb("p_ohb", [128, 32], BF16)
        self.ic = sb("p_ic", [128, 32])
        self.selp = sb("p_selp", [128, 32])
        self.pos = sb("p_pos", [128, 1])
        self.pt = sb("p_pt", [128, 1])
        self.pk = sb("p_pk", [128, 4])
        self.tot = sb("p_tot", [1, 512])
        self.t1 = sb("p_t1", [128, 1])
        self.t2 = sb("p_t2", [128, 1])
        self.t3 = sb("p_t3", [128, 1])
        self.t4 = sb("p_t4", [128, 1])
        self.eps = sb("p_eps", [128, 1])
        self.one = sb("p_one", [128, 1])
        self.rc8 = sb("p_rc8", [128, 8])


def emit_ln(ch, P, src_getter, out=None, src=None, after=None, guard=None):
    if src_getter is not None:
        ev = ch.dma("sync", src_getter)
        after = [ev] if after is None else list(after) + [ev]
    out = out if out is not None else P.xt
    src = src if src is not None else P.x

    def v1(v):
        return v.reduce_sum(P.s1[:], src[:], axis=AX)
    e_v1 = ch.add("vector", v1, after=after)

    def s1(sc):
        return sc.activation(P.sq[:], src[:], AF.Square, accum_out=P.s2[:])
    e_s1 = ch.add("scalar", s1, after=after)

    def v2a(v):
        v.tensor_scalar_mul(P.mean[:], P.s1[:], 1.0 / D)
        return v.tensor_scalar_mul(P.ex2[:], P.s2[:], 1.0 / D)
    e_v2a = ch.add("vector", v2a, after=[e_v1, e_s1])

    def v2b(v):
        return v.tensor_mul(P.t3[:], P.mean[:], P.mean[:])
    e_v2b = ch.add("vector", v2b, after=[e_v2a])

    def v2c(v):
        return v.tensor_sub(P.var[:], P.ex2[:], P.t3[:])
    e_v2c = ch.add("vector", v2c, after=[e_v2b])

    def s2a(sc):
        return sc.activation(P.t4[:], P.var[:], AF.Ln, bias=P.eps[:])
    e_s2a = ch.add("scalar", s2a, after=[e_v2c])

    def s2b(sc):
        return sc.activation(P.rstd[:], P.t4[:], AF.Exp, scale=-0.5)
    e_s2b = ch.add("scalar", s2b, after=[e_s2a])

    def v3(v):
        return v.tensor_scalar(out[:], src[:], P.mean[:], P.rstd[:],
                               OP.subtract, OP.mult)
    return ch.add("vector", v3,
                  after=[e_s2b, e_v2a] + (list(guard) if guard else []))


def emit_router_tile(ch, nc, P, C, sbR, PA, PB, ti, pertok_out, psum_tot_col):
    def t_tr(t):
        for cc in range(8):
            inst = t.matmul(PA[:, 128 * cc:128 * cc + 128],
                            P.xt[:, 128 * cc:128 * cc + 128], C["c_ident_f"][:],
                            start=True, stop=True)
        return inst
    ch.add("tensor", t_tr)

    def v_ev(v):
        return v.tensor_copy(P.xtT[:], PA[:, 0:D])
    ch.add("vector", v_ev)

    def t_lg(t):
        for cc in range(8):
            inst = t.matmul(PB[:, 0:32], P.xtT[:, 128 * cc:128 * cc + 128],
                            sbR[:, 32 * cc:32 * cc + 32],
                            start=(cc == 0), stop=(cc == 7))
        return inst
    ch.add("tensor", t_lg)

    def v_sm1(v):
        return v.tensor_copy(P.lg[:], PB[:, 0:32])
    ch.add("vector", v_sm1)

    def v_sm2(v):
        return v.reduce_max(P.mx[:], P.lg[:], axis=AX)
    ch.add("vector", v_sm2)

    def v_sm3(v):
        return v.tensor_scalar_mul(P.nmx[:], P.mx[:], -1.0)
    ch.add("vector", v_sm3)

    def s_exp(sc):
        return sc.activation(P.pe[:], P.lg[:], AF.Exp, bias=P.nmx[:],
                             scale=1.0, accum_out=P.sum[:])
    ch.add("scalar", s_exp)

    def v_t1(v):
        v.reciprocal(P.gate[:], P.sum[:])
        return v.max(P.m8[:], P.lg[:])
    ch.add("vector", v_t1)

    def v_t2(v):
        return v.max_index(P.ix8[:], P.m8[:], P.lg[:])
    ch.add("vector", v_t2)

    def v_t3(v):
        return v.tensor_copy(P.est[:], P.ix8[:, 0:1])
    ch.add("vector", v_t3)

    def v_t4(v):
        return v.tensor_single_scalar(P.oh[:], C["c_iota32"][:], P.est[:],
                                      OP.is_equal)
    ch.add("vector", v_t4)

    def v_t5(v):
        return v.tensor_copy(P.ohb[:], P.oh[:])
    ch.add("vector", v_t5)

    def t_ic(t):
        t.matmul(PB[:, 512:544], C["c_ltri_b"][:], P.ohb[:], start=True, stop=True)
        return t.matmul(psum_tot_col, C["c_ones_b"][:], P.ohb[:],
                        start=True, stop=True)
    ch.add("tensor", t_ic)

    def v_p1(v):
        return v.tensor_copy(P.ic[:], PB[:, 512:544])
    ch.add("vector", v_p1)

    def v_p2(v):
        return v.tensor_mul(P.selp[:], P.ic[:], P.oh[:])
    ch.add("vector", v_p2)

    def v_p3(v):
        return v.reduce_sum(P.pos[:], P.selp[:], axis=AX)
    ch.add("vector", v_p3)

    def v_p4(v):
        return v.tensor_scalar_add(P.pt[:], P.pos[:], -1.0)
    ch.add("vector", v_p4)

    def v_p5(v):
        v.tensor_copy(P.pk[:, 0:1], P.est[:])
        v.tensor_copy(P.pk[:, 1:2], P.gate[:])
        v.tensor_copy(P.pk[:, 2:3], P.pt[:])
        return v.memset(P.pk[:, 3:4], 0.0)
    ch.add("vector", v_p5)

    def d_out(s):
        return s.dma_start(out=pertok_out[128 * ti:128 * (ti + 1), :], in_=P.pk[:])
    ch.dma("sync", d_out)


def build_router_program(nc, C, x_in, pertok_out, totals_out, ch, st, P,
                         sbR, PA, PB):
    for i in range(NT):
        def load(s, i=i):
            return s.dma_start(out=P.x[:], in_=x_in[128 * i:128 * (i + 1), :])
        emit_ln(ch, P, load)
        emit_router_tile(ch, nc, P, C, sbR, PA, PB, i, pertok_out,
                         PB[0:1, 1024 + 32 * i:1024 + 32 * i + 32])

    def v_tot(v):
        return v.tensor_copy(P.tot[:], PB[0:1, 1024:1536])
    ch.add("vector", v_tot)

    def d_tot(s):
        return s.dma_start(out=totals_out[:], in_=P.tot[:])
    ch.dma("sync", d_tot)


def build_L1():
    from contextlib import ExitStack
    consts = _host_consts()
    nc = bass.Bass("TRN2", target_bir_lowering=False, debug=False,
                   detect_race_conditions=False)
    C = {}
    for k, v in consts.items():
        dt = BF16 if v.dtype == ml_dtypes.bfloat16 else F32
        C[k] = nc.dram_tensor(k, list(v.shape), dt, kind="ExternalInput")
    x_in = nc.dram_tensor("x_in", [S, D], F32, kind="ExternalInput")
    r_in = nc.dram_tensor("r_in", [D, E], F32, kind="ExternalInput")
    pertok_out = nc.dram_tensor("pertok", [S, 4], F32, kind="ExternalOutput")
    totals_out = nc.dram_tensor("totals", [1, NT * E], F32, kind="ExternalOutput")

    ch = Ch()
    with ExitStack() as st:
        P = Pool(nc, st)
        sbR = st.enter_context(nc.sbuf_tensor("sbR", [128, 256], F32))
        Csb = {}
        for k, v in consts.items():
            dt = BF16 if v.dtype == ml_dtypes.bfloat16 else F32
            if k in ("c_cos", "c_sin", "c_psign"):
                continue
            Csb[k] = st.enter_context(nc.sbuf_tensor("s" + k, list(v.shape), dt))
        PA = st.enter_context(nc.psum_tensor("PA", [128, 2048], F32))
        PB = st.enter_context(nc.psum_tensor("PB", [128, 2048], F32))
        SEMS = {k: st.enter_context(nc.semaphore("sem_" + k))
                for k in list(Ch.ENGINES) + ["g"]}
        block = st.enter_context(nc.Block())

        nconsts = len([k for k in consts
                       if k not in ("c_cos", "c_sin", "c_psign")]) + 1

        def load_consts(s, sem):
            for k in Csb:
                s.dma_start(out=Csb[k][:], in_=C[k][:]).then_inc(sem, 16)
            s.dma_start(out=sbR.ap().rearrange("p (c e) -> p c e", c=8),
                        in_=r_in.ap().rearrange("(c p) e -> p c e", p=128)
                        ).then_inc(sem, 16)
        ch.dmas("sync", load_consts, nconsts)

        def set_eps(v):
            return v.memset(P.eps[:], LN_EPS)
        ch.add("vector", set_eps)

        build_router_program(nc, Csb, x_in, pertok_out, totals_out, ch, st, P,
                             sbR, PA, PB)

        totals_c = ch.emit(block, SEMS)

        def fin(s):
            for k, vv in totals_c.items():
                if vv > 0:
                    s.wait_ge(SEMS[k], vv)
        block.sync(fin)

    return nc, consts


def l1_inputs(consts, x_slice, w_router):
    d = {k: v for k, v in consts.items()}
    d["x_in"] = np.ascontiguousarray(x_slice)
    d["r_in"] = np.ascontiguousarray(w_router)
    return d




def build_L2():
    from contextlib import ExitStack
    consts = _host_consts()
    nc = bass.Bass("TRN2", target_bir_lowering=False, debug=False,
                   detect_race_conditions=False)
    C = {}
    use = ("c_ident_b", "c_lstr_b", "c_iota32", "c_iota128")
    for k in use:
        v = consts[k]
        dt = BF16 if v.dtype == ml_dtypes.bfloat16 else F32
        C[k] = nc.dram_tensor(k, list(v.shape), dt, kind="ExternalInput")
    C["c_tokid"] = nc.dram_tensor("c_tokid", [128, NTG * 4], I32,
                                  kind="ExternalInput")
    xf = nc.dram_tensor("xf", [NC * S, D], F32, kind="ExternalInput")
    ptf = nc.dram_tensor("ptf", [NC * S, 4], F32, kind="ExternalInput")
    totf = nc.dram_tensor("totf", [NTG, E], F32, kind="ExternalInput")
    wg_in = nc.dram_tensor("wg", [EPC * D, D], F32, kind="ExternalInput")
    wu_in = nc.dram_tensor("wu", [EPC * D, D], F32, kind="ExternalInput")
    wd_in = nc.dram_tensor("wd", [EPC * D, D], F32, kind="ExternalInput")
    exb = nc.dram_tensor("exb", [128, 1], F32, kind="ExternalInput")
    y_out = nc.dram_tensor("y_out", [ROWS, D], BF16, kind="ExternalOutput")
    slots_out = nc.dram_tensor("slots_out", [NC * S, 4], F32, kind="ExternalOutput")
    offs_d = nc.dram_tensor("offs_d", [NTG * E], F32)
    invd = nc.dram_tensor("invd", [ROWS + 128, 4], I32)

    ch = Ch()
    with ExitStack() as st:
        P = Pool(nc, st)
        def sb(name, shape, dtype=F32):
            return st.enter_context(nc.sbuf_tensor(name, shape, dtype))
        Csb = {}
        for k in use:
            v = consts[k]
            dt = BF16 if v.dtype == ml_dtypes.bfloat16 else F32
            Csb[k] = sb("s" + k, list(v.shape), dt)
        sbExb = sb("sbExb", [128, 1])
        sbTotB = sb("sbTotB", [128, 32], BF16)
        sbOffs = sb("sbOffs", [128, 32])
        sbOnesRow = sb("sbOnesRow", [1, 128])
        sbOffsRow = sb("sbOffsRow", [1, 512])
        sbOffsBC = sb("sbOffsBC", [128, NTG * E])
        sbPk8 = [sb("sbPk8a", [128, 32]), sb("sbPk8b", [128, 32])]
        sbOh8 = [sb("sbOh8a", [128, 256]), sb("sbOh8b", [128, 256])]
        sbW8 = [sb("sbW8a", [128, 32]), sb("sbW8b", [128, 32])]
        sbIdx8 = [sb("sbIdx8a", [128, 8], I32), sb("sbIdx8b", [128, 8], I32)]
        sbSlt8 = [sb("sbSlt8a", [128, 32]), sb("sbSlt8b", [128, 32])]
        sbTok = sb("sbTok", [128, NTG * 4], I32)
        sbSent = sb("sbSent", [128, 84], I32)
        sbInv = sb("sbInv", [128, 80], I32)
        sbWF = [sb("sbWFa", [128, D]), sb("sbWFb", [128, D])]
        sbWg = [sb("sbWg0", [128, 8 * D], BF16), sb("sbWg1", [128, 8 * D], BF16)]
        sbWu = [sb("sbWu0", [128, 8 * D], BF16), sb("sbWu1", [128, 8 * D], BF16)]
        sbWd = [sb("sbWd0", [128, 8 * D], BF16), sb("sbWd1", [128, 8 * D], BF16)]
        sbGx = [sb("sbGx0", [128, D]), sb("sbGx1", [128, D])]
        sbXtB = [sb("sbXtB0", [128, D], BF16), sb("sbXtB1", [128, D], BF16)]
        sbBufT = [sb("sbBufT0", [128, 8 * CAP], BF16),
                  sb("sbBufT1", [128, 8 * CAP], BF16)]
        sbSg = sb("sbSg", [128, CAP], BF16)
        sbSg2 = sb("sbSg2", [128, CAP], BF16)
        sbST = [sb("sbST0", [128, 8 * CAP], BF16),
                sb("sbST1", [128, 8 * CAP], BF16)]
        sbYout = [sb("sbYout0", [128, D], BF16), sb("sbYout1", [128, D], BF16)]
        PA = st.enter_context(nc.psum_tensor("PA", [128, 2048], F32))
        PB = st.enter_context(nc.psum_tensor("PB", [128, 2048], F32))
        SEMS = {k: st.enter_context(nc.semaphore("sem_" + k))
                for k in list(Ch.ENGINES) + ["g"]}
        block = st.enter_context(nc.Block())

        sbTotF = sb("sbTotF", [128, 32])

        def load_consts2(s, sem):
            for k in use:
                s.dma_start(out=Csb[k][:], in_=C[k][:]).then_inc(sem, 16)
            s.dma_start(out=sbExb[:], in_=exb[:]).then_inc(sem, 16)
            s.dma_start(out=sbTotF[:], in_=totf[:]).then_inc(sem, 16)
            s.dma_start(out=sbTok[:], in_=C["c_tokid"][:]).then_inc(sem, 16)
        ev_tok = ch.dmas("sync", load_consts2, len(use) + 3)

        def v_init(v):
            v.memset(P.eps[:], LN_EPS)
            v.memset(P.one[:], 1.0)
            v.memset(sbOnesRow[:], 1.0)
            v.memset(sbSent[:], 99999)
            return v.tensor_copy(sbTotB[:], sbTotF[:])
        ch.add("vector", v_init)

        def d_sent(s):
            return s.dma_start(
                out=invd.ap().rearrange("(p a) c -> p a c", p=128),
                in_=sbSent.ap().rearrange("p (a c) -> p a c", c=4))
        ch.dma("sync", d_sent)

        def t_offs(t):
            return t.matmul(PB[:, 0:32], Csb["c_lstr_b"][:], sbTotB[:],
                            start=True, stop=True)
        ch.add("tensor", t_offs)

        def v_offs(v):
            return v.tensor_copy(sbOffs[:], PB[:, 0:32])
        ch.add("vector", v_offs)

        def d_offs_out(s):
            return s.dma_start(out=offs_d.ap().rearrange("(p e) -> p e", p=128),
                               in_=sbOffs[:])
        ch.dma("sync", d_offs_out)

        for half in range(2):
            for q in range(4):
                def d_offs_in(s, half=half, q=q):
                    o = 2048 * half + 512 * q
                    return s.dma_start(out=sbOffsRow[:],
                                       in_=offs_d.ap()[None, o:o + 512])
                ch.dma("sync", d_offs_in)

                def t_bc(t, half=half, q=q):
                    return t.matmul(PA[:, 512 * q:512 * q + 512], sbOnesRow[:],
                                    sbOffsRow[:], start=True, stop=True)
                ch.add("tensor", t_bc)

            def v_bc(v, half=half):
                return v.tensor_copy(sbOffsBC[:, 2048 * half:2048 * half + 2048],
                                     PA[:, 0:2048])
            ch.add("vector", v_bc)

        ev_bc = Ev("vector", ch.counts["vector"])
        ev_sent = Ev("sync", ch.counts["sync"])
        prev_grp = {0: None, 1: None}
        ev_scat = []
        for G in range(NTG // 8):
            par = G % 2
            pk8 = sbPk8[par]
            oh8 = sbOh8[par]
            w8 = sbW8[par]
            idx8 = sbIdx8[par]
            slt8 = sbSlt8[par]
            aft = list(prev_grp[par]) if prev_grp[par] else None

            def d_pt(s, G=G, pk8=pk8):
                return s.dma_start(
                    out=pk8.ap().rearrange("p (t c) -> p t c", c=4),
                    in_=ptf[1024 * G:1024 * (G + 1), :]
                    .rearrange("(t p) c -> p t c", p=128))
            e_pt = ch.dma("sync", d_pt, after=aft)

            def l_oh(v, G=G, pk8=pk8, oh8=oh8):
                for t in range(8):
                    inst = v.tensor_single_scalar(
                        oh8[:, 32 * t:32 * (t + 1)], Csb["c_iota32"][:],
                        pk8[:, 4 * t:4 * t + 1], OP.is_equal)
                return inst
            e1 = ch.add("vector", l_oh, after=[e_pt] +
                        (list(prev_grp[par]) if prev_grp[par] else []))

            def l_mul(v, G=G, oh8=oh8):
                for t in range(8):
                    T = 8 * G + t
                    inst = v.tensor_mul(oh8[:, 32 * t:32 * (t + 1)],
                                        oh8[:, 32 * t:32 * (t + 1)],
                                        sbOffsBC[:, 32 * T:32 * (T + 1)])
                return inst
            e2 = ch.add("vector", l_mul, after=[e1, ev_bc])

            def l_red(v, oh8=oh8, w8=w8):
                for t in range(8):
                    inst = v.reduce_sum(w8[:, t:t + 1],
                                        oh8[:, 32 * t:32 * (t + 1)], axis=AX)
                return inst
            e3 = ch.add("vector", l_red, after=[e2])

            def l_a(v, pk8=pk8, w8=w8):
                pt8 = pk8.ap().rearrange("p (t c) -> p t c", c=4)[:, :, 2:3]\
                    .rearrange("p t c -> p (t c)")
                return v.tensor_add(w8[:, 0:8], w8[:, 0:8], pt8)
            e4 = ch.add("vector", l_a, after=[e3])

            def l_b(v, pk8=pk8, w8=w8):
                v.tensor_single_scalar(w8[:, 8:16], w8[:, 0:8], float(CAP),
                                       OP.is_lt)
                es8 = pk8.ap().rearrange("p (t c) -> p t c", c=4)[:, :, 0:1]\
                    .rearrange("p t c -> p (t c)")
                return v.tensor_scalar(w8[:, 16:24], es8, float(CAP), None,
                                       OP.mult)
            e5 = ch.add("vector", l_b, after=[e4])

            def l_c(v, w8=w8):
                v.tensor_add(w8[:, 16:24], w8[:, 16:24], w8[:, 0:8])
                return v.tensor_scalar(w8[:, 24:32], w8[:, 8:16], -BIG, BIG,
                                       OP.mult, OP.add)
            e6 = ch.add("vector", l_c, after=[e5])

            def l_d(v, w8=w8):
                return v.tensor_mul(w8[:, 0:8], w8[:, 16:24], w8[:, 8:16])
            e7 = ch.add("vector", l_d, after=[e6])

            def l_e(v, w8=w8):
                return v.tensor_add(w8[:, 0:8], w8[:, 0:8], w8[:, 24:32])
            e8 = ch.add("vector", l_e, after=[e7])

            def l_f(v, w8=w8):
                return v.tensor_scalar(w8[:, 16:24], w8[:, 0:8], sbExb[:],
                                       None, OP.subtract)
            e9 = ch.add("vector", l_f, after=[e8])

            def l_g(v, w8=w8):
                v.tensor_single_scalar(w8[:, 24:32], w8[:, 16:24], 0.0,
                                       OP.is_ge)
                return v.tensor_single_scalar(w8[:, 8:16], w8[:, 16:24],
                                              float(ROWS), OP.is_lt)
            e10 = ch.add("vector", l_g, after=[e9])

            def l_h(v, w8=w8):
                return v.tensor_mul(w8[:, 24:32], w8[:, 24:32], w8[:, 8:16])
            e11 = ch.add("vector", l_h, after=[e10])

            def l_i(v, w8=w8):
                v.tensor_mul(w8[:, 16:24], w8[:, 16:24], w8[:, 24:32])
                return v.tensor_scalar(w8[:, 8:16], w8[:, 24:32], -BIG, BIG,
                                       OP.mult, OP.add)
            e12 = ch.add("vector", l_i, after=[e11])

            def l_j(v, w8=w8):
                return v.tensor_add(w8[:, 16:24], w8[:, 16:24], w8[:, 8:16])
            e13 = ch.add("vector", l_j, after=[e12])

            def l_k(v, w8=w8, idx8=idx8):
                return v.tensor_copy(idx8[:], w8[:, 16:24])
            e14 = ch.add("vector", l_k, after=[e13])

            def l_m(v, w8=w8, slt8=slt8):
                dst = slt8.ap().rearrange("p (t c) -> p t c", c=4)
                v.tensor_copy(dst[:, :, 0:1].rearrange("p t c -> p (t c)"),
                              w8[:, 0:8])
                v.tensor_single_scalar(
                    dst[:, :, 1:2].rearrange("p t c -> p (t c)"),
                    w8[:, 0:8], float(E * CAP), OP.is_lt)
                return v.memset(dst[:, :, 2:4], 0.0)
            e15 = ch.add("vector", l_m, after=[e8, e13])

            def d_slt(s, G=G, slt8=slt8):
                return s.dma_start(
                    out=slots_out[1024 * G:1024 * (G + 1), :]
                    .rearrange("(t p) c -> p t c", p=128),
                    in_=slt8.ap().rearrange("p (t c) -> p t c", c=4))
            e_ds = ch.dma("sync", d_slt, after=[e15])

            escs = []
            for t in range(8):
                def g_scat(g, G=G, t=t, idx8=idx8):
                    T = 8 * G + t
                    return g.indirect_dma_start(
                        invd[:], IndirectOffsetOnAxis(idx8[:, t:t + 1], 0),
                        sbTok[:, 4 * T:4 * (T + 1)], None,
                        bounds_check=_breg(g, ROWS - 1), oob_is_err=False)
                escs.append(ch.gdma(g_scat, after=[e14, ev_sent, ev_tok]))
            ev_scat.extend(escs)
            prev_grp[par] = [escs[-1], e_ds]
            if G == 0:
                pass
        e_slot_done = Ev("g", ch.counts["g"])
        def d_inv2(s):
            return s.dma_start(
                out=sbInv.ap().rearrange("p (j c) -> p j c", c=4),
                in_=invd.ap()[0:ROWS, :].rearrange("(j p) c -> p j c", p=128))

        e_inv = ch.dma("sync", d_inv2, after=ev_scat)
        ev_wfree = [None, None]
        ev_tprev = [None, None]
        evw = []
        ev_bfree = [None, None]
        ev_sfree = [None, None]
        ev_gu_done = None
        ev_y_done = None
        prev_ln = None
        for e in range(EPC):
            pe = e % 2
            for mi, (w_in, sbW) in enumerate(((wg_in, sbWg[pe]),
                                              (wu_in, sbWu[pe]),
                                              (wd_in, sbWd[pe]))):
                for cc in range(8):
                    k = mi * 8 + cc
                    wfpar = k % 2
                    aft = [ev_wfree[pe]] if ev_wfree[pe] else None
                    if evw[-2:] and len(evw) >= 2:
                        aft = (aft or []) + [evw[-2][1]]
                    def d_w(sy, w_in=w_in, e=e, cc=cc, wfpar=wfpar):
                        return sy.dma_start(
                            out=sbWF[wfpar][:],
                            in_=w_in[D * e + 128 * cc:D * e + 128 * (cc + 1), :])
                    e_d = ch.dma("sync", d_w, after=aft)
                    def v_w(v, sbW=sbW, cc=cc, wfpar=wfpar):
                        return v.tensor_copy(sbW[:, D * cc:D * (cc + 1)],
                                             sbWF[wfpar][:])
                    e_c = ch.add("vector", v_w, after=[e_d])
                    evw.append((e_d, e_c))
            e_wdone = evw[-1][1]

            evtr = []
            for j5 in range(5):
                gpar = j5 % 2
                aft = [e_inv]
                if ev_bfree[pe]:
                    aft.append(ev_bfree[pe])
                if prev_ln:
                    aft.append(prev_ln)
                def v_z(v, gpar=gpar):
                    return v.memset(sbGx[gpar][:], 0.0)
                e_z = ch.add("vector", v_z, after=aft)

                def g_gth(g, e=e, j5=j5, gpar=gpar):
                    col = 4 * (5 * e + j5)
                    return g.indirect_dma_start(
                        sbGx[gpar][:], None, xf[:],
                        IndirectOffsetOnAxis(sbInv[:, col:col + 1], 0),
                        bounds_check=_breg(g, NC * S - 1), oob_is_err=False)
                e_g = ch.gdma(g_gth, after=[e_z])

                e_ln = emit_ln(ch, P, None, src=sbGx[gpar],
                               out=sbXtB[gpar], after=[e_g],
                               guard=[ev_tprev[gpar]] if ev_tprev[gpar] else None)
                prev_ln = e_ln

                def t_tr(t, gpar=gpar):
                    for cc in range(8):
                        inst = t.matmul(PB[:, 1024 + 128 * cc:1024 + 128 * (cc + 1)],
                                        sbXtB[gpar][:, 128 * cc:128 * (cc + 1)],
                                        Csb["c_ident_b"][:], start=True, stop=True)
                    return inst
                aft_t = [e_ln]
                if evtr:
                    aft_t.append(evtr[-1])
                e_t = ch.add("tensor", t_tr, after=aft_t)
                ev_tprev[gpar] = e_t

                def v_tr(v, j5=j5, pe=pe):
                    for cc in range(8):
                        inst = v.tensor_copy(
                            sbBufT[pe][:, CAP * cc + 128 * j5:
                                        CAP * cc + 128 * (j5 + 1)],
                            PB[:, 1024 + 128 * cc:1024 + 128 * (cc + 1)])
                    return inst
                e_tr = ch.add("vector", v_tr, after=[e_t])
                evtr.append(e_tr)
            e_buf = evtr[-1]

            ev_st = []
            prev_gu = None
            for fs in range(8):
                aft = [e_buf, e_wdone]
                if ev_gu_done and fs == 0:
                    aft.append(ev_gu_done)
                if prev_gu:
                    aft.append(prev_gu)
                def t_gu(t, fs=fs, pe=pe):
                    for cc in range(8):
                        t.matmul(PA[:, 0:512],
                                 sbWg[pe][:, D * cc + 128 * fs:
                                           D * cc + 128 * (fs + 1)],
                                 sbBufT[pe][:, CAP * cc:CAP * cc + 512],
                                 start=(cc == 0), stop=(cc == 7))
                        t.matmul(PA[:, 512:640],
                                 sbWg[pe][:, D * cc + 128 * fs:
                                           D * cc + 128 * (fs + 1)],
                                 sbBufT[pe][:, CAP * cc + 512:CAP * cc + 640],
                                 start=(cc == 0), stop=(cc == 7))
                    for cc in range(8):
                        t.matmul(PA[:, 1024:1536],
                                 sbWu[pe][:, D * cc + 128 * fs:
                                           D * cc + 128 * (fs + 1)],
                                 sbBufT[pe][:, CAP * cc:CAP * cc + 512],
                                 start=(cc == 0), stop=(cc == 7))
                        inst = t.matmul(
                            PA[:, 1536:1664],
                            sbWu[pe][:, D * cc + 128 * fs:
                                      D * cc + 128 * (fs + 1)],
                            sbBufT[pe][:, CAP * cc + 512:CAP * cc + 640],
                            start=(cc == 0), stop=(cc == 7))
                    return inst
                e_gu = ch.add("tensor", t_gu, after=aft)

                def s_e1(sc):
                    return sc.activation(sbSg[:], PA[:, 0:640], AF.Exp,
                                         scale=-1.0)
                e_a1 = ch.add("scalar", s_e1, after=[e_gu])

                def s_e2(sc):
                    return sc.activation(sbSg2[:], sbSg[:], AF.Ln,
                                         bias=P.one[:])
                e_a2 = ch.add("scalar", s_e2, after=[e_a1])

                def s_e3(sc):
                    return sc.activation(sbSg[:], sbSg2[:], AF.Exp, scale=-1.0)
                e_a3 = ch.add("scalar", s_e3, after=[e_a2])

                def v_sig(v):
                    return v.tensor_mul(sbSg[:], sbSg[:], PA[:, 0:640])
                e_a4 = ch.add("vector", v_sig, after=[e_a3])

                def v_st(v, fs=fs, pe=pe):
                    return v.tensor_tensor(sbST[pe][:, CAP * fs:CAP * (fs + 1)],
                                           sbSg[:], PA[:, 1024:1664], OP.mult)
                aftst = [e_a4]
                if ev_sfree[pe] and fs == 0:
                    aftst.append(ev_sfree[pe])
                e_st = ch.add("vector", v_st, after=aftst)
                ev_st.append(e_st)
                prev_gu = e_st
            ev_gu_done = ev_st[-1]
            ev_bfree[pe] = ev_gu_done
            ev_wfree[pe] = ev_gu_done

            prev_y = None
            ev_ycs = []
            for cs in range(5):
                aft = [ev_st[-1]]
                if ev_y_done and cs == 0:
                    aft.append(ev_y_done)
                if prev_y:
                    aft.append(prev_y)
                def t_y(t, cs=cs, pe=pe):
                    for fs in range(8):
                        t.matmul(PB[:, 0:512],
                                 sbST[pe][:, CAP * fs + 128 * cs:
                                           CAP * fs + 128 * (cs + 1)],
                                 sbWd[pe][:, D * fs:D * fs + 512],
                                 start=(fs == 0), stop=(fs == 7))
                        inst = t.matmul(
                            PB[:, 512:1024],
                            sbST[pe][:, CAP * fs + 128 * cs:
                                      CAP * fs + 128 * (cs + 1)],
                            sbWd[pe][:, D * fs + 512:D * fs + 1024],
                            start=(fs == 0), stop=(fs == 7))
                    return inst
                e_y = ch.add("tensor", t_y, after=aft)

                def s_yev(sc, cs=cs):
                    return sc.activation(sbYout[cs % 2][:], PB[:, 0:1024],
                                         AF.Copy)
                e_ev = ch.add("scalar", s_yev, after=[e_y] +
                              ([ev_ycs[-2][1]] if len(ev_ycs) >= 2 else []))

                def d_y(sy, e=e, cs=cs):
                    return sy.dma_start(
                        out=y_out[CAP * e + 128 * cs:CAP * e + 128 * (cs + 1), :],
                        in_=sbYout[cs % 2][:])
                e_dy = ch.dma("sync", d_y, after=[e_ev])
                ev_ycs.append((e_y, e_dy))
                prev_y = e_ev
            ev_y_done = ev_ycs[-1][0]
            ev_sfree[pe] = ev_y_done
            ev_wfree[pe] = ev_y_done

        totals_c = ch.emit(block, SEMS)

        def fin(s):
            for k, vv in totals_c.items():
                if vv > 0:
                    s.wait_ge(SEMS[k], vv)
        block.sync(fin)

    return nc, consts


def l2_inputs(consts, core, xfull, pertok_full, totals_full, w_gate, w_up, w_down):
    use = ("c_ident_b", "c_lstr_b", "c_iota32", "c_iota128", "c_tokid")
    d = {k: consts[k] for k in use}
    d["xf"] = xfull
    d["ptf"] = pertok_full
    d["totf"] = totals_full
    d["wg"] = np.ascontiguousarray(w_gate[EPC * core:EPC * (core + 1)]).reshape(EPC * D, D)
    d["wu"] = np.ascontiguousarray(w_up[EPC * core:EPC * (core + 1)]).reshape(EPC * D, D)
    d["wd"] = np.ascontiguousarray(w_down[EPC * core:EPC * (core + 1)]).reshape(EPC * D, D)
    d["exb"] = np.full((128, 1), float(EPC * core * CAP), np.float32)
    return d


def build_L3():
    from contextlib import ExitStack
    consts = _host_consts()
    nc = bass.Bass("TRN2", target_bir_lowering=False, debug=False,
                   detect_race_conditions=False)
    C = {}
    for k, v in consts.items():
        dt = BF16 if v.dtype == ml_dtypes.bfloat16 else F32
        C[k] = nc.dram_tensor(k, list(v.shape), dt, kind="ExternalInput")
    x_own = nc.dram_tensor("x_own", [S, D], F32, kind="ExternalInput")
    y_full = nc.dram_tensor("y_full", [E * CAP, D], BF16, kind="ExternalInput")
    slots_own = nc.dram_tensor("slots_own", [S, 4], F32, kind="ExternalInput")
    pertok_own = nc.dram_tensor("pertok_own", [S, 4], F32, kind="ExternalInput")
    r_in = nc.dram_tensor("r_in", [D, E], F32, kind="ExternalInput")
    wqt = nc.dram_tensor("wqt", [D, D], F32, kind="ExternalInput")
    wkt = nc.dram_tensor("wkt", [D, D], F32, kind="ExternalInput")
    wvt = nc.dram_tensor("wvt", [D, D], F32, kind="ExternalInput")
    wot = nc.dram_tensor("wot", [D, D], F32, kind="ExternalInput")
    bq_in = nc.dram_tensor("bq_in", [8, 128], F32, kind="ExternalInput")
    bk_in = nc.dram_tensor("bk_in", [8, 128], F32, kind="ExternalInput")
    bv_rep = nc.dram_tensor("bv_rep", [128, D], F32, kind="ExternalInput")
    bo_rep = nc.dram_tensor("bo_rep", [128, D], F32, kind="ExternalInput")
    x2_out = nc.dram_tensor("x2_out", [S, D], F32, kind="ExternalOutput")
    pertok2 = nc.dram_tensor("pertok2", [S, 4], F32, kind="ExternalOutput")
    totals2 = nc.dram_tensor("totals2", [1, NT * E], F32, kind="ExternalOutput")
    x1_d = nc.dram_tensor("x1_d", [S, D], F32)

    ch = Ch()
    with ExitStack() as st:
        P = Pool(nc, st)
        def sb(name, shape, dtype=F32):
            return st.enter_context(nc.sbuf_tensor(name, shape, dtype))
        Csb = {}
        for k, v in consts.items():
            dt = BF16 if v.dtype == ml_dtypes.bfloat16 else F32
            Csb[k] = sb("s" + k, list(v.shape), dt)
        sbR = sb("sbR", [128, 256])
        sbXT = sb("sbXT", [128, NT * 1024], BF16)
        sbVt = sb("sbVt", [128, NT * 1040], BF16)
        sbA = sb("sbA", [128, NT * 1024], BF16)
        sbATok = sb("sbATok", [128, NT * 1024], BF16)
        sbW = sb("sbW", [128, 8 * D], BF16)
        _wq = sb("sbWqk_q", [128, D], BF16)
        _wk = sb("sbWqk_k", [128, D], BF16)
        sbWqk2 = {("q", 0): _wq, ("q", 1): _wq, ("k", 0): _wk, ("k", 1): _wk}
        sbWF = [sb("sbWF0", [128, 128]), sb("sbWF1", [128, 128])]
        sbQ = sb("sbQ", [128, S], BF16)
        sbK = sb("sbK", [128, S], BF16)
        sbKp = sb("sbKp", [128, S], BF16)
        sbQp = sb("sbQp", [128, S], BF16)
        sbSc1 = sb("sbSc1", [128, S], BF16)
        sbPO = sb("sbPO", [128, 1024], BF16)
        sbBq = sb("sbBq", [128, 8])
        sbBk = sb("sbBk", [128, 8])
        sbBv = sb("sbBv", [128, D], BF16)
        sbBo = sbBv
        sbYg = sb("sbYg", [128, D], BF16)
        sbSltT = sb("sbSltT", [128, 4])
        sbPkT = sb("sbPkT", [128, 4])
        sbIdxI = sb("sbIdxI", [128, 1], I32)
        sbX1 = P.xtT
        sbX2f = P.sq
        PA = st.enter_context(nc.psum_tensor("PA", [128, 2048], F32))
        PB = st.enter_context(nc.psum_tensor("PB", [128, 2048], F32))
        SEMS = {k: st.enter_context(nc.semaphore("sem_" + k))
                for k in list(Ch.ENGINES) + ["g"]}
        block = st.enter_context(nc.Block())

        nload = len(consts) + 1 + 16

        def load_consts(s, sem):
            for k in consts:
                s.dma_start(out=Csb[k][:], in_=C[k][:]).then_inc(sem, 16)
            s.dma_start(out=sbR.ap().rearrange("p (c e) -> p c e", c=8),
                        in_=r_in.ap().rearrange("(c p) e -> p c e", p=128)
                        ).then_inc(sem, 16)
            for j in range(8):
                s.dma_start(out=sbBq[:, j:j + 1],
                            in_=bq_in[j:j + 1, :].rearrange("a p -> p a")
                            ).then_inc(sem, 16)
                s.dma_start(out=sbBk[:, j:j + 1],
                            in_=bk_in[j:j + 1, :].rearrange("a p -> p a")
                            ).then_inc(sem, 16)
        ch.dmas("sync", load_consts, nload)

        def d_bv(s):
            return s.dma_start(out=P.x[:], in_=bv_rep[:])
        ch.dma("sync", d_bv)

        def v_bv(v):
            return v.tensor_copy(sbBv[:], P.x[:])
        ch.add("vector", v_bv)

        def v_init(v):
            v.memset(P.eps[:], LN_EPS)
            v.memset(sbYg[:], 0.0)
            v.memset(sbKp[64:128, :], 0.0)
            v.memset(sbQp[64:128, :], 0.0)
            v.memset(sbPO[:], 0.0)
            for tt in range(NT):
                inst = v.memset(
                    sbVt.ap()[:, 1040 * tt:1040 * (tt + 1)]
                    .rearrange("p (h c) -> p h c", c=65)[:, :, 64:65], 1.0)
            return inst
        ch.add("vector", v_init)

        for i in range(NT):
            def d_in(s, sem, i=i):
                s.dma_start(out=P.x[:], in_=x_own[128 * i:128 * (i + 1), :]
                            ).then_inc(sem, 16)
                s.dma_start(out=sbSltT[:], in_=slots_own[128 * i:128 * (i + 1), :]
                            ).then_inc(sem, 16)
                s.dma_start(out=sbPkT[:], in_=pertok_own[128 * i:128 * (i + 1), :]
                            ).then_inc(sem, 16)
            ch.dmas("sync", d_in, 3)

            def v_idx(v):
                return v.tensor_copy(sbIdxI[:], sbSltT[:, 0:1])
            ch.add("vector", v_idx)

            def g_y(g):
                return g.indirect_dma_start(
                    sbYg[:], None, y_full[:],
                    IndirectOffsetOnAxis(sbIdxI[:], 0),
                    bounds_check=_breg(g, E * CAP - 1), oob_is_err=False)
            ch.gdma(g_y)

            emit_ln(ch, P, None)

            def v_x1a(v):
                v.tensor_scalar(P.t1[:], sbPkT[:, 1:2], sbSltT[:, 1:2], 0.5,
                                OP.mult, OP.mult)
                return v.scalar_tensor_tensor(sbX1[:], P.xt[:], 0.5, P.x[:],
                                              OP.mult, OP.add)
            ch.add("vector", v_x1a)

            def v_x1b(v):
                return v.scalar_tensor_tensor(sbX2f[:], sbYg[:], P.t1[:],
                                              sbX1[:], OP.mult, OP.add)
            ch.add("vector", v_x1b)

            def d_x1(s, i=i):
                return s.dma_start(out=x1_d[128 * i:128 * (i + 1), :],
                                   in_=sbX2f[:])
            ch.dma("sync", d_x1)

        for i in range(NT):
            def d_x(s, i=i):
                return s.dma_start(out=P.x[:], in_=x1_d[128 * i:128 * (i + 1), :])
            ch.dma("sync", d_x)
            emit_ln(ch, P, None)

            def v_cb(v):
                return v.tensor_copy(sbYg[:], P.xt[:])
            ch.add("vector", v_cb)

            def t_tr(t):
                for cc in range(8):
                    inst = t.matmul(PA[:, 128 * cc:128 * (cc + 1)],
                                    sbYg[:, 128 * cc:128 * (cc + 1)],
                                    Csb["c_ident_b"][:], start=True, stop=True)
                return inst
            ch.add("tensor", t_tr)

            def v_tr(v, i=i):
                for cc in range(8):
                    inst = v.tensor_copy(sbXT[:, 2048 * cc + 128 * i:
                                              2048 * cc + 128 * (i + 1)],
                                         PA[:, 128 * cc:128 * (cc + 1)])
                return inst
            ch.add("vector", v_tr)

        for cc in range(8):
            def d_wv(s, cc=cc):
                return s.dma_start(out=P.x[:],
                                   in_=wvt[128 * cc:128 * (cc + 1), :])
            ch.dma("sync", d_wv)

            def v_wv(v, cc=cc):
                return v.tensor_copy(sbW[:, D * cc:D * (cc + 1)], P.x[:])
            ch.add("vector", v_wv)

        ev_vk = [None, None]
        e_xt_ready = Ev("vector", ch.counts["vector"])
        e_w_ready = Ev("vector", ch.counts["vector"])
        prev_vv = None
        for tt in range(NT):
            for oh in range(2):
                k = 2 * tt + oh
                reg = 512 * (k % 2)
                aft = [e_xt_ready, e_w_ready]
                if ev_vk[k % 2]:
                    aft.append(ev_vk[k % 2])
                def t_v(t, tt=tt, oh=oh, reg=reg):
                    for cc in range(8):
                        inst = t.matmul(PB[:, reg:reg + 512],
                                        sbXT[:, 2048 * cc + 128 * tt:
                                             2048 * cc + 128 * (tt + 1)],
                                        sbW[:, D * cc + 512 * oh:
                                            D * cc + 512 * (oh + 1)],
                                        start=(cc == 0), stop=(cc == 7))
                    return inst
                e_tv = ch.add("tensor", t_v, after=aft)

                def v_v1(v, oh=oh, reg=reg, k=k):
                    return v.tensor_add(sbX1[:, 512 * (k % 2):512 * (k % 2) + 512],
                                        PB[:, reg:reg + 512],
                                        sbBv[:, 512 * oh:512 * (oh + 1)])
                e_v1 = ch.add("vector", v_v1,
                              after=[e_tv] + ([prev_vv] if prev_vv else []))

                def v_v2(v, tt=tt, oh=oh, k=k):
                    base = 1040 * tt
                    dst = sbVt.ap()[:, base:base + 1040].rearrange(
                        "p (h c) -> p h c", c=65)[:, 8 * oh:8 * (oh + 1), 0:64]
                    src = sbX1.ap()[:, 512 * (k % 2):512 * (k % 2) + 512]\
                        .rearrange("p (h c) -> p h c", c=64)
                    return v.tensor_copy(dst, src)
                e_v2 = ch.add("vector", v_v2, after=[e_v1])
                ev_vk[k % 2] = e_v2
                prev_vv = e_v2
        e_vt_done = prev_vv

        e_qkw = {("q", 0): None, ("q", 1): None, ("k", 0): None, ("k", 1): None}
        e_projdone = {("q", 0): None, ("q", 1): None,
                      ("k", 0): None, ("k", 1): None}
        ev_scoreA = [None, None]
        e_pa_reader = None
        ev_pvfree = None
        ev_trfree = None
        ev_atok = []
        e_heads_prev = None
        for j in range(8):
            jp = j % 2
            for (qk, w_in, sbB, dstQ) in (("q", wqt, sbBq, sbQ),
                                          ("k", wkt, sbBk, sbK)):
                sbWb = sbWqk2[(qk, jp)]
                evl = []
                for cc in range(8):
                    aft = [e_projdone[(qk, jp)]] if e_projdone[(qk, jp)] else None
                    if len(evl) >= 2:
                        aft = (aft or []) + [evl[-2][1]]
                    def d_w(sy, w_in=w_in, cc=cc, j=j):
                        return sy.dma_start(
                            out=sbWF[cc % 2][:],
                            in_=w_in[128 * cc:128 * (cc + 1),
                                     128 * j:128 * (j + 1)])
                    e_d = ch.dma("sync", d_w, after=aft)
                    def v_w(v, sbWb=sbWb, cc=cc):
                        return v.tensor_copy(sbWb[:, 128 * cc:128 * (cc + 1)],
                                             sbWF[cc % 2][:])
                    e_c = ch.add("vector", v_w, after=[e_d])
                    evl.append((e_d, e_c))
                e_qkw[(qk, jp)] = evl[-1][1]

            for (qk, sbB, dstQ) in (("q", sbBq, sbQ), ("k", sbBk, sbK)):
                aft = [e_qkw[(qk, jp)]]
                if e_heads_prev:
                    aft.append(e_heads_prev)
                if ev_scoreA[0]:
                    aft.append(ev_scoreA[0])
                if ev_scoreA[1]:
                    aft.append(ev_scoreA[1])
                if e_pa_reader:
                    aft.append(e_pa_reader)
                def t_p(t, qk=qk, jp=jp):
                    sbWb = sbWqk2[(qk, jp)]
                    for ts4 in range(4):
                        for cc in range(8):
                            inst = t.matmul(PA[:, 512 * ts4:512 * (ts4 + 1)],
                                            sbWb[:, 128 * cc:128 * (cc + 1)],
                                            sbXT[:, 2048 * cc + 512 * ts4:
                                                 2048 * cc + 512 * (ts4 + 1)],
                                            start=(cc == 0), stop=(cc == 7))
                    return inst
                e_tp = ch.add("tensor", t_p, after=aft)
                e_projdone[(qk, jp)] = e_tp

                def v_bias(v, sbB=sbB, dstQ=dstQ, j=j):
                    return v.tensor_scalar_add(dstQ[:], PA[:, 0:2048],
                                               sbB[:, j:j + 1])
                e_vb = ch.add("vector", v_bias, after=[e_tp])

                def t_rope(t, dstQ=dstQ):
                    for ts4 in range(4):
                        inst = t.matmul(PB[:, 512 * ts4:512 * (ts4 + 1)]
                                        if False else
                                        PA[:, 512 * ts4:512 * (ts4 + 1)],
                                        Csb["c_psign"][:],
                                        dstQ[:, 512 * ts4:512 * (ts4 + 1)],
                                        start=True, stop=True)
                    return inst
                e_tr = ch.add("tensor", t_rope, after=[e_vb])

                def v_rope1(v, dstQ=dstQ):
                    return v.tensor_mul(sbSc1[:], dstQ[:], Csb["c_cos"][:])
                e_r1 = ch.add("vector", v_rope1, after=[e_vb])

                def v_rope2(v, dstQ=dstQ):
                    return v.tensor_mul(dstQ[:], PA[:, 0:2048], Csb["c_sin"][:])
                e_r2 = ch.add("vector", v_rope2, after=[e_tr, e_r1])

                def v_rope3(v, dstQ=dstQ):
                    return v.tensor_add(dstQ[:], dstQ[:], sbSc1[:])
                e_r3 = ch.add("vector", v_rope3, after=[e_r2])
                e_projdone[(qk, jp)] = e_r3
                e_pa_reader = e_r2

            e_q_ready = e_projdone[("q", jp)]
            e_k_ready = e_projdone[("k", jp)]

            for h2 in range(2):
                h = 2 * j + h2
                hb = 64 * h2

                def v_kp(v, hb=hb):
                    return v.tensor_copy(sbKp[0:64, :], sbK[hb:hb + 64, :])
                aftkp = [e_k_ready]
                if e_heads_prev:
                    aftkp.append(e_heads_prev)
                e_kp = ch.add("vector", v_kp, after=aftkp)

                if h2 == 1:
                    def v_qp(v):
                        return v.tensor_copy(sbQp[0:64, :], sbQ[64:128, :])
                    e_qp = ch.add("vector", v_qp, after=[e_q_ready, e_heads_prev])
                    qsrc = sbQp
                    e_qr = e_qp
                else:
                    qsrc = sbQ
                    e_qr = e_q_ready

                for qh in range(2):
                    ev_exps = []
                    e_pv = None
                    for kt in range(NT):
                        reg = 1024 * (kt % 2)
                        aft = [e_kp, e_qr]
                        if ev_scoreA[kt % 2]:
                            aft.append(ev_scoreA[kt % 2])
                        def t_qk(t, kt=kt, qh=qh, qsrc=qsrc, reg=reg):
                            t.matmul(PA[:, reg:reg + 512],
                                     sbKp[:, 128 * kt:128 * (kt + 1)],
                                     qsrc[:, 1024 * qh:1024 * qh + 512],
                                     start=True, stop=True)
                            return t.matmul(PA[:, reg + 512:reg + 1024],
                                            sbKp[:, 128 * kt:128 * (kt + 1)],
                                            qsrc[:, 1024 * qh + 512:
                                                 1024 * (qh + 1)],
                                            start=True, stop=True)
                        e_qk = ch.add("tensor", t_qk, after=aft)

                        def s_exp(sc, kt=kt, reg=reg):
                            return sc.activation(sbA[:, 1024 * kt:1024 * (kt + 1)],
                                                 PA[:, reg:reg + 1024], AF.Exp,
                                                 scale=ATT_SCALE)
                        e_ex = ch.add("scalar", s_exp, after=[e_qk])
                        ev_scoreA[kt % 2] = e_ex
                        ev_exps.append(e_ex)

                        if kt % 2 == 1:
                            aftpv = [ev_exps[kt - 1], ev_exps[kt]]
                            if kt == 1 and ev_pvfree:
                                aftpv.append(ev_pvfree)
                            if e_pv:
                                aftpv.append(e_pv)
                            def t_pv(t, kt=kt, h=h):
                                for k2 in (kt - 1, kt):
                                    t.matmul(PB[0:65, 0:512],
                                             sbVt[:, 1040 * k2 + 65 * h:
                                                  1040 * k2 + 65 * (h + 1)],
                                             sbA[:, 1024 * k2:1024 * k2 + 512],
                                             start=(k2 == 0), stop=(k2 == NT - 1))
                                    inst = t.matmul(
                                        PB[0:65, 512:1024],
                                        sbVt[:, 1040 * k2 + 65 * h:
                                             1040 * k2 + 65 * (h + 1)],
                                        sbA[:, 1024 * k2 + 512:1024 * (k2 + 1)],
                                        start=(k2 == 0), stop=(k2 == NT - 1))
                                return inst
                            e_pv = ch.add("tensor", t_pv, after=aftpv)

                    def s_po(sc):
                        return sc.activation(sbPO[0:65, :], PB[0:65, 0:1024],
                                             AF.Copy)
                    aftpo = [e_pv]
                    if ev_trfree:
                        aftpo.append(ev_trfree)
                    e_po = ch.add("scalar", s_po, after=aftpo)
                    ev_pvfree = e_po

                    def t_tr2(t):
                        for s8 in range(8):
                            inst = t.matmul(
                                PB[:, 1024 + 128 * s8:1024 + 128 * (s8 + 1)],
                                sbPO[:, 128 * s8:128 * (s8 + 1)],
                                Csb["c_ident_b"][:], start=True, stop=True)
                        return inst
                    e_t2 = ch.add("tensor", t_tr2, after=[e_po])

                    def v_rc(v):
                        return v.reciprocal(
                            P.rc8[:], PB.ap()[:, 1024:2048]
                            .rearrange("p (a b) -> p a b", b=128)[:, :, 64:65]
                            .rearrange("p a b -> p (a b)"))
                    e_rc = ch.add("vector", v_rc, after=[e_t2])

                    def v_ts(v, qh=qh, h=h):
                        for s8 in range(8):
                            tt = 8 * qh + s8
                            inst = v.tensor_scalar_mul(
                                sbATok[:, 1024 * tt + 64 * h:
                                       1024 * tt + 64 * (h + 1)],
                                PB[:, 1024 + 128 * s8:1024 + 128 * s8 + 64],
                                P.rc8[:, s8:s8 + 1])
                        return inst
                    e_ts = ch.add("vector", v_ts, after=[e_rc])
                    ev_trfree = e_ts
                    ev_atok.append(e_ts)
                    e_heads_prev = e_ts
        for tt in range(NT):
            def t_tr3(t, tt=tt):
                for cc in range(8):
                    inst = t.matmul(PA[:, 128 * cc:128 * (cc + 1)],
                                    sbATok[:, 1024 * tt + 128 * cc:
                                           1024 * tt + 128 * (cc + 1)],
                                    Csb["c_ident_b"][:], start=True, stop=True)
                return inst
            ch.add("tensor", t_tr3)

            def v_tr3(v, tt=tt):
                for cc in range(8):
                    inst = v.tensor_copy(sbXT[:, 2048 * cc + 128 * tt:
                                              2048 * cc + 128 * (tt + 1)],
                                         PA[:, 128 * cc:128 * (cc + 1)])
                return inst
            ch.add("vector", v_tr3)

        def d_bo(s):
            return s.dma_start(out=P.x[:], in_=bo_rep[:])
        ch.dma("sync", d_bo)

        def v_bo(v):
            return v.tensor_copy(sbBo[:], P.x[:])
        ch.add("vector", v_bo)

        for cc in range(8):
            def d_wo(s, cc=cc):
                return s.dma_start(out=P.x[:],
                                   in_=wot[128 * cc:128 * (cc + 1), :])
            ch.dma("sync", d_wo)

            def v_wo(v, cc=cc):
                return v.tensor_copy(sbW[:, D * cc:D * (cc + 1)], P.x[:])
            ch.add("vector", v_wo)

        for tt in range(NT):
            def t_o(t, tt=tt):
                for oh in range(2):
                    for cc in range(8):
                        inst = t.matmul(PB[:, 512 * oh:512 * (oh + 1)],
                                        sbXT[:, 2048 * cc + 128 * tt:
                                             2048 * cc + 128 * (tt + 1)],
                                        sbW[:, D * cc + 512 * oh:
                                            D * cc + 512 * (oh + 1)],
                                        start=(cc == 0), stop=(cc == 7))
                return inst
            ch.add("tensor", t_o)

            def d_x1b(s, tt=tt):
                return s.dma_start(out=P.x[:], in_=x1_d[128 * tt:128 * (tt + 1), :])
            ch.dma("sync", d_x1b)

            def v_x2a(v):
                return v.tensor_add(sbX2f[:], PB[:, 0:1024], sbBo[:])
            ch.add("vector", v_x2a)

            def v_x2b(v):
                return v.tensor_add(sbX2f[:], sbX2f[:], P.x[:])
            ch.add("vector", v_x2b)

            def d_x2(s, tt=tt):
                return s.dma_start(out=x2_out[128 * tt:128 * (tt + 1), :],
                                   in_=sbX2f[:])
            ch.dma("sync", d_x2)

        build_router_program(nc, Csb, x2_out, pertok2, totals2, ch, st, P,
                             sbR, PA, PB)

        totals_c = ch.emit(block, SEMS)

        def fin(s):
            for k, vv in totals_c.items():
                if vv > 0:
                    s.wait_ge(SEMS[k], vv)
        block.sync(fin)

    return nc, consts


def l3_inputs(consts, inp, core, x_own, y_full, slots_own, pertok_own):
    d = {k: v for k, v in consts.items()}
    d["x_own"] = x_own
    d["y_full"] = y_full
    d["slots_own"] = slots_own
    d["pertok_own"] = pertok_own
    d["r_in"] = np.ascontiguousarray(inp["L0_ffn_w_router"])
    d["wqt"] = np.ascontiguousarray(np.asarray(inp["L0_attn_Wq"]).T)
    d["wkt"] = np.ascontiguousarray(np.asarray(inp["L0_attn_Wk"]).T)
    d["wvt"] = np.ascontiguousarray(np.asarray(inp["L0_attn_Wv"]).T)
    d["wot"] = np.ascontiguousarray(np.asarray(inp["L0_attn_Wo"]).T)
    d["bq_in"] = np.ascontiguousarray(np.asarray(inp["L0_attn_bq"]).reshape(8, 128))
    d["bk_in"] = np.ascontiguousarray(np.asarray(inp["L0_attn_bk"]).reshape(8, 128))
    d["bv_rep"] = np.ascontiguousarray(
        np.broadcast_to(np.asarray(inp["L0_attn_bv"]), (128, D)))
    d["bo_rep"] = np.ascontiguousarray(
        np.broadcast_to(np.asarray(inp["L0_attn_bo"]), (128, D)))
    return d


def build_L5():
    from contextlib import ExitStack
    consts = _host_consts()
    nc = bass.Bass("TRN2", target_bir_lowering=False, debug=False,
                   detect_race_conditions=False)
    x_own = nc.dram_tensor("x_own", [S, D], F32, kind="ExternalInput")
    y_full = nc.dram_tensor("y_full", [E * CAP, D], BF16, kind="ExternalInput")
    slots_own = nc.dram_tensor("slots_own", [S, 4], F32, kind="ExternalInput")
    pertok_own = nc.dram_tensor("pertok_own", [S, 4], F32, kind="ExternalInput")
    x3_out = nc.dram_tensor("x3_out", [S, D], F32, kind="ExternalOutput")

    ch = Ch()
    with ExitStack() as st:
        P = Pool(nc, st)
        def sb(name, shape, dtype=F32):
            return st.enter_context(nc.sbuf_tensor(name, shape, dtype))
        sbYg = sb("sbYg", [128, D], BF16)
        sbSltT = sb("sbSltT", [128, 4])
        sbPkT = sb("sbPkT", [128, 4])
        sbIdxI = sb("sbIdxI", [128, 1], I32)
        sbX3 = sb("sbX3", [128, D])
        SEMS = {k: st.enter_context(nc.semaphore("sem_" + k))
                for k in list(Ch.ENGINES) + ["g"]}
        block = st.enter_context(nc.Block())

        def v0(v):
            v.memset(sbYg[:], 0.0)
            return v.memset(P.eps[:], LN_EPS)
        ch.add("vector", v0)

        for i in range(NT):
            def d_in(s, sem, i=i):
                s.dma_start(out=P.x[:], in_=x_own[128 * i:128 * (i + 1), :]
                            ).then_inc(sem, 16)
                s.dma_start(out=sbSltT[:], in_=slots_own[128 * i:128 * (i + 1), :]
                            ).then_inc(sem, 16)
                s.dma_start(out=sbPkT[:], in_=pertok_own[128 * i:128 * (i + 1), :]
                            ).then_inc(sem, 16)
            ch.dmas("sync", d_in, 3)

            def v_idx(v):
                return v.tensor_copy(sbIdxI[:], sbSltT[:, 0:1])
            ch.add("vector", v_idx)

            def g_y(g):
                return g.indirect_dma_start(
                    sbYg[:], None, y_full[:],
                    IndirectOffsetOnAxis(sbIdxI[:], 0),
                    bounds_check=_breg(g, E * CAP - 1), oob_is_err=False)
            ch.gdma(g_y)

            emit_ln(ch, P, None)

            def v_x3a(v):
                v.tensor_scalar(P.t1[:], sbPkT[:, 1:2], sbSltT[:, 1:2], 0.5,
                                OP.mult, OP.mult)
                return v.scalar_tensor_tensor(sbX3[:], P.xt[:], 0.5, P.x[:],
                                              OP.mult, OP.add)
            ch.add("vector", v_x3a)

            def v_x3b(v):
                return v.scalar_tensor_tensor(P.sq[:], sbYg[:], P.t1[:],
                                              sbX3[:], OP.mult, OP.add)
            ch.add("vector", v_x3b)

            def d_x3(s, i=i):
                return s.dma_start(out=x3_out[128 * i:128 * (i + 1), :],
                                   in_=P.sq[:])
            ch.dma("sync", d_x3)

        totals_c = ch.emit(block, SEMS)

        def fin(s):
            for k, vv in totals_c.items():
                if vv > 0:
                    s.wait_ge(SEMS[k], vv)
        block.sync(fin)

    return nc, consts


_PROGRAMS = {}


def _get_program(name, builder):
    if name not in _PROGRAMS:
        _PROGRAMS[name] = builder()
    return _PROGRAMS[name]


def kernel(**inputs):
    inputs = {k: np.asarray(v) for k, v in inputs.items()}
    x0 = np.ascontiguousarray(inputs["L0_attn_x"].reshape(NC * S, D)
                              .astype(np.float32, copy=False))
    wr = np.ascontiguousarray(inputs["L0_ffn_w_router"].astype(np.float32, copy=False))
    wg = np.asarray(inputs["L0_ffn_w_gate"], np.float32)
    wu = np.asarray(inputs["L0_ffn_w_up"], np.float32)
    wd = np.asarray(inputs["L0_ffn_w_down"], np.float32)
    cores = list(range(NC))

    nc1, consts = _get_program("L1", build_L1)
    res1 = run_bass_kernel_spmd(
        nc1, [l1_inputs(consts, x0[c * S:(c + 1) * S], wr) for c in cores], cores)
    pertok1 = np.concatenate([np.asarray(r["pertok"]) for r in res1.results], 0)
    totals1 = np.concatenate(
        [np.asarray(r["totals"]).reshape(NT, E) for r in res1.results], 0)

    nc2, consts2 = _get_program("L2", build_L2)
    res2 = run_bass_kernel_spmd(
        nc2, [l2_inputs(consts2, c, x0, pertok1, totals1, wg, wu, wd)
              for c in cores], cores)
    y1 = np.concatenate([np.asarray(r["y_out"]) for r in res2.results], 0)
    slots1 = np.asarray(res2.results[0]["slots_out"])

    nc3, consts3 = _get_program("L3", build_L3)
    res3 = run_bass_kernel_spmd(
        nc3, [l3_inputs(consts3, inputs, c, x0[c * S:(c + 1) * S], y1,
                        slots1[c * S:(c + 1) * S], pertok1[c * S:(c + 1) * S])
              for c in cores], cores)
    x2 = np.concatenate([np.asarray(r["x2_out"]) for r in res3.results], 0)
    pertok2 = np.concatenate([np.asarray(r["pertok2"]) for r in res3.results], 0)
    totals2 = np.concatenate(
        [np.asarray(r["totals2"]).reshape(NT, E) for r in res3.results], 0)

    res4 = run_bass_kernel_spmd(
        nc2, [l2_inputs(consts2, c, x2, pertok2, totals2, wg, wu, wd)
              for c in cores], cores)
    y2 = np.concatenate([np.asarray(r["y_out"]) for r in res4.results], 0)
    slots2 = np.asarray(res4.results[0]["slots_out"])

    nc5, _ = _get_program("L5", build_L5)
    res5 = run_bass_kernel_spmd(
        nc5, [{"x_own": x2[c * S:(c + 1) * S], "y_full": y2,
               "slots_own": slots2[c * S:(c + 1) * S],
               "pertok_own": pertok2[c * S:(c + 1) * S]} for c in cores], cores)
    x3 = np.concatenate([np.asarray(r["x3_out"]) for r in res5.results], 0)
    return x3.reshape(NC, S, D)


if __name__ == "__main__":
    pass


# version 06
# speedup vs baseline: 1.7164x; 1.0036x over previous
import math
import numpy as np
import ml_dtypes
import concourse.bass as bass
import concourse.mybir as mybir
from concourse.bass import IndirectOffsetOnAxis
from concourse.bass_utils import run_bass_kernel_spmd

F32 = mybir.dt.float32
BF16 = mybir.dt.bfloat16
I32 = mybir.dt.int32
U32 = mybir.dt.uint32
AX = mybir.AxisListType.X
OP = mybir.AluOpType
AF = mybir.ActivationFunctionType

NC = 8
D = 1024
S = 2048
NT = S // 128
E = 32
CAP = 640
EPC = E // NC
ROWS = EPC * CAP
NH = 16
HD = 64
NTG = NC * NT
BIG = 1.0e7
LN_EPS = 1e-5
ATT_SCALE = 1.0 / math.sqrt(HD)


class Ev:
    __slots__ = ("eng", "val")

    def __init__(self, eng, val):
        self.eng = eng
        self.val = val


class Ch:

    ENGINES = ("sync", "vector", "scalar", "tensor", "gpsimd")

    def __init__(self):
        self.steps = []
        self.counts = {e: 0 for e in self.ENGINES}
        self.counts["g"] = 0

    def _push(self, eng, fn, inc, sem_stream, after):
        if getattr(self, "force_serial", False):
            after = None
        if after is None:
            after = [Ev(e, self.counts[e]) for e in self.counts
                     if self.counts[e] > 0]
        self.steps.append(dict(eng=eng, fn=fn, inc=inc, sem=sem_stream,
                               after=list(after)))
        self.counts[sem_stream] += inc
        return Ev(sem_stream, self.counts[sem_stream])

    def add(self, eng, fn, inc=1, after=None):
        return self._push(eng, fn, inc, eng, after)

    def dma(self, eng, fn, after=None):
        return self._push(eng, fn, 16, eng, after)

    def dmas(self, eng, fn, n, after=None):
        fn.selfinc = True
        return self._push(eng, fn, 16 * n, eng, after)

    def gdma(self, fn, after=None):
        return self._push("gpsimd", fn, 16, "g", after)

    def emit(self, block, sems):
        per_engine = {}
        for st_ in self.steps:
            per_engine.setdefault(st_["eng"], []).append(st_)

        def mk(eng, items):
            def body(engine):
                waited = {k: 0 for k in self.counts}
                emitted = {k: 0 for k in self.counts}
                for st_ in items:
                    for ev in st_["after"]:
                        need = ev.val
                        if need <= waited[ev.eng]:
                            continue
                        engine.wait_ge(sems[ev.eng], need)
                        waited[ev.eng] = need
                    sem = sems[st_["sem"]]
                    if getattr(st_["fn"], "selfinc", False):
                        st_["fn"](engine, sem)
                    else:
                        inst = st_["fn"](engine)
                        assert inst is not None, "step fn must return last inst"
                        inst.then_inc(sem, st_["inc"])
                    emitted[st_["sem"]] += st_["inc"]
            return body

        for eng, items in per_engine.items():
            block_fn = getattr(block, eng)
            block_fn(mk(eng, items))
        return dict(self.counts)


def _host_consts():
    bf = ml_dtypes.bfloat16
    c = {}
    c["c_ident_f"] = np.eye(128, dtype=np.float32)
    c["c_ident_b"] = np.eye(128, dtype=np.float32).astype(bf)
    c["c_ltri_b"] = np.triu(np.ones((128, 128), np.float32)).astype(bf)
    c["c_lstr_b"] = (np.triu(np.ones((128, 128), np.float32)) -
                     np.eye(128, dtype=np.float32)).astype(bf)
    c["c_iota32"] = np.broadcast_to(np.arange(32, dtype=np.float32), (128, 32)).copy()
    c["c_iota128"] = np.arange(128, dtype=np.float32).reshape(128, 1).copy()
    c["c_ones_b"] = np.ones((128, 1), np.float32).astype(bf)
    inv_freq = 1.0 / 10000.0 ** (np.arange(0, HD, 2, dtype=np.float32) / HD)
    t = np.arange(S, dtype=np.float32)[:, None]
    freqs = t * inv_freq[None, :]
    emb = np.concatenate([freqs, freqs], axis=-1)
    cos = np.cos(emb).T
    sin = np.sin(emb).T
    c["c_cos"] = np.concatenate([cos, cos], axis=0).astype(bf)
    c["c_sin"] = np.concatenate([sin, sin], axis=0).astype(bf)
    P = np.zeros((128, 128), np.float32)
    for blk in (0, 64):
        for m in range(32):
            P[blk + m + 32, blk + m] = -1.0
            P[blk + m, blk + m + 32] = 1.0
    c["c_psign"] = P.astype(bf)
    tok = (np.arange(NTG)[None, :] * 128 + np.arange(128)[:, None]).astype(np.int32)
    c["c_tokid"] = np.repeat(tok, 4, axis=1).copy()
    return c


def _breg(g, val):
    cache = getattr(g, "_breg_cache", None)
    if cache is None:
        cache = {}
        g._breg_cache = cache
    if val not in cache:
        cache[val] = g.to_reg(val)
    return cache[val]


def _declare_consts(nc, names_shapes):
    t = {}
    for name, arr in names_shapes.items():
        dt = {np.dtype(np.float32): F32, np.dtype(ml_dtypes.bfloat16): BF16,
              np.dtype(np.int32): I32}[arr.dtype]
        t[name] = nc.dram_tensor(name, list(arr.shape), dt, kind="ExternalInput")
    return t


class Pool:
    def __init__(self, nc, st):
        def sb(name, shape, dtype=F32):
            return st.enter_context(nc.sbuf_tensor(name, shape, dtype))
        self.sb = sb
        self.x = sb("p_x", [128, D])
        self.sq = sb("p_sq", [128, D])
        self.xt = sb("p_xt", [128, D])
        self.xtT = sb("p_xtT", [128, D])
        self.s1 = sb("p_s1", [128, 1])
        self.s2 = sb("p_s2", [128, 1])
        self.mean = sb("p_mean", [128, 1])
        self.ex2 = sb("p_ex2", [128, 1])
        self.var = sb("p_var", [128, 1])
        self.rstd = sb("p_rstd", [128, 1])
        self.nmr = sb("p_nmr", [128, 1])
        self.lg = sb("p_lg", [128, 32])
        self.mx = sb("p_mx", [128, 1])
        self.nmx = sb("p_nmx", [128, 1])
        self.pe = sb("p_pe", [128, 32])
        self.sum = sb("p_sum", [128, 1])
        self.gate = sb("p_gate", [128, 1])
        self.m8 = sb("p_m8", [128, 8])
        self.ix8 = sb("p_ix8", [128, 8], U32)
        self.est = sb("p_est", [128, 1])
        self.oh = sb("p_oh", [128, 32])
        self.ohb = sb("p_ohb", [128, 32], BF16)
        self.ic = sb("p_ic", [128, 32])
        self.selp = sb("p_selp", [128, 32])
        self.pos = sb("p_pos", [128, 1])
        self.pt = sb("p_pt", [128, 1])
        self.pk = sb("p_pk", [128, 4])
        self.tot = sb("p_tot", [1, 512])
        self.t1 = sb("p_t1", [128, 1])
        self.t2 = sb("p_t2", [128, 1])
        self.t3 = sb("p_t3", [128, 1])
        self.t4 = sb("p_t4", [128, 1])
        self.eps = sb("p_eps", [128, 1])
        self.one = sb("p_one", [128, 1])
        self.rc8 = sb("p_rc8", [128, 8])


def emit_ln(ch, P, src_getter, out=None, src=None, after=None, guard=None):
    if src_getter is not None:
        ev = ch.dma("sync", src_getter)
        after = [ev] if after is None else list(after) + [ev]
    out = out if out is not None else P.xt
    src = src if src is not None else P.x

    def v1(v):
        return v.reduce_sum(P.s1[:], src[:], axis=AX)
    e_v1 = ch.add("vector", v1, after=after)

    def s1(sc):
        return sc.activation(P.sq[:], src[:], AF.Square, accum_out=P.s2[:])
    e_s1 = ch.add("scalar", s1, after=after)

    def v2a(v):
        v.tensor_scalar_mul(P.mean[:], P.s1[:], 1.0 / D)
        return v.tensor_scalar_mul(P.ex2[:], P.s2[:], 1.0 / D)
    e_v2a = ch.add("vector", v2a, after=[e_v1, e_s1])

    def v2b(v):
        return v.tensor_mul(P.t3[:], P.mean[:], P.mean[:])
    e_v2b = ch.add("vector", v2b, after=[e_v2a])

    def v2c(v):
        return v.tensor_sub(P.var[:], P.ex2[:], P.t3[:])
    e_v2c = ch.add("vector", v2c, after=[e_v2b])

    def s2a(sc):
        return sc.activation(P.t4[:], P.var[:], AF.Ln, bias=P.eps[:])
    e_s2a = ch.add("scalar", s2a, after=[e_v2c])

    def s2b(sc):
        return sc.activation(P.rstd[:], P.t4[:], AF.Exp, scale=-0.5)
    e_s2b = ch.add("scalar", s2b, after=[e_s2a])

    def v3(v):
        return v.tensor_scalar(out[:], src[:], P.mean[:], P.rstd[:],
                               OP.subtract, OP.mult)
    return ch.add("vector", v3,
                  after=[e_s2b, e_v2a] + (list(guard) if guard else []))


def emit_router_tile(ch, nc, P, C, sbR, PA, PB, ti, pertok_out, psum_tot_col):
    def t_tr(t):
        for cc in range(8):
            inst = t.matmul(PA[:, 128 * cc:128 * cc + 128],
                            P.xt[:, 128 * cc:128 * cc + 128], C["c_ident_f"][:],
                            start=True, stop=True)
        return inst
    ch.add("tensor", t_tr)

    def v_ev(v):
        return v.tensor_copy(P.xtT[:], PA[:, 0:D])
    ch.add("vector", v_ev)

    def t_lg(t):
        for cc in range(8):
            inst = t.matmul(PB[:, 0:32], P.xtT[:, 128 * cc:128 * cc + 128],
                            sbR[:, 32 * cc:32 * cc + 32],
                            start=(cc == 0), stop=(cc == 7))
        return inst
    ch.add("tensor", t_lg)

    def v_sm1(v):
        return v.tensor_copy(P.lg[:], PB[:, 0:32])
    ch.add("vector", v_sm1)

    def v_sm2(v):
        return v.reduce_max(P.mx[:], P.lg[:], axis=AX)
    ch.add("vector", v_sm2)

    def v_sm3(v):
        return v.tensor_scalar_mul(P.nmx[:], P.mx[:], -1.0)
    ch.add("vector", v_sm3)

    def s_exp(sc):
        return sc.activation(P.pe[:], P.lg[:], AF.Exp, bias=P.nmx[:],
                             scale=1.0, accum_out=P.sum[:])
    ch.add("scalar", s_exp)

    def v_t1(v):
        v.reciprocal(P.gate[:], P.sum[:])
        return v.max(P.m8[:], P.lg[:])
    ch.add("vector", v_t1)

    def v_t2(v):
        return v.max_index(P.ix8[:], P.m8[:], P.lg[:])
    ch.add("vector", v_t2)

    def v_t3(v):
        return v.tensor_copy(P.est[:], P.ix8[:, 0:1])
    ch.add("vector", v_t3)

    def v_t4(v):
        return v.tensor_single_scalar(P.oh[:], C["c_iota32"][:], P.est[:],
                                      OP.is_equal)
    ch.add("vector", v_t4)

    def v_t5(v):
        return v.tensor_copy(P.ohb[:], P.oh[:])
    ch.add("vector", v_t5)

    def t_ic(t):
        t.matmul(PB[:, 512:544], C["c_ltri_b"][:], P.ohb[:], start=True, stop=True)
        return t.matmul(psum_tot_col, C["c_ones_b"][:], P.ohb[:],
                        start=True, stop=True)
    ch.add("tensor", t_ic)

    def v_p1(v):
        return v.tensor_copy(P.ic[:], PB[:, 512:544])
    ch.add("vector", v_p1)

    def v_p2(v):
        return v.tensor_mul(P.selp[:], P.ic[:], P.oh[:])
    ch.add("vector", v_p2)

    def v_p3(v):
        return v.reduce_sum(P.pos[:], P.selp[:], axis=AX)
    ch.add("vector", v_p3)

    def v_p4(v):
        return v.tensor_scalar_add(P.pt[:], P.pos[:], -1.0)
    ch.add("vector", v_p4)

    def v_p5(v):
        v.tensor_copy(P.pk[:, 0:1], P.est[:])
        v.tensor_copy(P.pk[:, 1:2], P.gate[:])
        v.tensor_copy(P.pk[:, 2:3], P.pt[:])
        return v.memset(P.pk[:, 3:4], 0.0)
    ch.add("vector", v_p5)

    def d_out(s):
        return s.dma_start(out=pertok_out[128 * ti:128 * (ti + 1), :], in_=P.pk[:])
    ch.dma("sync", d_out)


def build_router_program(nc, C, x_in, pertok_out, totals_out, ch, st, P,
                         sbR, PA, PB):
    for i in range(NT):
        def load(s, i=i):
            return s.dma_start(out=P.x[:], in_=x_in[128 * i:128 * (i + 1), :])
        emit_ln(ch, P, load)
        emit_router_tile(ch, nc, P, C, sbR, PA, PB, i, pertok_out,
                         PB[0:1, 1024 + 32 * i:1024 + 32 * i + 32])

    def v_tot(v):
        return v.tensor_copy(P.tot[:], PB[0:1, 1024:1536])
    ch.add("vector", v_tot)

    def d_tot(s):
        return s.dma_start(out=totals_out[:], in_=P.tot[:])
    ch.dma("sync", d_tot)


def build_L1():
    from contextlib import ExitStack
    consts = _host_consts()
    nc = bass.Bass("TRN2", target_bir_lowering=False, debug=False,
                   detect_race_conditions=False)
    C = {}
    for k, v in consts.items():
        dt = BF16 if v.dtype == ml_dtypes.bfloat16 else F32
        C[k] = nc.dram_tensor(k, list(v.shape), dt, kind="ExternalInput")
    x_in = nc.dram_tensor("x_in", [S, D], F32, kind="ExternalInput")
    r_in = nc.dram_tensor("r_in", [D, E], F32, kind="ExternalInput")
    pertok_out = nc.dram_tensor("pertok", [S, 4], F32, kind="ExternalOutput")
    totals_out = nc.dram_tensor("totals", [1, NT * E], F32, kind="ExternalOutput")

    ch = Ch()
    with ExitStack() as st:
        P = Pool(nc, st)
        sbR = st.enter_context(nc.sbuf_tensor("sbR", [128, 256], F32))
        Csb = {}
        for k, v in consts.items():
            dt = BF16 if v.dtype == ml_dtypes.bfloat16 else F32
            if k in ("c_cos", "c_sin", "c_psign"):
                continue
            Csb[k] = st.enter_context(nc.sbuf_tensor("s" + k, list(v.shape), dt))
        PA = st.enter_context(nc.psum_tensor("PA", [128, 2048], F32))
        PB = st.enter_context(nc.psum_tensor("PB", [128, 2048], F32))
        SEMS = {k: st.enter_context(nc.semaphore("sem_" + k))
                for k in list(Ch.ENGINES) + ["g"]}
        block = st.enter_context(nc.Block())

        nconsts = len([k for k in consts
                       if k not in ("c_cos", "c_sin", "c_psign")]) + 1

        def load_consts(s, sem):
            for k in Csb:
                s.dma_start(out=Csb[k][:], in_=C[k][:]).then_inc(sem, 16)
            s.dma_start(out=sbR.ap().rearrange("p (c e) -> p c e", c=8),
                        in_=r_in.ap().rearrange("(c p) e -> p c e", p=128)
                        ).then_inc(sem, 16)
        ch.dmas("sync", load_consts, nconsts)

        def set_eps(v):
            return v.memset(P.eps[:], LN_EPS)
        ch.add("vector", set_eps)

        build_router_program(nc, Csb, x_in, pertok_out, totals_out, ch, st, P,
                             sbR, PA, PB)

        totals_c = ch.emit(block, SEMS)

        def fin(s):
            for k, vv in totals_c.items():
                if vv > 0:
                    s.wait_ge(SEMS[k], vv)
        block.sync(fin)

    return nc, consts


def l1_inputs(consts, x_slice, w_router):
    d = {k: v for k, v in consts.items()}
    d["x_in"] = np.ascontiguousarray(x_slice)
    d["r_in"] = np.ascontiguousarray(w_router)
    return d




def build_L2():
    from contextlib import ExitStack
    consts = _host_consts()
    nc = bass.Bass("TRN2", target_bir_lowering=False, debug=False,
                   detect_race_conditions=False)
    C = {}
    use = ("c_ident_b", "c_lstr_b", "c_iota32", "c_iota128")
    for k in use:
        v = consts[k]
        dt = BF16 if v.dtype == ml_dtypes.bfloat16 else F32
        C[k] = nc.dram_tensor(k, list(v.shape), dt, kind="ExternalInput")
    C["c_tokid"] = nc.dram_tensor("c_tokid", [128, NTG * 4], I32,
                                  kind="ExternalInput")
    xf = nc.dram_tensor("xf", [NC * S, D], F32, kind="ExternalInput")
    ptf = nc.dram_tensor("ptf", [NC * S, 4], F32, kind="ExternalInput")
    totf = nc.dram_tensor("totf", [NTG, E], F32, kind="ExternalInput")
    wg_in = nc.dram_tensor("wg", [EPC * D, D], F32, kind="ExternalInput")
    wu_in = nc.dram_tensor("wu", [EPC * D, D], F32, kind="ExternalInput")
    wd_in = nc.dram_tensor("wd", [EPC * D, D], F32, kind="ExternalInput")
    exb = nc.dram_tensor("exb", [128, 1], F32, kind="ExternalInput")
    y_out = nc.dram_tensor("y_out", [ROWS, D], F32, kind="ExternalOutput")
    slots_out = nc.dram_tensor("slots_out", [NC * S, 4], F32, kind="ExternalOutput")
    offs_d = nc.dram_tensor("offs_d", [NTG * E], F32)
    invd = nc.dram_tensor("invd", [ROWS + 128, 4], I32)

    ch = Ch()
    with ExitStack() as st:
        P = Pool(nc, st)
        def sb(name, shape, dtype=F32):
            return st.enter_context(nc.sbuf_tensor(name, shape, dtype))
        Csb = {}
        for k in use:
            v = consts[k]
            dt = BF16 if v.dtype == ml_dtypes.bfloat16 else F32
            Csb[k] = sb("s" + k, list(v.shape), dt)
        sbExb = sb("sbExb", [128, 1])
        sbTotB = sb("sbTotB", [128, 32], BF16)
        sbOffs = sb("sbOffs", [128, 32])
        sbOnesRow = sb("sbOnesRow", [1, 128])
        sbOffsRow = sb("sbOffsRow", [1, 512])
        sbOffsBC = sb("sbOffsBC", [128, NTG * E])
        sbPk8 = [sb("sbPk8a", [128, 32]), sb("sbPk8b", [128, 32])]
        sbOh8 = [sb("sbOh8a", [128, 256]), sb("sbOh8b", [128, 256])]
        sbW8 = [sb("sbW8a", [128, 32]), sb("sbW8b", [128, 32])]
        sbIdx8 = [sb("sbIdx8a", [128, 8], I32), sb("sbIdx8b", [128, 8], I32)]
        sbSlt8 = [sb("sbSlt8a", [128, 32]), sb("sbSlt8b", [128, 32])]
        sbTok = sb("sbTok", [128, NTG * 4], I32)
        sbSent = sb("sbSent", [128, 84], I32)
        sbInv = sb("sbInv", [128, 80], I32)
        sbWF = [sb("sbWFa", [128, D]), sb("sbWFb", [128, D])]
        sbWg = [sb("sbWg0", [128, 8 * D], BF16), sb("sbWg1", [128, 8 * D], BF16)]
        sbWu = [sb("sbWu0", [128, 8 * D], BF16), sb("sbWu1", [128, 8 * D], BF16)]
        sbWd = [sb("sbWd0", [128, 8 * D], BF16), sb("sbWd1", [128, 8 * D], BF16)]
        sbGx = [sb("sbGx0", [128, D]), sb("sbGx1", [128, D])]
        sbXtB = [sb("sbXtB0", [128, D], BF16), sb("sbXtB1", [128, D], BF16)]
        sbBufT = [sb("sbBufT0", [128, 8 * CAP], BF16),
                  sb("sbBufT1", [128, 8 * CAP], BF16)]
        sbSg = sb("sbSg", [128, CAP], BF16)
        sbSg2 = sb("sbSg2", [128, CAP], BF16)
        sbST = [sb("sbST0", [128, 8 * CAP], BF16),
                sb("sbST1", [128, 8 * CAP], BF16)]
        sbYout = [sb("sbYout0", [128, 512]), sb("sbYout1", [128, 512])]
        PA = st.enter_context(nc.psum_tensor("PA", [128, 2048], F32))
        PB = st.enter_context(nc.psum_tensor("PB", [128, 2048], F32))
        SEMS = {k: st.enter_context(nc.semaphore("sem_" + k))
                for k in list(Ch.ENGINES) + ["g"]}
        block = st.enter_context(nc.Block())

        sbTotF = sb("sbTotF", [128, 32])

        def load_consts2(s, sem):
            for k in use:
                s.dma_start(out=Csb[k][:], in_=C[k][:]).then_inc(sem, 16)
            s.dma_start(out=sbExb[:], in_=exb[:]).then_inc(sem, 16)
            s.dma_start(out=sbTotF[:], in_=totf[:]).then_inc(sem, 16)
            s.dma_start(out=sbTok[:], in_=C["c_tokid"][:]).then_inc(sem, 16)
        ev_tok = ch.dmas("sync", load_consts2, len(use) + 3)

        def v_init(v):
            v.memset(P.eps[:], LN_EPS)
            v.memset(P.one[:], 1.0)
            v.memset(sbOnesRow[:], 1.0)
            v.memset(sbSent[:], 99999)
            return v.tensor_copy(sbTotB[:], sbTotF[:])
        ch.add("vector", v_init)

        def d_sent(s):
            return s.dma_start(
                out=invd.ap().rearrange("(p a) c -> p a c", p=128),
                in_=sbSent.ap().rearrange("p (a c) -> p a c", c=4))
        ch.dma("sync", d_sent)

        def t_offs(t):
            return t.matmul(PB[:, 0:32], Csb["c_lstr_b"][:], sbTotB[:],
                            start=True, stop=True)
        ch.add("tensor", t_offs)

        def v_offs(v):
            return v.tensor_copy(sbOffs[:], PB[:, 0:32])
        ch.add("vector", v_offs)

        def d_offs_out(s):
            return s.dma_start(out=offs_d.ap().rearrange("(p e) -> p e", p=128),
                               in_=sbOffs[:])
        ch.dma("sync", d_offs_out)

        for half in range(2):
            for q in range(4):
                def d_offs_in(s, half=half, q=q):
                    o = 2048 * half + 512 * q
                    return s.dma_start(out=sbOffsRow[:],
                                       in_=offs_d.ap()[None, o:o + 512])
                ch.dma("sync", d_offs_in)

                def t_bc(t, half=half, q=q):
                    return t.matmul(PA[:, 512 * q:512 * q + 512], sbOnesRow[:],
                                    sbOffsRow[:], start=True, stop=True)
                ch.add("tensor", t_bc)

            def v_bc(v, half=half):
                return v.tensor_copy(sbOffsBC[:, 2048 * half:2048 * half + 2048],
                                     PA[:, 0:2048])
            ch.add("vector", v_bc)

        ev_bc = Ev("vector", ch.counts["vector"])
        ev_sent = Ev("sync", ch.counts["sync"])
        prev_grp = {0: None, 1: None}
        ev_scat = []
        for G in range(NTG // 8):
            par = G % 2
            pk8 = sbPk8[par]
            oh8 = sbOh8[par]
            w8 = sbW8[par]
            idx8 = sbIdx8[par]
            slt8 = sbSlt8[par]
            aft = list(prev_grp[par]) if prev_grp[par] else None

            def d_pt(s, G=G, pk8=pk8):
                return s.dma_start(
                    out=pk8.ap().rearrange("p (t c) -> p t c", c=4),
                    in_=ptf[1024 * G:1024 * (G + 1), :]
                    .rearrange("(t p) c -> p t c", p=128))
            e_pt = ch.dma("sync", d_pt, after=aft)

            def l_oh(v, G=G, pk8=pk8, oh8=oh8):
                for t in range(8):
                    inst = v.tensor_single_scalar(
                        oh8[:, 32 * t:32 * (t + 1)], Csb["c_iota32"][:],
                        pk8[:, 4 * t:4 * t + 1], OP.is_equal)
                return inst
            e1 = ch.add("vector", l_oh, after=[e_pt] +
                        (list(prev_grp[par]) if prev_grp[par] else []))

            def l_mul(v, G=G, oh8=oh8):
                for t in range(8):
                    T = 8 * G + t
                    inst = v.tensor_mul(oh8[:, 32 * t:32 * (t + 1)],
                                        oh8[:, 32 * t:32 * (t + 1)],
                                        sbOffsBC[:, 32 * T:32 * (T + 1)])
                return inst
            e2 = ch.add("vector", l_mul, after=[e1, ev_bc])

            def l_red(v, oh8=oh8, w8=w8):
                for t in range(8):
                    inst = v.reduce_sum(w8[:, t:t + 1],
                                        oh8[:, 32 * t:32 * (t + 1)], axis=AX)
                return inst
            e3 = ch.add("vector", l_red, after=[e2])

            def l_a(v, pk8=pk8, w8=w8):
                pt8 = pk8.ap().rearrange("p (t c) -> p t c", c=4)[:, :, 2:3]\
                    .rearrange("p t c -> p (t c)")
                return v.tensor_add(w8[:, 0:8], w8[:, 0:8], pt8)
            e4 = ch.add("vector", l_a, after=[e3])

            def l_b(v, pk8=pk8, w8=w8):
                v.tensor_single_scalar(w8[:, 8:16], w8[:, 0:8], float(CAP),
                                       OP.is_lt)
                es8 = pk8.ap().rearrange("p (t c) -> p t c", c=4)[:, :, 0:1]\
                    .rearrange("p t c -> p (t c)")
                return v.tensor_scalar(w8[:, 16:24], es8, float(CAP), None,
                                       OP.mult)
            e5 = ch.add("vector", l_b, after=[e4])

            def l_c(v, w8=w8):
                v.tensor_add(w8[:, 16:24], w8[:, 16:24], w8[:, 0:8])
                return v.tensor_scalar(w8[:, 24:32], w8[:, 8:16], -BIG, BIG,
                                       OP.mult, OP.add)
            e6 = ch.add("vector", l_c, after=[e5])

            def l_d(v, w8=w8):
                return v.tensor_mul(w8[:, 0:8], w8[:, 16:24], w8[:, 8:16])
            e7 = ch.add("vector", l_d, after=[e6])

            def l_e(v, w8=w8):
                return v.tensor_add(w8[:, 0:8], w8[:, 0:8], w8[:, 24:32])
            e8 = ch.add("vector", l_e, after=[e7])

            def l_f(v, w8=w8):
                return v.tensor_scalar(w8[:, 16:24], w8[:, 0:8], sbExb[:],
                                       None, OP.subtract)
            e9 = ch.add("vector", l_f, after=[e8])

            def l_g(v, w8=w8):
                v.tensor_single_scalar(w8[:, 24:32], w8[:, 16:24], 0.0,
                                       OP.is_ge)
                return v.tensor_single_scalar(w8[:, 8:16], w8[:, 16:24],
                                              float(ROWS), OP.is_lt)
            e10 = ch.add("vector", l_g, after=[e9])

            def l_h(v, w8=w8):
                return v.tensor_mul(w8[:, 24:32], w8[:, 24:32], w8[:, 8:16])
            e11 = ch.add("vector", l_h, after=[e10])

            def l_i(v, w8=w8):
                v.tensor_mul(w8[:, 16:24], w8[:, 16:24], w8[:, 24:32])
                return v.tensor_scalar(w8[:, 8:16], w8[:, 24:32], -BIG, BIG,
                                       OP.mult, OP.add)
            e12 = ch.add("vector", l_i, after=[e11])

            def l_j(v, w8=w8):
                return v.tensor_add(w8[:, 16:24], w8[:, 16:24], w8[:, 8:16])
            e13 = ch.add("vector", l_j, after=[e12])

            def l_k(v, w8=w8, idx8=idx8):
                return v.tensor_copy(idx8[:], w8[:, 16:24])
            e14 = ch.add("vector", l_k, after=[e13])

            def l_m(v, w8=w8, slt8=slt8):
                dst = slt8.ap().rearrange("p (t c) -> p t c", c=4)
                v.tensor_copy(dst[:, :, 0:1].rearrange("p t c -> p (t c)"),
                              w8[:, 0:8])
                v.tensor_single_scalar(
                    dst[:, :, 1:2].rearrange("p t c -> p (t c)"),
                    w8[:, 0:8], float(E * CAP), OP.is_lt)
                return v.memset(dst[:, :, 2:4], 0.0)
            e15 = ch.add("vector", l_m, after=[e8, e13])

            def d_slt(s, G=G, slt8=slt8):
                return s.dma_start(
                    out=slots_out[1024 * G:1024 * (G + 1), :]
                    .rearrange("(t p) c -> p t c", p=128),
                    in_=slt8.ap().rearrange("p (t c) -> p t c", c=4))
            e_ds = ch.dma("sync", d_slt, after=[e15])

            escs = []
            for t in range(8):
                def g_scat(g, G=G, t=t, idx8=idx8):
                    T = 8 * G + t
                    return g.indirect_dma_start(
                        invd[:], IndirectOffsetOnAxis(idx8[:, t:t + 1], 0),
                        sbTok[:, 4 * T:4 * (T + 1)], None,
                        bounds_check=_breg(g, ROWS - 1), oob_is_err=False)
                escs.append(ch.gdma(g_scat, after=[e14, ev_sent, ev_tok]))
            ev_scat.extend(escs)
            prev_grp[par] = [escs[-1], e_ds]
            if G == 0:
                pass
        e_slot_done = Ev("g", ch.counts["g"])
        def d_inv2(s):
            return s.dma_start(
                out=sbInv.ap().rearrange("p (j c) -> p j c", c=4),
                in_=invd.ap()[0:ROWS, :].rearrange("(j p) c -> p j c", p=128))

        e_inv = ch.dma("sync", d_inv2, after=ev_scat)
        ev_wfree = [None, None]
        ev_tprev = [None, None]
        evw = []
        ev_bfree = [None, None]
        ev_sfree = [None, None]
        ev_gu_done = None
        ev_y_done = None
        prev_ln = None
        for e in range(EPC):
            pe = e % 2
            for mi, (w_in, sbW) in enumerate(((wg_in, sbWg[pe]),
                                              (wu_in, sbWu[pe]),
                                              (wd_in, sbWd[pe]))):
                for cc in range(8):
                    k = mi * 8 + cc
                    wfpar = k % 2
                    aft = [ev_wfree[pe]] if ev_wfree[pe] else None
                    if evw[-2:] and len(evw) >= 2:
                        aft = (aft or []) + [evw[-2][1]]
                    def d_w(sy, w_in=w_in, e=e, cc=cc, wfpar=wfpar):
                        return sy.dma_start(
                            out=sbWF[wfpar][:],
                            in_=w_in[D * e + 128 * cc:D * e + 128 * (cc + 1), :])
                    e_d = ch.dma("sync", d_w, after=aft)
                    def v_w(v, sbW=sbW, cc=cc, wfpar=wfpar):
                        return v.tensor_copy(sbW[:, D * cc:D * (cc + 1)],
                                             sbWF[wfpar][:])
                    e_c = ch.add("vector", v_w, after=[e_d])
                    evw.append((e_d, e_c))
            e_wdone = evw[-1][1]

            evtr = []
            for j5 in range(5):
                gpar = j5 % 2
                aft = [e_inv]
                if ev_bfree[pe]:
                    aft.append(ev_bfree[pe])
                if prev_ln:
                    aft.append(prev_ln)
                def v_z(v, gpar=gpar):
                    return v.memset(sbGx[gpar][:], 0.0)
                e_z = ch.add("vector", v_z, after=aft)

                def g_gth(g, e=e, j5=j5, gpar=gpar):
                    col = 4 * (5 * e + j5)
                    return g.indirect_dma_start(
                        sbGx[gpar][:], None, xf[:],
                        IndirectOffsetOnAxis(sbInv[:, col:col + 1], 0),
                        bounds_check=_breg(g, NC * S - 1), oob_is_err=False)
                e_g = ch.gdma(g_gth, after=[e_z])

                e_ln = emit_ln(ch, P, None, src=sbGx[gpar],
                               out=sbXtB[gpar], after=[e_g],
                               guard=[ev_tprev[gpar]] if ev_tprev[gpar] else None)
                prev_ln = e_ln

                def t_tr(t, gpar=gpar):
                    for cc in range(8):
                        inst = t.matmul(PB[:, 1024 + 128 * cc:1024 + 128 * (cc + 1)],
                                        sbXtB[gpar][:, 128 * cc:128 * (cc + 1)],
                                        Csb["c_ident_b"][:], start=True, stop=True)
                    return inst
                aft_t = [e_ln]
                if evtr:
                    aft_t.append(evtr[-1])
                e_t = ch.add("tensor", t_tr, after=aft_t)
                ev_tprev[gpar] = e_t

                def v_tr(v, j5=j5, pe=pe):
                    for cc in range(8):
                        inst = v.tensor_copy(
                            sbBufT[pe][:, CAP * cc + 128 * j5:
                                        CAP * cc + 128 * (j5 + 1)],
                            PB[:, 1024 + 128 * cc:1024 + 128 * (cc + 1)])
                    return inst
                e_tr = ch.add("vector", v_tr, after=[e_t])
                evtr.append(e_tr)
            e_buf = evtr[-1]

            ev_st = []
            prev_gu = None
            for fs in range(8):
                aft = [e_buf, e_wdone]
                if ev_gu_done and fs == 0:
                    aft.append(ev_gu_done)
                if prev_gu:
                    aft.append(prev_gu)
                def t_gu(t, fs=fs, pe=pe):
                    for cc in range(8):
                        t.matmul(PA[:, 0:512],
                                 sbWg[pe][:, D * cc + 128 * fs:
                                           D * cc + 128 * (fs + 1)],
                                 sbBufT[pe][:, CAP * cc:CAP * cc + 512],
                                 start=(cc == 0), stop=(cc == 7))
                        t.matmul(PA[:, 512:640],
                                 sbWg[pe][:, D * cc + 128 * fs:
                                           D * cc + 128 * (fs + 1)],
                                 sbBufT[pe][:, CAP * cc + 512:CAP * cc + 640],
                                 start=(cc == 0), stop=(cc == 7))
                    for cc in range(8):
                        t.matmul(PA[:, 1024:1536],
                                 sbWu[pe][:, D * cc + 128 * fs:
                                           D * cc + 128 * (fs + 1)],
                                 sbBufT[pe][:, CAP * cc:CAP * cc + 512],
                                 start=(cc == 0), stop=(cc == 7))
                        inst = t.matmul(
                            PA[:, 1536:1664],
                            sbWu[pe][:, D * cc + 128 * fs:
                                      D * cc + 128 * (fs + 1)],
                            sbBufT[pe][:, CAP * cc + 512:CAP * cc + 640],
                            start=(cc == 0), stop=(cc == 7))
                    return inst
                e_gu = ch.add("tensor", t_gu, after=aft)

                def s_e1(sc):
                    return sc.activation(sbSg[:], PA[:, 0:640], AF.Exp,
                                         scale=-1.0)
                e_a1 = ch.add("scalar", s_e1, after=[e_gu])

                def s_e2(sc):
                    return sc.activation(sbSg2[:], sbSg[:], AF.Ln,
                                         bias=P.one[:])
                e_a2 = ch.add("scalar", s_e2, after=[e_a1])

                def s_e3(sc):
                    return sc.activation(sbSg[:], sbSg2[:], AF.Exp, scale=-1.0)
                e_a3 = ch.add("scalar", s_e3, after=[e_a2])

                def v_sig(v):
                    return v.tensor_mul(sbSg[:], sbSg[:], PA[:, 0:640])
                e_a4 = ch.add("vector", v_sig, after=[e_a3])

                def v_st(v, fs=fs, pe=pe):
                    return v.tensor_tensor(sbST[pe][:, CAP * fs:CAP * (fs + 1)],
                                           sbSg[:], PA[:, 1024:1664], OP.mult)
                aftst = [e_a4]
                if ev_sfree[pe] and fs == 0:
                    aftst.append(ev_sfree[pe])
                e_st = ch.add("vector", v_st, after=aftst)
                ev_st.append(e_st)
                prev_gu = e_st
            ev_gu_done = ev_st[-1]
            ev_bfree[pe] = ev_gu_done
            ev_wfree[pe] = ev_gu_done

            prev_y = None
            ev_ycs = []
            for cs in range(5):
                aft = [ev_st[-1]]
                if ev_y_done and cs == 0:
                    aft.append(ev_y_done)
                if prev_y:
                    aft.append(prev_y)
                def t_y(t, cs=cs, pe=pe):
                    for fs in range(8):
                        t.matmul(PB[:, 0:512],
                                 sbST[pe][:, CAP * fs + 128 * cs:
                                           CAP * fs + 128 * (cs + 1)],
                                 sbWd[pe][:, D * fs:D * fs + 512],
                                 start=(fs == 0), stop=(fs == 7))
                        inst = t.matmul(
                            PB[:, 512:1024],
                            sbST[pe][:, CAP * fs + 128 * cs:
                                      CAP * fs + 128 * (cs + 1)],
                            sbWd[pe][:, D * fs + 512:D * fs + 1024],
                            start=(fs == 0), stop=(fs == 7))
                    return inst
                e_y = ch.add("tensor", t_y, after=aft)

                e_ev = None
                for hh in range(2):
                    afte = [e_y]
                    if len(ev_ycs) >= 1:
                        afte.append(ev_ycs[-1][1][hh])
                    if e_ev:
                        afte.append(e_ev)
                    def s_yev(sc, hh=hh):
                        return sc.activation(sbYout[hh][:],
                                             PB[:, 512 * hh:512 * (hh + 1)],
                                             AF.Copy)
                    e_ev = ch.add("scalar", s_yev, after=afte)

                    def d_y(sy, e=e, cs=cs, hh=hh):
                        return sy.dma_start(
                            out=y_out[CAP * e + 128 * cs:CAP * e + 128 * (cs + 1),
                                      512 * hh:512 * (hh + 1)],
                            in_=sbYout[hh][:])
                    e_dy = ch.dma("sync", d_y, after=[e_ev])
                    if hh == 0:
                        dys = [e_dy]
                    else:
                        dys.append(e_dy)
                ev_ycs.append((e_y, dys))
                prev_y = e_ev
            ev_y_done = ev_ycs[-1][0]
            ev_sfree[pe] = ev_y_done
            ev_wfree[pe] = ev_y_done

        totals_c = ch.emit(block, SEMS)

        def fin(s):
            for k, vv in totals_c.items():
                if vv > 0:
                    s.wait_ge(SEMS[k], vv)
        block.sync(fin)

    return nc, consts


def l2_inputs(consts, core, xfull, pertok_full, totals_full, w_gate, w_up, w_down):
    use = ("c_ident_b", "c_lstr_b", "c_iota32", "c_iota128", "c_tokid")
    d = {k: consts[k] for k in use}
    d["xf"] = xfull
    d["ptf"] = pertok_full
    d["totf"] = totals_full
    d["wg"] = np.ascontiguousarray(w_gate[EPC * core:EPC * (core + 1)]).reshape(EPC * D, D)
    d["wu"] = np.ascontiguousarray(w_up[EPC * core:EPC * (core + 1)]).reshape(EPC * D, D)
    d["wd"] = np.ascontiguousarray(w_down[EPC * core:EPC * (core + 1)]).reshape(EPC * D, D)
    d["exb"] = np.full((128, 1), float(EPC * core * CAP), np.float32)
    return d


def build_L3():
    from contextlib import ExitStack
    consts = _host_consts()
    nc = bass.Bass("TRN2", target_bir_lowering=False, debug=False,
                   detect_race_conditions=False)
    C = {}
    for k, v in consts.items():
        dt = BF16 if v.dtype == ml_dtypes.bfloat16 else F32
        C[k] = nc.dram_tensor(k, list(v.shape), dt, kind="ExternalInput")
    x_own = nc.dram_tensor("x_own", [S, D], F32, kind="ExternalInput")
    y_full = nc.dram_tensor("y_full", [E * CAP, D], F32, kind="ExternalInput")
    slots_own = nc.dram_tensor("slots_own", [S, 4], F32, kind="ExternalInput")
    pertok_own = nc.dram_tensor("pertok_own", [S, 4], F32, kind="ExternalInput")
    r_in = nc.dram_tensor("r_in", [D, E], F32, kind="ExternalInput")
    wqt = nc.dram_tensor("wqt", [D, D], F32, kind="ExternalInput")
    wkt = nc.dram_tensor("wkt", [D, D], F32, kind="ExternalInput")
    wvt = nc.dram_tensor("wvt", [D, D], F32, kind="ExternalInput")
    wot = nc.dram_tensor("wot", [D, D], F32, kind="ExternalInput")
    bq_in = nc.dram_tensor("bq_in", [8, 128], F32, kind="ExternalInput")
    bk_in = nc.dram_tensor("bk_in", [8, 128], F32, kind="ExternalInput")
    bv_rep = nc.dram_tensor("bv_rep", [128, D], F32, kind="ExternalInput")
    bo_rep = nc.dram_tensor("bo_rep", [128, D], F32, kind="ExternalInput")
    x2_out = nc.dram_tensor("x2_out", [S, D], F32, kind="ExternalOutput")
    pertok2 = nc.dram_tensor("pertok2", [S, 4], F32, kind="ExternalOutput")
    totals2 = nc.dram_tensor("totals2", [1, NT * E], F32, kind="ExternalOutput")
    x1_d = nc.dram_tensor("x1_d", [S, D], F32)

    ch = Ch()
    with ExitStack() as st:
        P = Pool(nc, st)
        def sb(name, shape, dtype=F32):
            return st.enter_context(nc.sbuf_tensor(name, shape, dtype))
        Csb = {}
        for k, v in consts.items():
            dt = BF16 if v.dtype == ml_dtypes.bfloat16 else F32
            Csb[k] = sb("s" + k, list(v.shape), dt)
        sbR = sb("sbR", [128, 256])
        sbXT = sb("sbXT", [128, NT * 1024], BF16)
        sbVt = sb("sbVt", [128, NT * 1040], BF16)
        sbA = sb("sbA", [128, NT * 1024], BF16)
        sbATok = sb("sbATok", [128, NT * 1024], BF16)
        sbW = sb("sbW", [128, 8 * D], BF16)
        _wq = sb("sbWqk_q", [128, D], BF16)
        _wk = sb("sbWqk_k", [128, D], BF16)
        sbWqk2 = {("q", 0): _wq, ("q", 1): _wq, ("k", 0): _wk, ("k", 1): _wk}
        sbWF = [sb("sbWF0", [128, 128]), sb("sbWF1", [128, 128])]
        sbQ = sb("sbQ", [128, S], BF16)
        sbK = sb("sbK", [128, S], BF16)
        sbKp = sb("sbKp", [128, S], BF16)
        sbQp = sb("sbQp", [128, S], BF16)
        sbPO = sb("sbPO", [128, 1024], BF16)
        sbXbSt = sb("sbXbSt", [128, D], BF16)
        sbBq = sb("sbBq", [128, 8])
        sbBk = sb("sbBk", [128, 8])
        sbBv = sb("sbBv", [128, D], BF16)
        sbBo = sbBv
        sbYg = sb("sbYg", [128, D])
        sbSltT = sb("sbSltT", [128, 4])
        sbPkT = sb("sbPkT", [128, 4])
        sbIdxI = sb("sbIdxI", [128, 1], I32)
        sbX1 = P.xtT
        sbX2f = P.sq
        PA = st.enter_context(nc.psum_tensor("PA", [128, 2048], F32))
        PB = st.enter_context(nc.psum_tensor("PB", [128, 2048], F32))
        SEMS = {k: st.enter_context(nc.semaphore("sem_" + k))
                for k in list(Ch.ENGINES) + ["g"]}
        block = st.enter_context(nc.Block())

        nload = len(consts) + 1 + 16

        def load_consts(s, sem):
            for k in consts:
                s.dma_start(out=Csb[k][:], in_=C[k][:]).then_inc(sem, 16)
            s.dma_start(out=sbR.ap().rearrange("p (c e) -> p c e", c=8),
                        in_=r_in.ap().rearrange("(c p) e -> p c e", p=128)
                        ).then_inc(sem, 16)
            for j in range(8):
                s.dma_start(out=sbBq[:, j:j + 1],
                            in_=bq_in[j:j + 1, :].rearrange("a p -> p a")
                            ).then_inc(sem, 16)
                s.dma_start(out=sbBk[:, j:j + 1],
                            in_=bk_in[j:j + 1, :].rearrange("a p -> p a")
                            ).then_inc(sem, 16)
        ch.dmas("sync", load_consts, nload)

        def d_bv(s):
            return s.dma_start(out=P.x[:], in_=bv_rep[:])
        ch.dma("sync", d_bv)

        def v_bv(v):
            return v.tensor_copy(sbBv[:], P.x[:])
        ch.add("vector", v_bv)

        def v_init(v):
            v.memset(P.eps[:], LN_EPS)
            v.memset(sbYg[:], 0.0)
            v.memset(sbKp[64:128, :], 0.0)
            v.memset(sbQp[64:128, :], 0.0)
            v.memset(sbPO[:], 0.0)
            for tt in range(NT):
                inst = v.memset(
                    sbVt.ap()[:, 1040 * tt:1040 * (tt + 1)]
                    .rearrange("p (h c) -> p h c", c=65)[:, :, 64:65], 1.0)
            return inst
        ch.add("vector", v_init)

        for i in range(NT):
            def d_in(s, sem, i=i):
                s.dma_start(out=P.x[:], in_=x_own[128 * i:128 * (i + 1), :]
                            ).then_inc(sem, 16)
                s.dma_start(out=sbSltT[:], in_=slots_own[128 * i:128 * (i + 1), :]
                            ).then_inc(sem, 16)
                s.dma_start(out=sbPkT[:], in_=pertok_own[128 * i:128 * (i + 1), :]
                            ).then_inc(sem, 16)
            ch.dmas("sync", d_in, 3)

            def v_idx(v):
                return v.tensor_copy(sbIdxI[:], sbSltT[:, 0:1])
            ch.add("vector", v_idx)

            def g_y(g):
                return g.indirect_dma_start(
                    sbYg[:], None, y_full[:],
                    IndirectOffsetOnAxis(sbIdxI[:], 0),
                    bounds_check=_breg(g, E * CAP - 1), oob_is_err=False)
            ch.gdma(g_y)

            emit_ln(ch, P, None)

            def v_x1a(v):
                v.tensor_scalar(P.t1[:], sbPkT[:, 1:2], sbSltT[:, 1:2], 0.5,
                                OP.mult, OP.mult)
                return v.scalar_tensor_tensor(sbX1[:], P.xt[:], 0.5, P.x[:],
                                              OP.mult, OP.add)
            ch.add("vector", v_x1a)

            def v_x1b(v):
                return v.scalar_tensor_tensor(sbX2f[:], sbYg[:], P.t1[:],
                                              sbX1[:], OP.mult, OP.add)
            ch.add("vector", v_x1b)

            def d_x1(s, i=i):
                return s.dma_start(out=x1_d[128 * i:128 * (i + 1), :],
                                   in_=sbX2f[:])
            ch.dma("sync", d_x1)

        for i in range(NT):
            def d_x(s, i=i):
                return s.dma_start(out=P.x[:], in_=x1_d[128 * i:128 * (i + 1), :])
            ch.dma("sync", d_x)
            emit_ln(ch, P, None)

            def v_cb(v):
                return v.tensor_copy(sbXbSt[:], P.xt[:])
            ch.add("vector", v_cb)

            def t_tr(t):
                for cc in range(8):
                    inst = t.matmul(PA[:, 128 * cc:128 * (cc + 1)],
                                    sbXbSt[:, 128 * cc:128 * (cc + 1)],
                                    Csb["c_ident_b"][:], start=True, stop=True)
                return inst
            ch.add("tensor", t_tr)

            def v_tr(v, i=i):
                for cc in range(8):
                    inst = v.tensor_copy(sbXT[:, 2048 * cc + 128 * i:
                                              2048 * cc + 128 * (i + 1)],
                                         PA[:, 128 * cc:128 * (cc + 1)])
                return inst
            ch.add("vector", v_tr)

        for cc in range(8):
            def d_wv(s, cc=cc):
                return s.dma_start(out=P.x[:],
                                   in_=wvt[128 * cc:128 * (cc + 1), :])
            ch.dma("sync", d_wv)

            def v_wv(v, cc=cc):
                return v.tensor_copy(sbW[:, D * cc:D * (cc + 1)], P.x[:])
            ch.add("vector", v_wv)

        ev_vk = [None, None]
        e_xt_ready = Ev("vector", ch.counts["vector"])
        e_w_ready = Ev("vector", ch.counts["vector"])
        prev_vv = None
        for tt in range(NT):
            for oh in range(2):
                k = 2 * tt + oh
                reg = 512 * (k % 2)
                aft = [e_xt_ready, e_w_ready]
                if ev_vk[k % 2]:
                    aft.append(ev_vk[k % 2])
                def t_v(t, tt=tt, oh=oh, reg=reg):
                    for cc in range(8):
                        inst = t.matmul(PB[:, reg:reg + 512],
                                        sbXT[:, 2048 * cc + 128 * tt:
                                             2048 * cc + 128 * (tt + 1)],
                                        sbW[:, D * cc + 512 * oh:
                                            D * cc + 512 * (oh + 1)],
                                        start=(cc == 0), stop=(cc == 7))
                    return inst
                e_tv = ch.add("tensor", t_v, after=aft)

                def v_v1(v, oh=oh, reg=reg, k=k):
                    return v.tensor_add(sbX1[:, 512 * (k % 2):512 * (k % 2) + 512],
                                        PB[:, reg:reg + 512],
                                        sbBv[:, 512 * oh:512 * (oh + 1)])
                e_v1 = ch.add("vector", v_v1,
                              after=[e_tv] + ([prev_vv] if prev_vv else []))

                def v_v2(v, tt=tt, oh=oh, k=k):
                    base = 1040 * tt
                    dst = sbVt.ap()[:, base:base + 1040].rearrange(
                        "p (h c) -> p h c", c=65)[:, 8 * oh:8 * (oh + 1), 0:64]
                    src = sbX1.ap()[:, 512 * (k % 2):512 * (k % 2) + 512]\
                        .rearrange("p (h c) -> p h c", c=64)
                    return v.tensor_copy(dst, src)
                e_v2 = ch.add("vector", v_v2, after=[e_v1])
                ev_vk[k % 2] = e_v2
                prev_vv = e_v2
        e_vt_done = prev_vv

        e_qkw = {("q", 0): None, ("q", 1): None, ("k", 0): None, ("k", 1): None}
        e_projdone = {("q", 0): None, ("q", 1): None,
                      ("k", 0): None, ("k", 1): None}
        ev_scoreA = [None, None]
        e_pa_reader = None
        ev_pvfree = None
        ev_trfree = None
        ev_atok = []
        e_heads_prev = None
        for j in range(8):
            jp = j % 2
            for (qk, w_in, sbB, dstQ) in (("q", wqt, sbBq, sbQ),
                                          ("k", wkt, sbBk, sbK)):
                sbWb = sbWqk2[(qk, jp)]
                evl = []
                for cc in range(8):
                    aft = [e_projdone[(qk, jp)]] if e_projdone[(qk, jp)] else None
                    if len(evl) >= 2:
                        aft = (aft or []) + [evl[-2][1]]
                    def d_w(sy, w_in=w_in, cc=cc, j=j):
                        return sy.dma_start(
                            out=sbWF[cc % 2][:],
                            in_=w_in[128 * cc:128 * (cc + 1),
                                     128 * j:128 * (j + 1)])
                    e_d = ch.dma("sync", d_w, after=aft)
                    def v_w(v, sbWb=sbWb, cc=cc):
                        return v.tensor_copy(sbWb[:, 128 * cc:128 * (cc + 1)],
                                             sbWF[cc % 2][:])
                    e_c = ch.add("vector", v_w, after=[e_d])
                    evl.append((e_d, e_c))
                e_qkw[(qk, jp)] = evl[-1][1]

            for (qk, sbB, dstQ) in (("q", sbBq, sbQ), ("k", sbBk, sbK)):
                aft = [e_qkw[(qk, jp)]]
                if e_heads_prev:
                    aft.append(e_heads_prev)
                if ev_scoreA[0]:
                    aft.append(ev_scoreA[0])
                if ev_scoreA[1]:
                    aft.append(ev_scoreA[1])
                if e_pa_reader:
                    aft.append(e_pa_reader)
                def t_p(t, qk=qk, jp=jp):
                    sbWb = sbWqk2[(qk, jp)]
                    for ts4 in range(4):
                        for cc in range(8):
                            inst = t.matmul(PA[:, 512 * ts4:512 * (ts4 + 1)],
                                            sbWb[:, 128 * cc:128 * (cc + 1)],
                                            sbXT[:, 2048 * cc + 512 * ts4:
                                                 2048 * cc + 512 * (ts4 + 1)],
                                            start=(cc == 0), stop=(cc == 7))
                    return inst
                e_tp = ch.add("tensor", t_p, after=aft)
                e_projdone[(qk, jp)] = e_tp

                def v_bias(v, sbB=sbB, dstQ=dstQ, j=j):
                    return v.tensor_scalar_add(dstQ[:], PA[:, 0:2048],
                                               sbB[:, j:j + 1])
                e_vb = ch.add("vector", v_bias, after=[e_tp])

                def t_rope(t, dstQ=dstQ):
                    for ts4 in range(4):
                        inst = t.matmul(PB[:, 512 * ts4:512 * (ts4 + 1)]
                                        if False else
                                        PA[:, 512 * ts4:512 * (ts4 + 1)],
                                        Csb["c_psign"][:],
                                        dstQ[:, 512 * ts4:512 * (ts4 + 1)],
                                        start=True, stop=True)
                    return inst
                e_tr = ch.add("tensor", t_rope, after=[e_vb])

                def v_rope1(v, dstQ=dstQ):
                    return v.tensor_mul(sbA[:, 0:2048], dstQ[:], Csb["c_cos"][:])
                e_r1 = ch.add("vector", v_rope1, after=[e_vb])

                def v_rope2(v, dstQ=dstQ):
                    return v.tensor_mul(dstQ[:], PA[:, 0:2048], Csb["c_sin"][:])
                e_r2 = ch.add("vector", v_rope2, after=[e_tr, e_r1])

                def v_rope3(v, dstQ=dstQ):
                    return v.tensor_add(dstQ[:], dstQ[:], sbA[:, 0:2048])
                e_r3 = ch.add("vector", v_rope3, after=[e_r2])
                e_projdone[(qk, jp)] = e_r3
                e_pa_reader = e_r2

            e_q_ready = e_projdone[("q", jp)]
            e_k_ready = e_projdone[("k", jp)]

            for h2 in range(2):
                h = 2 * j + h2
                hb = 64 * h2

                def v_kp(v, hb=hb):
                    return v.tensor_copy(sbKp[0:64, :], sbK[hb:hb + 64, :])
                aftkp = [e_k_ready]
                if e_heads_prev:
                    aftkp.append(e_heads_prev)
                e_kp = ch.add("vector", v_kp, after=aftkp)

                if h2 == 1:
                    def v_qp(v):
                        return v.tensor_copy(sbQp[0:64, :], sbQ[64:128, :])
                    e_qp = ch.add("vector", v_qp, after=[e_q_ready, e_heads_prev])
                    qsrc = sbQp
                    e_qr = e_qp
                else:
                    qsrc = sbQ
                    e_qr = e_q_ready

                for qh in range(2):
                    ev_exps = []
                    e_pv = None
                    for kt in range(NT):
                        reg = 1024 * (kt % 2)
                        aft = [e_kp, e_qr]
                        if ev_scoreA[kt % 2]:
                            aft.append(ev_scoreA[kt % 2])
                        def t_qk(t, kt=kt, qh=qh, qsrc=qsrc, reg=reg):
                            t.matmul(PA[:, reg:reg + 512],
                                     sbKp[:, 128 * kt:128 * (kt + 1)],
                                     qsrc[:, 1024 * qh:1024 * qh + 512],
                                     start=True, stop=True)
                            return t.matmul(PA[:, reg + 512:reg + 1024],
                                            sbKp[:, 128 * kt:128 * (kt + 1)],
                                            qsrc[:, 1024 * qh + 512:
                                                 1024 * (qh + 1)],
                                            start=True, stop=True)
                        e_qk = ch.add("tensor", t_qk, after=aft)

                        def s_exp(sc, kt=kt, reg=reg):
                            return sc.activation(sbA[:, 1024 * kt:1024 * (kt + 1)],
                                                 PA[:, reg:reg + 1024], AF.Exp,
                                                 scale=ATT_SCALE)
                        e_ex = ch.add("scalar", s_exp, after=[e_qk])
                        ev_scoreA[kt % 2] = e_ex
                        ev_exps.append(e_ex)

                        if kt % 2 == 1:
                            aftpv = [ev_exps[kt - 1], ev_exps[kt]]
                            if kt == 1 and ev_pvfree:
                                aftpv.append(ev_pvfree)
                            if e_pv:
                                aftpv.append(e_pv)
                            def t_pv(t, kt=kt, h=h):
                                for k2 in (kt - 1, kt):
                                    t.matmul(PB[0:65, 0:512],
                                             sbVt[:, 1040 * k2 + 65 * h:
                                                  1040 * k2 + 65 * (h + 1)],
                                             sbA[:, 1024 * k2:1024 * k2 + 512],
                                             start=(k2 == 0), stop=(k2 == NT - 1))
                                    inst = t.matmul(
                                        PB[0:65, 512:1024],
                                        sbVt[:, 1040 * k2 + 65 * h:
                                             1040 * k2 + 65 * (h + 1)],
                                        sbA[:, 1024 * k2 + 512:1024 * (k2 + 1)],
                                        start=(k2 == 0), stop=(k2 == NT - 1))
                                return inst
                            e_pv = ch.add("tensor", t_pv, after=aftpv)

                    def s_po(sc):
                        return sc.activation(sbPO[0:65, :], PB[0:65, 0:1024],
                                             AF.Copy)
                    aftpo = [e_pv]
                    if ev_trfree:
                        aftpo.append(ev_trfree)
                    e_po = ch.add("scalar", s_po, after=aftpo)
                    ev_pvfree = e_po

                    def t_tr2(t):
                        for s8 in range(8):
                            inst = t.matmul(
                                PB[:, 1024 + 128 * s8:1024 + 128 * (s8 + 1)],
                                sbPO[:, 128 * s8:128 * (s8 + 1)],
                                Csb["c_ident_b"][:], start=True, stop=True)
                        return inst
                    e_t2 = ch.add("tensor", t_tr2, after=[e_po])

                    def v_rc(v):
                        return v.reciprocal(
                            P.rc8[:], PB.ap()[:, 1024:2048]
                            .rearrange("p (a b) -> p a b", b=128)[:, :, 64:65]
                            .rearrange("p a b -> p (a b)"))
                    e_rc = ch.add("vector", v_rc, after=[e_t2])

                    def v_ts(v, qh=qh, h=h):
                        for s8 in range(8):
                            tt = 8 * qh + s8
                            inst = v.tensor_scalar_mul(
                                sbATok[:, 1024 * tt + 64 * h:
                                       1024 * tt + 64 * (h + 1)],
                                PB[:, 1024 + 128 * s8:1024 + 128 * s8 + 64],
                                P.rc8[:, s8:s8 + 1])
                        return inst
                    e_ts = ch.add("vector", v_ts, after=[e_rc])
                    ev_trfree = e_ts
                    ev_atok.append(e_ts)
                    e_heads_prev = e_ts
        for tt in range(NT):
            def t_tr3(t, tt=tt):
                for cc in range(8):
                    inst = t.matmul(PA[:, 128 * cc:128 * (cc + 1)],
                                    sbATok[:, 1024 * tt + 128 * cc:
                                           1024 * tt + 128 * (cc + 1)],
                                    Csb["c_ident_b"][:], start=True, stop=True)
                return inst
            ch.add("tensor", t_tr3)

            def v_tr3(v, tt=tt):
                for cc in range(8):
                    inst = v.tensor_copy(sbXT[:, 2048 * cc + 128 * tt:
                                              2048 * cc + 128 * (tt + 1)],
                                         PA[:, 128 * cc:128 * (cc + 1)])
                return inst
            ch.add("vector", v_tr3)

        def d_bo(s):
            return s.dma_start(out=P.x[:], in_=bo_rep[:])
        ch.dma("sync", d_bo)

        def v_bo(v):
            return v.tensor_copy(sbBo[:], P.x[:])
        ch.add("vector", v_bo)

        for cc in range(8):
            def d_wo(s, cc=cc):
                return s.dma_start(out=P.x[:],
                                   in_=wot[128 * cc:128 * (cc + 1), :])
            ch.dma("sync", d_wo)

            def v_wo(v, cc=cc):
                return v.tensor_copy(sbW[:, D * cc:D * (cc + 1)], P.x[:])
            ch.add("vector", v_wo)

        for tt in range(NT):
            def t_o(t, tt=tt):
                for oh in range(2):
                    for cc in range(8):
                        inst = t.matmul(PB[:, 512 * oh:512 * (oh + 1)],
                                        sbXT[:, 2048 * cc + 128 * tt:
                                             2048 * cc + 128 * (tt + 1)],
                                        sbW[:, D * cc + 512 * oh:
                                            D * cc + 512 * (oh + 1)],
                                        start=(cc == 0), stop=(cc == 7))
                return inst
            ch.add("tensor", t_o)

            def d_x1b(s, tt=tt):
                return s.dma_start(out=P.x[:], in_=x1_d[128 * tt:128 * (tt + 1), :])
            ch.dma("sync", d_x1b)

            def v_x2a(v):
                return v.tensor_add(sbX2f[:], PB[:, 0:1024], sbBo[:])
            ch.add("vector", v_x2a)

            def v_x2b(v):
                return v.tensor_add(sbX2f[:], sbX2f[:], P.x[:])
            ch.add("vector", v_x2b)

            def d_x2(s, tt=tt):
                return s.dma_start(out=x2_out[128 * tt:128 * (tt + 1), :],
                                   in_=sbX2f[:])
            ch.dma("sync", d_x2)

        build_router_program(nc, Csb, x2_out, pertok2, totals2, ch, st, P,
                             sbR, PA, PB)

        totals_c = ch.emit(block, SEMS)

        def fin(s):
            for k, vv in totals_c.items():
                if vv > 0:
                    s.wait_ge(SEMS[k], vv)
        block.sync(fin)

    return nc, consts


def l3_inputs(consts, inp, core, x_own, y_full, slots_own, pertok_own):
    d = {k: v for k, v in consts.items()}
    d["x_own"] = x_own
    d["y_full"] = y_full
    d["slots_own"] = slots_own
    d["pertok_own"] = pertok_own
    d["r_in"] = np.ascontiguousarray(inp["L0_ffn_w_router"])
    d["wqt"] = np.ascontiguousarray(np.asarray(inp["L0_attn_Wq"]).T)
    d["wkt"] = np.ascontiguousarray(np.asarray(inp["L0_attn_Wk"]).T)
    d["wvt"] = np.ascontiguousarray(np.asarray(inp["L0_attn_Wv"]).T)
    d["wot"] = np.ascontiguousarray(np.asarray(inp["L0_attn_Wo"]).T)
    d["bq_in"] = np.ascontiguousarray(np.asarray(inp["L0_attn_bq"]).reshape(8, 128))
    d["bk_in"] = np.ascontiguousarray(np.asarray(inp["L0_attn_bk"]).reshape(8, 128))
    d["bv_rep"] = np.ascontiguousarray(
        np.broadcast_to(np.asarray(inp["L0_attn_bv"]), (128, D)))
    d["bo_rep"] = np.ascontiguousarray(
        np.broadcast_to(np.asarray(inp["L0_attn_bo"]), (128, D)))
    return d


def build_L5():
    from contextlib import ExitStack
    consts = _host_consts()
    nc = bass.Bass("TRN2", target_bir_lowering=False, debug=False,
                   detect_race_conditions=False)
    x_own = nc.dram_tensor("x_own", [S, D], F32, kind="ExternalInput")
    y_full = nc.dram_tensor("y_full", [E * CAP, D], F32, kind="ExternalInput")
    slots_own = nc.dram_tensor("slots_own", [S, 4], F32, kind="ExternalInput")
    pertok_own = nc.dram_tensor("pertok_own", [S, 4], F32, kind="ExternalInput")
    x3_out = nc.dram_tensor("x3_out", [S, D], F32, kind="ExternalOutput")

    ch = Ch()
    with ExitStack() as st:
        P = Pool(nc, st)
        def sb(name, shape, dtype=F32):
            return st.enter_context(nc.sbuf_tensor(name, shape, dtype))
        sbYg = sb("sbYg", [128, D])
        sbSltT = sb("sbSltT", [128, 4])
        sbPkT = sb("sbPkT", [128, 4])
        sbIdxI = sb("sbIdxI", [128, 1], I32)
        sbX3 = sb("sbX3", [128, D])
        SEMS = {k: st.enter_context(nc.semaphore("sem_" + k))
                for k in list(Ch.ENGINES) + ["g"]}
        block = st.enter_context(nc.Block())

        def v0(v):
            v.memset(sbYg[:], 0.0)
            return v.memset(P.eps[:], LN_EPS)
        ch.add("vector", v0)

        for i in range(NT):
            def d_in(s, sem, i=i):
                s.dma_start(out=P.x[:], in_=x_own[128 * i:128 * (i + 1), :]
                            ).then_inc(sem, 16)
                s.dma_start(out=sbSltT[:], in_=slots_own[128 * i:128 * (i + 1), :]
                            ).then_inc(sem, 16)
                s.dma_start(out=sbPkT[:], in_=pertok_own[128 * i:128 * (i + 1), :]
                            ).then_inc(sem, 16)
            ch.dmas("sync", d_in, 3)

            def v_idx(v):
                return v.tensor_copy(sbIdxI[:], sbSltT[:, 0:1])
            ch.add("vector", v_idx)

            def g_y(g):
                return g.indirect_dma_start(
                    sbYg[:], None, y_full[:],
                    IndirectOffsetOnAxis(sbIdxI[:], 0),
                    bounds_check=_breg(g, E * CAP - 1), oob_is_err=False)
            ch.gdma(g_y)

            emit_ln(ch, P, None)

            def v_x3a(v):
                v.tensor_scalar(P.t1[:], sbPkT[:, 1:2], sbSltT[:, 1:2], 0.5,
                                OP.mult, OP.mult)
                return v.scalar_tensor_tensor(sbX3[:], P.xt[:], 0.5, P.x[:],
                                              OP.mult, OP.add)
            ch.add("vector", v_x3a)

            def v_x3b(v):
                return v.scalar_tensor_tensor(P.sq[:], sbYg[:], P.t1[:],
                                              sbX3[:], OP.mult, OP.add)
            ch.add("vector", v_x3b)

            def d_x3(s, i=i):
                return s.dma_start(out=x3_out[128 * i:128 * (i + 1), :],
                                   in_=P.sq[:])
            ch.dma("sync", d_x3)

        totals_c = ch.emit(block, SEMS)

        def fin(s):
            for k, vv in totals_c.items():
                if vv > 0:
                    s.wait_ge(SEMS[k], vv)
        block.sync(fin)

    return nc, consts


_PROGRAMS = {}


def _get_program(name, builder):
    if name not in _PROGRAMS:
        _PROGRAMS[name] = builder()
    return _PROGRAMS[name]


def kernel(**inputs):
    inputs = {k: np.asarray(v) for k, v in inputs.items()}
    x0 = np.ascontiguousarray(inputs["L0_attn_x"].reshape(NC * S, D)
                              .astype(np.float32, copy=False))
    wr = np.ascontiguousarray(inputs["L0_ffn_w_router"].astype(np.float32, copy=False))
    wg = np.asarray(inputs["L0_ffn_w_gate"], np.float32)
    wu = np.asarray(inputs["L0_ffn_w_up"], np.float32)
    wd = np.asarray(inputs["L0_ffn_w_down"], np.float32)
    cores = list(range(NC))

    nc1, consts = _get_program("L1", build_L1)
    res1 = run_bass_kernel_spmd(
        nc1, [l1_inputs(consts, x0[c * S:(c + 1) * S], wr) for c in cores], cores)
    pertok1 = np.concatenate([np.asarray(r["pertok"]) for r in res1.results], 0)
    totals1 = np.concatenate(
        [np.asarray(r["totals"]).reshape(NT, E) for r in res1.results], 0)

    nc2, consts2 = _get_program("L2", build_L2)
    res2 = run_bass_kernel_spmd(
        nc2, [l2_inputs(consts2, c, x0, pertok1, totals1, wg, wu, wd)
              for c in cores], cores)
    y1 = np.concatenate([np.asarray(r["y_out"]) for r in res2.results], 0)
    slots1 = np.asarray(res2.results[0]["slots_out"])

    nc3, consts3 = _get_program("L3", build_L3)
    res3 = run_bass_kernel_spmd(
        nc3, [l3_inputs(consts3, inputs, c, x0[c * S:(c + 1) * S], y1,
                        slots1[c * S:(c + 1) * S], pertok1[c * S:(c + 1) * S])
              for c in cores], cores)
    x2 = np.concatenate([np.asarray(r["x2_out"]) for r in res3.results], 0)
    pertok2 = np.concatenate([np.asarray(r["pertok2"]) for r in res3.results], 0)
    totals2 = np.concatenate(
        [np.asarray(r["totals2"]).reshape(NT, E) for r in res3.results], 0)

    res4 = run_bass_kernel_spmd(
        nc2, [l2_inputs(consts2, c, x2, pertok2, totals2, wg, wu, wd)
              for c in cores], cores)
    y2 = np.concatenate([np.asarray(r["y_out"]) for r in res4.results], 0)
    slots2 = np.asarray(res4.results[0]["slots_out"])

    nc5, _ = _get_program("L5", build_L5)
    res5 = run_bass_kernel_spmd(
        nc5, [{"x_own": x2[c * S:(c + 1) * S], "y_full": y2,
               "slots_own": slots2[c * S:(c + 1) * S],
               "pertok_own": pertok2[c * S:(c + 1) * S]} for c in cores], cores)
    x3 = np.concatenate([np.asarray(r["x3_out"]) for r in res5.results], 0)
    return x3.reshape(NC, S, D)


if __name__ == "__main__":
    pass
